# version 12
# baseline (speedup 1.0000x reference)
import numpy as np
import ml_dtypes
from contextlib import ExitStack

import concourse.bass as bass
import concourse.mybir as mybir
from concourse.bass_utils import run_bass_kernel_spmd
from concourse.bass_isa import ReduceOp
from concourse import library_config

F32 = mybir.dt.float32
BF16 = mybir.dt.bfloat16
I32 = mybir.dt.int32
ALU = mybir.AluOpType
ACT = mybir.ActivationFunctionType
AXL = mybir.AxisListType

B, T, D = 8, 2048, 1024
NCH = D // 128
NW = 4
WIN = T // NW

CHAIN = ("l0aa", "l0ffn", "l0ab", "l0ffn2", "l1aa", "l1ffn", "l1ab", "l1ffn2")


class Tracker:

    ENGINES = ("tensor", "vector", "scalar", "gpsimd", "sync")

    def __init__(self, nc, ctx):
        self.nc = nc
        self.ctx = ctx
        self.ops = {e: [] for e in self.ENGINES}
        self.sem = {e: ctx.enter_context(nc.semaphore(f"pg_{e}")) for e in self.ENGINES}
        self.count = {e: 0 for e in self.ENGINES}
        self.dch = {}
        self.seen = {e: {} for e in self.ENGINES}
        self.last_w = {}
        self.readers = {}
        self.nwait = 0

    def _dch(self, key, eng):
        if key not in self.dch:
            sem = self.ctx.enter_context(self.nc.semaphore(f"d{len(self.dch)}"))
            self.dch[key] = [sem, 0, eng]
        ch = self.dch[key]
        assert ch[2] == eng, f"dma channel {key} used from {eng} and {ch[2]}"
        return ch

    def _wait(self, eng, src, val):
        if val <= 0 or self.seen[eng].get(src, 0) >= val:
            return
        if src == eng and val > self.count[eng]:
            return
        self.seen[eng][src] = val
        sem = self.dch[src[4:]][0] if src.startswith("dch:") else self.sem[src]
        self.nwait += 1
        self.ops[eng].append(lambda e, sem=sem, v=val: e.wait_ge(sem, v))

    def emit(self, eng, fn, reads=(), writes=(), dma=None, inc=True):
        deps = {}
        def dep(src, val):
            if deps.get(src, 0) < val:
                deps[src] = val
        for r in reads:
            w = self.last_w.get(r)
            if w:
                dep(w[0], w[1])
        for wr in writes:
            w = self.last_w.get(wr)
            if w:
                dep(w[0], w[1])
            for (src, val) in self.readers.get(wr, ()):
                dep(src, val)
        for src, val in deps.items():
            self._wait(eng, src, val)
        lbl = f"{eng[0]}{self.count[eng] + 1}_w{writes[:1]}_r{reads[:1]}"
        def run(e, fn=fn, lbl=lbl):
            with self.nc.named_scope(lbl):
                return fn(e)
        if dma is not None:
            ch = self._dch(dma, eng)
            ch[1] += 16
            self.ops[eng].append(lambda e, run=run, sem=ch[0]: run(e).then_inc(sem, 16))
            tag, cnt = "dch:" + dma, ch[1]
        elif inc:
            self.count[eng] += 1
            self.ops[eng].append(lambda e, run=run, sem=self.sem[eng]:
                                 run(e).then_inc(sem, 1))
            tag, cnt = eng, self.count[eng]
        else:
            self.ops[eng].append(lambda e, run=run: run(e))
            tag, cnt = eng, self.count[eng] + 1
        for r in reads:
            rl = self.readers.setdefault(r, [])
            if (tag, cnt) not in rl:
                rl.append((tag, cnt))
        for wr in writes:
            self.last_w[wr] = (tag, cnt)
            self.readers[wr] = []
        return tag, cnt

    def raw(self, eng, fn):
        self.ops[eng].append(fn)

    def barrier(self):
        engines = ("tensor", "vector", "scalar", "gpsimd")
        snap = dict(self.count)
        for e in engines + ("sync",):
            for f in engines:
                if e != f:
                    self._wait(e, f, snap[f])

    def finalize(self, eng, keys):
        for k in keys:
            ch = self.dch[k]
            self.ops[eng].append(lambda e, sem=ch[0], v=ch[1]: e.wait_ge(sem, v))
        for r, lst in self.readers.items():
            for (tag, cnt) in lst:
                if not tag.startswith("dch:"):
                    assert cnt <= self.count[tag], (r, tag, cnt, self.count[tag])


class Builder:
    def __init__(self, layers):
        self.layers = layers
        self.nc = bass.Bass("TRN2", target_bir_lowering=False)
        self.weights = {}
        self.ctx = ExitStack()

    def win(self, name, shape, dt=F32):
        if name not in self.weights:
            self.weights[name] = (shape, dt,
                                  self.nc.dram_tensor(name, list(shape), dt,
                                                      kind="ExternalInput").ap())
        return self.weights[name][2]

    def sb(self, name, shape, dt=F32, ctx=None):
        return (ctx or self.ctx).enter_context(self.nc.sbuf_tensor(name, shape, dt))

    def bank(self):
        i = self._pb
        self._pb = (self._pb + 1) % 6
        return self.ps[i], ("ps", i)

    def mm(self, steps, out_ap, out_region, reads, start_clear=True, stop=True,
           tile_position=None):
        n = len(steps)
        for i, (l, r) in enumerate(steps):
            self.tk.emit(
                "tensor",
                lambda e, l=l, r=r, s=(start_clear and i == 0),
                t=(stop and i == n - 1):
                    e.matmul(out_ap, lhsT=l(), rhs=r(), start=s, stop=t,
                             tile_position=tile_position),
                reads=reads if i == 0 else (),
                writes=(out_region,) if i in (0, n - 1) else (),
                inc=(i == n - 1))

    def build(self):
        nc = self.nc
        x_h = nc.dram_tensor("x", [T, D], F32, kind="ExternalInput").ap()
        y_h = nc.dram_tensor("y", [T, D], F32, kind="ExternalOutput").ap()

        with self.ctx:
            tk = self.tk = Tracker(nc, self.ctx)
            self.ps = [self.sb_psum(f"ps{i}") for i in range(8)]
            self._pb = 0

            xT = self.xT = self.sb("xT", [128, NCH, T])
            self.ident = self.sb("ident", [128, 128])
            self.R_b = self.sb("R_b", [128, T])
            self.A_b = self.sb("A_b", [128, T])
            self.vec_names = []
            self.vecs = None

            tk.emit("vector", lambda e: e.memset(self.ones2[:], 1.0),
                    writes=("ones2",))
            self.firow = self.sb("firow", [128, WIN])
            self.ones2 = self.sb("ones2", [128, 128])
            self.patg = self.sb("patg", [128, 40])
            self.pcol = self.sb("pcol", [128, 1])
            self.jrow = self.sb("jrow", [128, 64])
            self.tm1 = self.sb("tm1", [128, 16])
            self.stril = self.sb("stril", [128, 128])
            self.striu = self.sb("striu", [128, 128])
            self.trilI = self.sb("trilI", [128, 128])
            self.triuI = self.sb("triuI", [128, 128])
            with ExitStack() as cctx:
                cmaski = self.sb("cmaski", [128, 4, WIN], I32, ctx=cctx)
                identi = self.sb("identi", [128, 128], I32, ctx=cctx)
                tk.emit("gpsimd", lambda e: e.iota(identi[:], pattern=[[1, 128]],
                                                   base=0, channel_multiplier=-1),
                        writes=("identi",))
                tk.emit("gpsimd", lambda e: e.tensor_scalar(
                    out=self.ident[:], in0=identi[:], scalar1=0, op0=ALU.is_equal,
                    scalar2=None), reads=("identi",), writes=("ident",))
                tk.emit("gpsimd", lambda e: e.iota(
                    cmaski[:, 0, :], pattern=[[1, WIN]], base=0,
                    channel_multiplier=0), writes=("cmaski",))
                tk.emit("gpsimd", lambda e: e.tensor_copy(
                    out=self.firow[:], in_=cmaski[:, 0, :]),
                    reads=("cmaski",), writes=("firow",))
                tk.emit("gpsimd", lambda e: e.iota(
                    cmaski[:, 0, 0:1], pattern=[[1, 1]], base=0,
                    channel_multiplier=1), writes=("cmaski",))
                tk.emit("gpsimd", lambda e: e.tensor_copy(
                    out=self.pcol[:], in_=cmaski[:, 0, 0:1]),
                    reads=("cmaski",), writes=("pcol",))
                tk.emit("gpsimd", lambda e: e.iota(
                    cmaski[:, 0, 0:64], pattern=[[1, 64]], base=0,
                    channel_multiplier=0), writes=("cmaski",))
                tk.emit("gpsimd", lambda e: e.tensor_copy(
                    out=self.jrow[:], in_=cmaski[:, 0, 0:64]),
                    reads=("cmaski",), writes=("jrow",))
                tk.emit("gpsimd", lambda e: e.iota(
                    cmaski[:, 0, 0:16], pattern=[[128, 16]], base=-1,
                    channel_multiplier=1), writes=("cmaski",))
                tk.emit("gpsimd", lambda e: e.tensor_copy(
                    out=self.tm1[:], in_=cmaski[:, 0, 0:16]),
                    reads=("cmaski",), writes=("tm1",))
                tk.emit("gpsimd", lambda e: e.iota(
                    cmaski[:, 1, 0:20], pattern=[[1, 20]], base=0,
                    channel_multiplier=-10), writes=("cmaski2",))
                tk.emit("gpsimd", lambda e: e.tensor_copy(
                    out=self.patg[:, 0:20], in_=cmaski[:, 1, 0:20]),
                    reads=("cmaski2",), writes=("patg",))
                tk.emit("gpsimd", lambda e: e.tensor_scalar(
                    out=self.patg[:, 20:40], in0=self.patg[:, 0:20], scalar1=0.0,
                    op0=ALU.is_ge, scalar2=None),
                    reads=("patg",), writes=("patg",))
                tk.emit("gpsimd", lambda e: e.tensor_scalar(
                    out=self.patg[:, 0:20], in0=self.patg[:, 0:20], scalar1=9.5,
                    op0=ALU.is_le, scalar2=None),
                    reads=("patg",), writes=("patg",))
                tk.emit("gpsimd", lambda e: e.tensor_tensor(
                    out=self.patg[:, 0:20], in0=self.patg[:, 0:20],
                    in1=self.patg[:, 20:40], op=ALU.mult),
                    reads=("patg",), writes=("patg",))
                tk.emit("gpsimd", lambda e: e.iota(
                    cmaski[:, 0, 0:128], pattern=[[1, 128]], base=0,
                    channel_multiplier=-1), writes=("cmaski",))
                tk.emit("gpsimd", lambda e: e.tensor_scalar(
                    out=self.stril[:], in0=cmaski[:, 0, 0:128], scalar1=0,
                    op0=ALU.is_lt, scalar2=None), reads=("cmaski",),
                    writes=("stril",))
                tk.emit("gpsimd", lambda e: e.tensor_scalar(
                    out=self.striu[:], in0=cmaski[:, 0, 0:128], scalar1=0,
                    op0=ALU.is_gt, scalar2=None), reads=("cmaski",),
                    writes=("striu",))
                tk.emit("gpsimd", lambda e: e.tensor_scalar(
                    out=self.trilI[:], in0=cmaski[:, 0, 0:128], scalar1=0,
                    op0=ALU.is_le, scalar2=None), reads=("cmaski",),
                    writes=("trilI",))
                tk.emit("gpsimd", lambda e: e.tensor_scalar(
                    out=self.triuI[:], in0=cmaski[:, 0, 0:128], scalar1=0,
                    op0=ALU.is_ge, scalar2=None), reads=("cmaski",),
                    writes=("triuI",))
            tk.barrier()

            with ExitStack() as lctx:
                ytile = self.sb("ytile", [128, 8, D], ctx=lctx)
                x_r = x_h.rearrange("(a p) d -> p a d", p=128)
                for g in range(4):
                    tk.emit("sync", lambda e, g=g: e.dma_start(
                        out=ytile[:, (g % 2) * 4:(g % 2) * 4 + 4, :],
                        in_=x_r[:, g * 4:(g + 1) * 4, :]),
                        writes=(("ytg", g % 2),), dma=f"xin{g % 2}")
                    for tc in range(g * 4, g * 4 + 4):
                        for cc in range(NCH):
                            bk, br = self.bank()
                            tk.emit("tensor", lambda e, tc=tc, cc=cc, bk=bk, g=g: e.transpose(
                                bk[:, 0:128],
                                in_=ytile[:, (g % 2) * 4 + tc - g * 4, cc * 128:(cc + 1) * 128],
                                identity=self.ident[:]),
                                reads=(("ytg", g % 2), "ident"), writes=(br,))
                            tk.emit("vector", lambda e, tc=tc, cc=cc, bk=bk: e.tensor_copy(
                                out=xT[:, cc, tc * 128:(tc + 1) * 128], in_=bk[:, 0:128]),
                                reads=(br,), writes=(("x", cc, tc // 4),))
                tk.barrier()

            self.vecs = self.sb("vecs", [128, 16, NCH])
            self._vec_slot = {}
            for li, name in enumerate(CHAIN):
                if name not in self.layers:
                    continue
                getattr(self, "emit_" + name.rstrip("2"))(name)
                tk.barrier()

            with ExitStack() as lctx:
                ytile2 = self.sb("ytile2", [128, 8, D], ctx=lctx)
                y_r = y_h.rearrange("(a p) d -> p a d", p=128)
                for g in range(4):
                    h = g % 2
                    for tc in range(g * 4, g * 4 + 4):
                        for cc in range(NCH):
                            bk, br = self.bank()
                            tk.emit("tensor", lambda e, tc=tc, cc=cc, bk=bk: e.transpose(
                                bk[:, 0:128], in_=xT[:, cc, tc * 128:(tc + 1) * 128],
                                identity=self.ident[:]),
                                reads=(("x", cc, tc // 4), "ident"), writes=(br,))
                            tk.emit("vector", lambda e, tc=tc, cc=cc, bk=bk, h=h, g=g: e.tensor_copy(
                                out=ytile2[:, h * 4 + tc - g * 4, cc * 128:(cc + 1) * 128],
                                in_=bk[:, 0:128]),
                                reads=(br,), writes=(("yo", h),))
                    tk.emit("scalar", lambda e, g=g, h=h: e.dma_start(
                        out=y_r[:, g * 4:(g + 1) * 4, :],
                        in_=ytile2[:, h * 4:h * 4 + 4, :]),
                        reads=(("yo", h),), writes=(("yhbm", g),), dma=f"yout{g % 2}")
                tk.finalize("scalar", [f"yout{g}" for g in range(2)])

            blk = self.ctx.enter_context(nc.Block())
            for eng in Tracker.ENGINES:
                opl = tk.ops[eng]
                if opl:
                    getattr(blk, eng)(lambda e, opl=opl: [f(e) for f in opl])
        return nc

    def sb_psum(self, name):
        return self.ctx.enter_context(self.nc.psum_tensor(name, [128, 512], F32))

    def vec(self, name):
        return self._vec_slot[name]

    def load_vecs(self, names, dram_ap, tag):
        base = len(self._vec_slot)
        for i, n in enumerate(names):
            self._vec_slot[n] = base + i
        self.tk.emit("sync", lambda e: e.dma_start(
            out=self.vecs[:, base:base + len(names), :], in_=dram_ap[:]),
            writes=(("vecs", tag),), dma=f"vec_{tag}")

    def bcast(self, src_fn, src_reads, dst_fn, dst_region, n, base=0):
        tk = self.tk
        for c0 in range(0, n, 512):
            cw = min(512, n - c0)
            bkk, br = self.bank()
            self.mm([(lambda base=base: self.ones2[base:base + 1, 0:128],
                      lambda c0=c0, cw=cw: src_fn(c0, cw))],
                    bkk[:, 0:cw], br, reads=src_reads + (("ones2"),))
            tk.emit("vector", lambda e, bkk=bkk, c0=c0, cw=cw: e.tensor_copy(
                out=dst_fn(c0, cw), in_=bkk[:, 0:cw]),
                reads=(br,), writes=(dst_region,))

    def ln_stats(self, lctx, tag):
        tk, xT = self.tk, self.xT
        acc = self.sb(f"lnacc_{tag}", [128, 2, T], ctx=lctx)
        lnr = self.sb(f"lnrw_{tag}", [128, 2, 512], ctx=lctx)
        tmp = self.sb(f"lntmp_{tag}", [128, T], ctx=lctx)
        xr = lambda c: tuple(("x", c, w) for w in range(NW))
        for c in range(NCH):
            tk.emit("scalar", lambda e, c=c: e.activation(
                out=tmp[:], in_=xT[:, c, :], func=ACT.Square),
                reads=xr(c), writes=(("lntmp", tag),))
            if c == 0:
                tk.emit("vector", lambda e: e.tensor_copy(out=acc[:, 0, :], in_=tmp[:]),
                        reads=(("lntmp", tag),), writes=(("lnacc0", tag),))
                tk.emit("vector", lambda e: e.tensor_copy(out=acc[:, 1, :], in_=xT[:, 0, :]),
                        reads=xr(0), writes=(("lnacc1", tag),))
            else:
                tk.emit("vector", lambda e, c=c: e.tensor_tensor(
                    out=acc[:, 0, :], in0=acc[:, 0, :], in1=tmp[:], op=ALU.add),
                    reads=(("lntmp", tag), ("lnacc0", tag)), writes=(("lnacc0", tag),))
                tk.emit("vector", lambda e, c=c: e.tensor_tensor(
                    out=acc[:, 1, :], in0=acc[:, 1, :], in1=xT[:, c, :], op=ALU.add),
                    reads=xr(c) + (("lnacc1", tag),), writes=(("lnacc1", tag),))
        for w in range(NW):
            sl = slice(w * WIN, (w + 1) * WIN)
            b2, r2_ = self.bank()
            self.mm([(lambda: self.ones2[:, 0:1],
                      lambda w=w: acc[:, 0, w * WIN:(w + 1) * WIN])],
                    b2[0:1, 0:WIN], r2_, reads=(("lnacc0", tag), "ones2"))
            b1, r1_ = self.bank()
            self.mm([(lambda: self.ones2[:, 0:1],
                      lambda w=w: acc[:, 1, w * WIN:(w + 1) * WIN])],
                    b1[0:1, 0:WIN], r1_, reads=(("lnacc1", tag), "ones2"))
            tk.emit("vector", lambda e, b1=b1, sl=sl: e.tensor_scalar(
                out=self.A_b[0:1, sl], in0=b1[0:1, 0:WIN], scalar1=1.0 / D,
                op0=ALU.mult, scalar2=None), reads=(r1_,), writes=("A_b",))
            tk.emit("vector", lambda e, b2=b2, sl=sl: e.tensor_scalar(
                out=lnr[0:1, 0, :], in0=b2[0:1, 0:WIN], scalar1=1.0 / D,
                op0=ALU.mult, scalar2=1e-5, op1=ALU.add),
                reads=(r2_,), writes=(("lnr0", tag),))
            tk.emit("vector", lambda e, sl=sl: e.tensor_tensor(
                out=lnr[0:1, 1, :], in0=self.A_b[0:1, sl], in1=self.A_b[0:1, sl],
                op=ALU.mult), reads=("A_b",), writes=(("lnr1", tag),))
            tk.emit("vector", lambda e: e.tensor_tensor(
                out=lnr[0:1, 0, :], in0=lnr[0:1, 0, :], in1=lnr[0:1, 1, :],
                op=ALU.subtract),
                reads=(("lnr0", tag), ("lnr1", tag)), writes=(("lnr0", tag),))
            tk.emit("scalar", lambda e: e.activation(
                out=lnr[0:1, 0, :], in_=lnr[0:1, 0, :], func=ACT.Sqrt),
                reads=(("lnr0", tag),), writes=(("lnr0", tag),))
            tk.emit("vector", lambda e, sl=sl: e.reciprocal(
                out=self.R_b[0:1, sl], in_=lnr[0:1, 0, :]),
                reads=(("lnr0", tag),), writes=("R_b",))
            tk.emit("vector", lambda e, sl=sl: e.tensor_tensor(
                out=self.A_b[0:1, sl], in0=self.A_b[0:1, sl],
                in1=self.R_b[0:1, sl], op=ALU.mult),
                reads=("A_b", "R_b"), writes=("A_b",))
        self.bcast(lambda c0, cw: self.R_b[0:1, c0:c0 + cw], ("R_b",),
                   lambda c0, cw: self.R_b[:, c0:c0 + cw], "R_b", T)
        self.bcast(lambda c0, cw: self.A_b[0:1, c0:c0 + cw], ("A_b",),
                   lambda c0, cw: self.A_b[:, c0:c0 + cw], "A_b", T)

    def normalize(self, out_ap, c, t0, n, out_region, tag=None):
        tk, xT = self.tk, self.xT
        w0, w1 = t0 // WIN, (t0 + n - 1) // WIN
        xreads = tuple(("x", c, w) for w in range(w0, w1 + 1))
        tk.emit("vector", lambda e: e.tensor_tensor(
            out=out_ap, in0=xT[:, c, t0:t0 + n], in1=self.R_b[:, t0:t0 + n],
            op=ALU.mult), reads=xreads + ("R_b",), writes=(out_region,))
        tk.emit("vector", lambda e: e.tensor_tensor(
            out=out_ap, in0=out_ap, in1=self.A_b[:, t0:t0 + n], op=ALU.subtract),
            reads=(out_region, "A_b"), writes=(out_region,))

    def emit_l0ffn(self, tag):
        nc, tk, xT = self.nc, self.tk, self.xT
        wr_h = self.win("ffn0_wrT", (D, D))
        wk_h = self.win("ffn0_wkT", (D, D))
        wv_h = self.win("ffn0_wvT", (D, D))
        if "mu_r" not in self._vec_slot:
            self.load_vecs(["mu_r", "mu_k"], self.win("ffn0_mu", (128, 2, NCH)), "ffnmu")
        mu_r, mu_k = self.vec("mu_r"), self.vec("mu_k")

        with ExitStack() as lctx:
            with ExitStack() as sctx:
                self.ln_stats(sctx, tag)
                tk.barrier()
            wbuf = [self.sb(f"{tag}_wa", [128, NCH, 512], ctx=lctx),
                    self.sb(f"{tag}_wb", [128, NCH, 512], ctx=lctx)]
            wreg = [(tag, "wa"), (tag, "wb")]
            wch = [f"{tag}_wa", f"{tag}_wb"]
            xn = self.sb(f"{tag}_xn", [128, NCH, WIN + 1], ctx=lctx)
            xr_ = self.sb(f"{tag}_xr", [128, NCH, WIN], ctx=lctx)
            xk_ = self.sb(f"{tag}_xk", [128, NCH, WIN], ctx=lctx)
            r_ = self.sb(f"{tag}_r", [128, NCH, WIN], ctx=lctx)
            ka = self.sb(f"{tag}_ka", [128, NCH, WIN], ctx=lctx)
            tmp = self.sb(f"{tag}_tmp", [128, WIN], ctx=lctx)

            seq = []
            for w in range(NW):
                for h_ in [(wr_h, 0), (wr_h, 1), (wk_h, 0), (wk_h, 1),
                           (wv_h, 0), (wv_h, 1)]:
                    seq.append(h_)

            def load(i):
                if i >= len(seq):
                    return
                h, half = seq[i]
                bi = i % 2
                h_r = h.rearrange("(c p) d -> p c d", p=128)
                tk.emit("sync", lambda e, h_r=h_r, bi=bi, half=half: e.dma_start(
                    out=wbuf[bi][:], in_=h_r[:, :, half * 512:(half + 1) * 512]),
                    writes=(wreg[bi],), dma=wch[bi])
            load(0)
            load(1)

            for w in range(NW):
                t0 = w * WIN
                if w == 0:
                    tk.emit("gpsimd", lambda e: e.memset(xn[:, :, 0:1], 0.0),
                            writes=((tag, "xnh"),))
                else:
                    tk.emit("vector", lambda e: e.tensor_copy(
                        out=xn[:, :, 0:1], in_=xn[:, :, WIN:WIN + 1]),
                        reads=tuple((tag, "xn", c) for c in range(NCH)),
                        writes=((tag, "xnh"),))
                for c in range(NCH):
                    self.normalize(xn[:, c, 1:WIN + 1], c, t0, WIN, (tag, "xn", c))
                for c in range(NCH):
                    rds = ((tag, "xn", c), (tag, "xnh"))
                    tk.emit("vector", lambda e, c=c: e.tensor_tensor(
                        out=xr_[:, c, :], in0=xn[:, c, 1:WIN + 1], in1=xn[:, c, 0:WIN],
                        op=ALU.subtract), reads=rds, writes=((tag, "d", c),))
                    tk.emit("vector", lambda e, c=c: e.scalar_tensor_tensor(
                        out=xk_[:, c, :], in0=xr_[:, c, :],
                        scalar=self.vecs[:, mu_k, c:c + 1], op0=ALU.mult,
                        in1=xn[:, c, 0:WIN], op1=ALU.add),
                        reads=((tag, "d", c), ("vecs", "ffnmu")) + rds,
                        writes=((tag, "xk", c),))
                    tk.emit("vector", lambda e, c=c: e.scalar_tensor_tensor(
                        out=xr_[:, c, :], in0=xr_[:, c, :],
                        scalar=self.vecs[:, mu_r, c:c + 1], op0=ALU.mult,
                        in1=xn[:, c, 0:WIN], op1=ALU.add),
                        reads=((tag, "d", c), ("vecs", "ffnmu")) + rds,
                        writes=((tag, "xr", c),))

                for ph in range(6):
                    gi = w * 6 + ph
                    bi = gi % 2
                    half = ph % 2
                    kind = ph // 2
                    src = [xr_, xk_, ka][kind]
                    sreg = [(tag, "xr"), (tag, "xk"), (tag, "ka")][kind]
                    for dj in range(4):
                        dd = half * 4 + dj
                        bk, br = self.bank()
                        self.mm([(lambda c=c, dj=dj, bi=bi: wbuf[bi][:, c, dj * 128:(dj + 1) * 128],
                                  lambda c=c, src=src: src[:, c, :]) for c in range(NCH)],
                                bk[:, 0:WIN], br,
                                reads=tuple(sreg + (c,) for c in range(NCH)) + (wreg[bi],))
                        if kind == 0:
                            tk.emit("scalar", lambda e, dd=dd, bk=bk: e.activation(
                                out=r_[:, dd, :], in_=bk[:, 0:WIN], func=ACT.Sigmoid),
                                reads=(br,), writes=((tag, "r", dd),))
                        elif kind == 1:
                            tk.emit("scalar", lambda e, dd=dd, bk=bk: e.activation(
                                out=ka[:, dd, :], in_=bk[:, 0:WIN], func=ACT.Relu),
                                reads=(br,), writes=((tag, "ka", dd),))
                            tk.emit("scalar", lambda e, dd=dd: e.activation(
                                out=ka[:, dd, :], in_=ka[:, dd, :], func=ACT.Square),
                                reads=((tag, "ka", dd),), writes=((tag, "ka", dd),))
                        else:
                            tk.emit("vector", lambda e, bk=bk, dd=dd: e.tensor_tensor(
                                out=tmp[:], in0=bk[:, 0:WIN], in1=r_[:, dd, :],
                                op=ALU.mult),
                                reads=(br, (tag, "r", dd)), writes=((tag, "tmp"),))
                            tk.emit("vector", lambda e, dd=dd, w=w: e.tensor_tensor(
                                out=xT[:, dd, w * WIN:(w + 1) * WIN],
                                in0=xT[:, dd, w * WIN:(w + 1) * WIN], in1=tmp[:],
                                op=ALU.add),
                                reads=((tag, "tmp"), ("x", dd, w)), writes=(("x", dd, w),))
                    load(gi + 2)

    def emit_l1ffn(self, tag):
        nc, tk, xT = self.nc, self.tk, self.xT
        wq_h = self.win("l1ffn_wqT", (D, D), BF16)
        if "l1ffn_ws" not in self._vec_slot:
            self.load_vecs(["l1ffn_ws", "l1ffn_bias"],
                           self.win("l1ffn_vec", (128, 2, NCH)), "l1ffnv")
        ws, bia = self.vec("l1ffn_ws"), self.vec("l1ffn_bias")

        with ExitStack() as lctx:
            with ExitStack() as sctx:
                self.ln_stats(sctx, tag)
                tk.barrier()
            acc = self.sb(f"{tag}_acc", [128, 2, WIN], ctx=lctx)
            rep = self.sb(f"{tag}_rep", [128, 2, WIN], ctx=lctx)
            xsb = self.sb(f"{tag}_xsb", [128, 2, WIN], ctx=lctx)
            r2 = self.sb(f"{tag}_r2", [128, 2, WIN], ctx=lctx)
            a2 = self.sb(f"{tag}_a2", [128, 2, WIN], ctx=lctx)
            tmp = self.sb(f"{tag}_tmp", [128, 3, 2, WIN], ctx=lctx)
            pmp = self.sb(f"{tag}_pmp", [128, 2, 2, WIN], ctx=lctx)
            ab2 = self.sb(f"{tag}_ab2", [128, 2, WIN], ctx=lctx)
            xq = self.sb(f"{tag}_xq", [128, NCH, 2, WIN], BF16, ctx=lctx)
            wqb = self.sb(f"{tag}_wqb", [128, NCH, D], BF16, ctx=lctx)
            qcl = self.sb(f"{tag}_qcl", [128, 3, 16], ctx=lctx)

            wq_r = wq_h.rearrange("(c p) d -> p c d", p=128)
            tk.emit("sync", lambda e: e.dma_start(out=wqb[:], in_=wq_r[:]),
                    writes=((tag, "wqb"),), dma=f"{tag}_w")

            FBITS = (64.0, 32.0, 16.0, 8.0, 4.0, 2.0, 1.0)
            AMAX_ORDER = (("vector", 0), ("gpsimd", 5), ("vector", 1),
                          ("gpsimd", 6), ("vector", 2), ("gpsimd", 7),
                          ("vector", 3), ("vector", 4))
            for w in range(NW):
                sl = slice(w * WIN, (w + 1) * WIN)
                par = w % 2
                for k, (eng, c) in enumerate(AMAX_ORDER):
                    if eng == "vector":
                        src_ap = tmp[:, 0, par, :]
                        nr = (tag, "qt0", par)
                    else:
                        src_ap = ab2[:, par, :]
                        nr = (tag, "np", par)
                    tk.emit(eng, lambda e, c=c, src_ap=src_ap, sl=sl:
                            e.tensor_tensor(out=src_ap, in0=xT[:, c, sl],
                                            in1=self.R_b[:, sl], op=ALU.mult),
                            reads=(("x", c, w), "R_b"), writes=(nr,))
                    tk.emit(eng, lambda e, src_ap=src_ap, sl=sl:
                            e.tensor_tensor(out=src_ap, in0=src_ap,
                                            in1=self.A_b[:, sl],
                                            op=ALU.subtract),
                            reads=(nr, "A_b"), writes=(nr,))
                    slot = 1 + (k % 2)
                    ab_ap = tmp[:, slot, par, :]
                    ar = (tag, f"qt{slot}", par)
                    tk.emit("scalar", lambda e, src_ap=src_ap, ab_ap=ab_ap:
                            e.activation(out=ab_ap, in_=src_ap,
                                         func=ACT.Abs),
                            reads=(nr,), writes=(ar,))
                    if k == 0:
                        tk.emit("vector", lambda e, ab_ap=ab_ap, par=par:
                                e.tensor_copy(out=acc[:, par, :], in_=ab_ap),
                                reads=(ar,), writes=((tag, "acc", par),))
                    else:
                        tk.emit("vector", lambda e, ab_ap=ab_ap, par=par:
                                e.tensor_tensor(out=acc[:, par, :],
                                                in0=acc[:, par, :],
                                                in1=ab_ap, op=ALU.max),
                                reads=(ar, (tag, "acc", par)),
                                writes=((tag, "acc", par),))
                for lb in range(4):
                    b = 4 * w + lb
                    bkk, br = self.bank()
                    tk.emit("tensor", lambda e, lb=lb, par=par, bkk=bkk:
                            e.transpose(bkk[:, 0:128],
                                        in_=acc[:, par,
                                                lb * 128:(lb + 1) * 128],
                                        identity=self.ident[:]),
                            reads=((tag, "acc", par), "ident"), writes=(br,))
                    tk.emit("vector", lambda e, b=b, bkk=bkk: e.tensor_reduce(
                        out=qcl[:, 0, b:b + 1], in_=bkk[:, 0:128], op=ALU.max,
                        axis=AXL.X), reads=(br,), writes=((tag, "amc", w),))
                bsl = slice(4 * w, 4 * w + 4)
                tk.emit("vector", lambda e, bsl=bsl: e.tensor_scalar(
                    out=qcl[:, 1, bsl], in0=qcl[:, 0, bsl], scalar1=1e-6,
                    op0=ALU.max, scalar2=1.0 / 127.0, op1=ALU.mult),
                    reads=((tag, "amc", w),), writes=((tag, "xsc", w),))
                tk.emit("vector", lambda e, bsl=bsl: e.reciprocal(
                    out=qcl[:, 2, bsl], in_=qcl[:, 1, bsl]),
                    reads=((tag, "xsc", w),), writes=((tag, "rbc", w),))
                for lb in range(4):
                    b = 4 * w + lb
                    for j, reg in ((1, (tag, "xsr", par)),
                                   (2, (tag, "rbr", par))):
                        bkk, br = self.bank()
                        tk.emit("tensor", lambda e, b=b, j=j, bkk=bkk:
                                e.transpose(bkk[0:1, 0:128],
                                            in_=qcl[:, j, b:b + 1],
                                            identity=self.ident[:]),
                                reads=((tag, "xsc", w), (tag, "rbc", w),
                                       "ident"), writes=(br,))
                        dst = xsb if j == 1 else rep
                        tk.emit("vector", lambda e, lb=lb, par=par, bkk=bkk,
                                dst=dst, reg=reg:
                                e.tensor_copy(out=dst[0:1, par,
                                                      lb * 128:
                                                      (lb + 1) * 128],
                                              in_=bkk[0:1, 0:128]),
                                reads=(br,), writes=(reg,))
                self.bcast(lambda c0, cw, par=par: xsb[0:1, par, c0:c0 + cw],
                           ((tag, "xsr", par),),
                           lambda c0, cw, par=par: xsb[:, par, c0:c0 + cw],
                           (tag, "xsb", par), WIN)
                self.bcast(lambda c0, cw, par=par: rep[0:1, par, c0:c0 + cw],
                           ((tag, "rbr", par),),
                           lambda c0, cw, par=par: rep[:, par, c0:c0 + cw],
                           (tag, "rb", par), WIN)
                tk.emit("gpsimd", lambda e, sl=sl, par=par: e.tensor_tensor(
                    out=r2[:, par, :], in0=self.R_b[:, sl],
                    in1=rep[:, par, :], op=ALU.mult),
                    reads=("R_b", (tag, "rb", par)),
                    writes=((tag, "r2", par),))
                tk.emit("gpsimd", lambda e, sl=sl, par=par: e.tensor_tensor(
                    out=a2[:, par, :], in0=self.A_b[:, sl],
                    in1=rep[:, par, :], op=ALU.mult),
                    reads=("A_b", (tag, "rb", par)),
                    writes=((tag, "a2", par),))
                tk.emit("vector", lambda e, par=par: e.tensor_scalar(
                    out=a2[:, par, :], in0=a2[:, par, :], scalar1=128.5,
                    op0=ALU.subtract, scalar2=None),
                    reads=((tag, "a2", par),), writes=((tag, "a2", par),))
                for c in range(NCH):
                    t0r = (tag, "qt0", par)
                    t1r = (tag, "qt1", par)
                    t2r = (tag, "qt2", par)
                    S = tmp
                    tk.emit("vector", lambda e, c=c, S=S, sl=sl, par=par:
                            e.tensor_tensor(out=S[:, 0, par, :],
                                            in0=xT[:, c, sl],
                                            in1=r2[:, par, :], op=ALU.mult),
                            reads=(("x", c, w), (tag, "r2", par)),
                            writes=(t0r,))
                    tk.emit("vector", lambda e, S=S, par=par: e.tensor_tensor(
                        out=S[:, 0, par, :], in0=S[:, 0, par, :],
                        in1=a2[:, par, :], op=ALU.subtract),
                        reads=(t0r, (tag, "a2", par)), writes=(t0r,))
                    tk.emit("vector", lambda e, S=S, par=par: e.tensor_scalar(
                        out=S[:, 2, par, :], in0=S[:, 0, par, :],
                        scalar1=128.0, op0=ALU.is_ge, scalar2=None),
                        reads=(t0r,), writes=(t2r,))
                    tk.emit("vector", lambda e, S=S, par=par:
                            e.scalar_tensor_tensor(
                                out=S[:, 1, par, :], in0=S[:, 2, par, :],
                                scalar=-128.0, op0=ALU.mult,
                                in1=S[:, 0, par, :], op1=ALU.add),
                            reads=(t0r, t2r), writes=(t1r,))
                    for fb in FBITS:
                        tk.emit("vector", lambda e, S=S, par=par, fb=fb:
                                e.tensor_scalar(out=S[:, 2, par, :],
                                                in0=S[:, 1, par, :],
                                                scalar1=fb, op0=ALU.is_ge,
                                                scalar2=None),
                                reads=(t1r,), writes=(t2r,))
                        tk.emit("vector", lambda e, S=S, par=par, fb=fb:
                                e.scalar_tensor_tensor(
                                    out=S[:, 1, par, :], in0=S[:, 2, par, :],
                                    scalar=-fb, op0=ALU.mult,
                                    in1=S[:, 1, par, :], op1=ALU.add),
                                reads=(t1r, t2r), writes=(t1r,))
                    tk.emit("vector", lambda e, c=c, S=S, par=par:
                            e.scalar_tensor_tensor(
                                out=xq[:, c, par, :], in0=S[:, 0, par, :],
                                scalar=-128.0, op0=ALU.add,
                                in1=S[:, 1, par, :], op1=ALU.subtract),
                            reads=(t0r, t1r), writes=((tag, "xq", c, par),))
                for dd in range(NCH):
                    bk, br = self.bank()
                    self.mm([(lambda c=c, dd=dd: wqb[:, c, dd * 128:
                                                     (dd + 1) * 128],
                              lambda c=c, par=par: xq[:, c, par, :])
                             for c in range(NCH)],
                            bk[:, 0:WIN], br,
                            reads=tuple((tag, "xq", c, par)
                                        for c in range(NCH))
                                  + ((tag, "wqb"),))
                    tk.emit("scalar", lambda e, bk=bk, dd=dd, par=par:
                            e.activation(out=pmp[:, 0, par, :],
                                         in_=bk[:, 0:WIN], func=ACT.Copy,
                                         scale=self.vecs[:, ws, dd:dd + 1]),
                            reads=(br, ("vecs", "l1ffnv")),
                            writes=((tag, "ep0", par),))
                    tk.emit("gpsimd", lambda e, par=par: e.tensor_tensor(
                        out=pmp[:, 1, par, :], in0=pmp[:, 0, par, :],
                        in1=xsb[:, par, :], op=ALU.mult),
                        reads=((tag, "ep0", par), (tag, "xsb", par)),
                        writes=((tag, "ep1", par),))
                    tk.emit("vector", lambda e, dd=dd, w=w, sl=sl, par=par:
                            e.scalar_tensor_tensor(
                                out=xT[:, dd, sl], in0=pmp[:, 1, par, :],
                                scalar=self.vecs[:, bia, dd:dd + 1],
                                op0=ALU.add, in1=xT[:, dd, sl], op1=ALU.add),
                            reads=((tag, "ep1", par), ("vecs", "l1ffnv"),
                                   ("x", dd, w)),
                            writes=(("x", dd, w),))

    def col_stats(self, lctx, tag):
        tk = self.tk
        rc = self.sb(f"{tag}_rc", [128, 16], ctx=lctx)
        nac = self.sb(f"{tag}_nac", [128, 16], ctx=lctx)
        for tc in range(16):
            tk.emit("tensor", lambda e, tc=tc: e.transpose(
                self.ps[7][:, 0:1], in_=self.R_b[0:1, tc * 128:(tc + 1) * 128],
                identity=self.ident[0:1, 0:1]),
                reads=("R_b", "ident"), writes=(("ps", 7),))
            tk.emit("vector", lambda e, tc=tc: e.tensor_copy(
                out=rc[:, tc:tc + 1], in_=self.ps[7][:, 0:1]),
                reads=(("ps", 7),), writes=((tag, "rc"),))
            tk.emit("tensor", lambda e, tc=tc: e.transpose(
                self.ps[7][:, 0:1], in_=self.A_b[0:1, tc * 128:(tc + 1) * 128],
                identity=self.ident[0:1, 0:1]),
                reads=("A_b", "ident"), writes=(("ps", 7),))
            tk.emit("vector", lambda e, tc=tc: e.tensor_scalar(
                out=nac[:, tc:tc + 1], in0=self.ps[7][:, 0:1], scalar1=-1.0,
                op0=ALU.mult, scalar2=None),
                reads=(("ps", 7),), writes=((tag, "nac"),))
        return rc, nac

    def emit_l1aa(self, tag):
        nc, tk, xT = self.nc, self.tk, self.xT
        wq_h = self.win("l1aa_wq", (D, D))
        wk_h = self.win("l1aa_wk", (D, D))
        wv_h = self.win("l1aa_wv", (D, D))
        wo_h = self.win("l1aa_wo", (D, D))
        if "l1aa_bq" not in self._vec_slot:
            self.load_vecs(["l1aa_bq", "l1aa_bk", "l1aa_bv", "l1aa_bo"],
                           self.win("l1aa_bias", (128, 4, NCH)), "l1aav")
        bq, bk_, bv, bo = (self.vec("l1aa_bq"), self.vec("l1aa_bk"),
                           self.vec("l1aa_bv"), self.vec("l1aa_bo"))

        with ExitStack() as lctx:
            with ExitStack() as sctx:
                self.ln_stats(sctx, tag)
                tk.barrier()
            yac = self.sb(f"{tag}_yac", [128, NCH, T], ctx=lctx)
            qt = self.sb(f"{tag}_qt", [128, T], ctx=lctx)
            kt = self.sb(f"{tag}_kt", [128, T], ctx=lctx)
            vg = self.sb(f"{tag}_vg", [128, 16, 130], ctx=lctx)
            cmask = self.sb(f"{tag}_cmask", [128, 4, WIN], ctx=lctx)
            for j in range(4):
                tk.emit("vector", lambda e, j=j: e.tensor_scalar(
                    out=cmask[:, j, 0:1], in0=self.pcol[:], scalar1=float(128 * j),
                    op0=ALU.add, scalar2=None),
                    reads=("pcol",), writes=((tag, "cth", j),))
                tk.emit("vector", lambda e, j=j: e.tensor_scalar(
                    out=cmask[:, j, :], in0=self.firow[:],
                    scalar1=cmask[:, j, 0:1], op0=ALU.is_ge, scalar2=None),
                    reads=("firow", (tag, "cth", j)), writes=((tag, "cmask"),))
            es = self.sb(f"{tag}_es", [128, 2, WIN], ctx=lctx)
            sq = es[:, 0, :]
            wbuf = [self.sb(f"{tag}_wba", [128, D], ctx=lctx),
                    self.sb(f"{tag}_wbb", [128, D], ctx=lctx)]
            aoh = self.sb(f"{tag}_aoh", [128, 2, WIN], ctx=lctx)
            woB = self.sb(f"{tag}_woB", [128, D], ctx=lctx)
            rr = self.sb(f"{tag}_rr", [128, 2, WIN], ctx=lctx)
            wvbt = self.sb(f"{tag}_wvbt", [128, 128], ctx=lctx)
            bvbt = self.sb(f"{tag}_bvbt", [128, 128], ctx=lctx)
            mng = self.sb(f"{tag}_mng", [128, 2, 16], ctx=lctx)
            mrow = self.sb(f"{tag}_mrow", [128, 56], ctx=lctx)
            tk.emit("vector", lambda e: e.memset(mrow[:], 0.0),
                    writes=((tag, "mneg", 0), (tag, "mneg", 1)))
            nws = self.sb(f"{tag}_nws", [128, 4], ctx=lctx)
            tca = es[:, 0, :]
            ones_t = self.sb(f"{tag}_ones", [128, 1], ctx=lctx)
            tk.emit("gpsimd", lambda e: e.memset(ones_t[:], 1.0),
                    writes=((tag, "ones"),))
            rc, nac = self.col_stats(lctx, tag)

            wq_r = wq_h.rearrange("(c p) d -> p c d", p=128)
            wk_r = wk_h.rearrange("(c p) d -> p c d", p=128)
            wv_r = wv_h.rearrange("(c p) d -> p c d", p=128)
            wo_r = wo_h.rearrange("(g p) d -> p g d", p=128)
            xr_all = tuple(("x", c, w) for c in range(NCH) for w in range(NW))

            def wload(g, i):
                bi = (4 * g + i) % 2
                srcs = (wq_r[:, :, g * 128:(g + 1) * 128],
                        wk_r[:, :, g * 128:(g + 1) * 128],
                        wv_r[:, :, g * 128:(g + 1) * 128], wo_r[:, g, :])
                src = srcs[i]
                tk.emit("sync", lambda e, src=src, bi=bi: e.dma_start(
                    out=(wbuf[bi][:].rearrange("p (c d) -> p c d", d=128)
                         if i < 3 else wbuf[bi][:]), in_=src),
                    writes=((tag, "wb", bi),), dma=f"{tag}_w{bi}")
                return bi, (tag, "wb", bi)

            for g in range(8):
                for wi, (dst, bslot, dreg) in enumerate(
                        ((qt, bq, (tag, "qt")), (kt, bk_, (tag, "kt")))):
                    bi, wreg = wload(g, wi)
                    wv3 = lambda bi=bi: wbuf[bi][:].rearrange("p (c d) -> p c d", d=128)
                    self.mm([(lambda c=c, wv3=wv3: wv3()[:, c, :],
                              lambda: ones_t[:, :]) for c in range(NCH)],
                            self.ps[7][:, 0:1], ("ps", 7),
                            reads=(wreg, (tag, "ones")))
                    tk.emit("vector", lambda e, wi=wi: e.tensor_scalar(
                        out=nws[:, wi:wi + 1], in0=self.ps[7][:, 0:1],
                        scalar1=-1.0, op0=ALU.mult, scalar2=None),
                        reads=(("ps", 7),), writes=((tag, "nws", wi),))
                    for w in range(NW):
                        bkk, br = self.bank()
                        self.mm([(lambda c=c, wv3=wv3: wv3()[:, c, :],
                                  lambda c=c, w=w: xT[:, c, w * WIN:(w + 1) * WIN])
                                 for c in range(NCH)],
                                bkk[:, 0:WIN], br, reads=xr_all + (wreg,))
                        tk.emit("vector", lambda e, bkk=bkk, w=w: e.tensor_tensor(
                            out=tca[:], in0=bkk[:, 0:WIN],
                            in1=self.R_b[:, w * WIN:(w + 1) * WIN], op=ALU.mult),
                            reads=(br, "R_b"), writes=((tag, "es", 0),))
                        tk.emit("vector", lambda e, w=w, wi=wi: e.scalar_tensor_tensor(
                            out=tca[:], in0=self.A_b[:, w * WIN:(w + 1) * WIN],
                            scalar=nws[:, wi:wi + 1], op0=ALU.mult,
                            in1=tca[:], op1=ALU.add),
                            reads=((tag, "es", 0), (tag, "nws", wi), "A_b"),
                            writes=((tag, "es", 0),))
                        tk.emit("vector", lambda e, w=w, dst=dst, bslot=bslot, g=g,
                                dreg=dreg: e.tensor_scalar(
                                    out=dst[:, w * WIN:(w + 1) * WIN], in0=tca[:],
                                    scalar1=self.vecs[:, bslot, g:g + 1],
                                    op0=ALU.add, scalar2=None),
                                reads=((tag, "es", 0), ("vecs", "l1aav")),
                                writes=(dreg,))
                bi, wreg = wload(g, 2)
                wv3 = lambda bi=bi: wbuf[bi][:].rearrange("p (c d) -> p c d", d=128)
                self.mm([(lambda: ones_t[:, :],
                          lambda c=c, wv3=wv3: wv3()[:, c, :]) for c in range(NCH)],
                        self.ps[7][0:1, 0:128], ("ps", 7),
                        reads=(wreg, (tag, "ones")))
                tk.emit("vector", lambda e: e.tensor_copy(
                    out=aoh[0:1, 0, 0:128], in_=self.ps[7][0:1, 0:128]),
                    reads=(("ps", 7), (tag, "ao", 0)), writes=((tag, "ao", 0),))
                self.bcast(lambda c0, cw: aoh[0:1, 0, c0:c0 + cw],
                           ((tag, "ao", 0),),
                           lambda c0, cw: wvbt[:, c0:c0 + cw], (tag, "wvb"), 128)
                tk.emit("tensor", lambda e, g=g: e.transpose(
                    self.ps[7][0:1, 0:128], in_=self.vecs[:, bv, g:g + 1],
                    identity=self.ident[:]),
                    reads=(("vecs", "l1aav"), "ident"), writes=(("ps", 7),))
                tk.emit("vector", lambda e: e.tensor_copy(
                    out=aoh[0:1, 1, 0:128], in_=self.ps[7][0:1, 0:128]),
                    reads=(("ps", 7), (tag, "ao", 1)), writes=((tag, "ao", 1),))
                self.bcast(lambda c0, cw: aoh[0:1, 1, c0:c0 + cw],
                           ((tag, "ao", 1),),
                           lambda c0, cw: bvbt[:, c0:c0 + cw], (tag, "bvb"), 128)
                for h2 in range(2):
                    tk.emit("gpsimd", lambda e, h2=h2: e.memset(
                        vg[:, :, h2 * 65 + 64:h2 * 65 + 65], 1.0),
                        writes=((tag, "vones", h2),))
                for tc in range(16):
                    bkk, br = self.bank()
                    self.mm([(lambda c=c, tc=tc: xT[:, c, tc * 128:(tc + 1) * 128],
                              lambda c=c, wv3=wv3: wv3()[:, c, :])
                             for c in range(NCH)],
                            bkk[:, 0:128], br, reads=xr_all + (wreg,))
                    tk.emit("vector", lambda e, tc=tc: e.scalar_tensor_tensor(
                        out=tca[:, 0:128], in0=wvbt[:],
                        scalar=nac[:, tc:tc + 1], op0=ALU.mult,
                        in1=bvbt[:], op1=ALU.add),
                        reads=((tag, "wvb"), (tag, "nac"), (tag, "bvb")),
                        writes=((tag, "es", 0),))
                    for h2 in range(2):
                        tk.emit("vector", lambda e, bkk=bkk, tc=tc, h2=h2:
                                e.scalar_tensor_tensor(
                                    out=vg[:, tc, h2 * 65:h2 * 65 + 64],
                                    in0=bkk[:, h2 * 64:h2 * 64 + 64],
                                    scalar=rc[:, tc:tc + 1], op0=ALU.mult,
                                    in1=tca[:, h2 * 64:h2 * 64 + 64], op1=ALU.add),
                                reads=(br, (tag, "rc"), (tag, "es", 0)),
                                writes=((tag, "v", tc),))
                obi, owreg = wload(g, 3)
                tk.emit("sync", lambda e, g=g: e.dma_start(
                    out=woB[0:64, :], in_=wo_h[g * 128 + 64:g * 128 + 128, :]),
                    writes=((tag, "woB"),), dma=f"{tag}_woB")

                for src_, srg, mi in ((qt, (tag, "qt"), 0), (kt, (tag, "kt"), 1)):
                    for h2 in range(2):
                        for w in range(NW):
                            tk.emit("scalar", lambda e, src_=src_, h2=h2, w=w:
                                    e.activation(
                                        out=sq[h2 * 64:h2 * 64 + 64, :],
                                        in_=src_[h2 * 64:h2 * 64 + 64,
                                                 w * WIN:(w + 1) * WIN],
                                        func=ACT.Square),
                                    reads=(srg,), writes=((tag, "es", 0),))
                            bkk, br = self.bank()
                            self.mm([(lambda h2=h2: ones_t[h2 * 64:h2 * 64 + 64, :],
                                      lambda h2=h2: sq[h2 * 64:h2 * 64 + 64, :])],
                                    bkk[0:1, 0:WIN], br,
                                    reads=((tag, "es", 0), (tag, "ones")))
                            tk.emit("vector", lambda e, bkk=bkk, mi=mi, h2=h2, w=w:
                                    e.tensor_reduce(
                                        out=mrow[0:1, mi * 8 + h2 * 4 + w:
                                                 mi * 8 + h2 * 4 + w + 1],
                                        in_=bkk[0:1, 0:WIN], op=ALU.max, axis=AXL.X),
                                    reads=(br,), writes=((tag, "mr", mi, h2, w),))
                for h2 in range(2):
                    rd = tuple((tag, "mr", mi, h2, w) for mi in range(2)
                               for w in range(NW))
                    tk.emit("vector", lambda e, h2=h2: e.tensor_reduce(
                        out=mrow[0:1, 16 + h2:17 + h2],
                        in_=mrow[0:1, h2 * 4:h2 * 4 + 4],
                        op=ALU.max, axis=AXL.X), reads=rd, writes=((tag, "mq", h2),))
                    tk.emit("vector", lambda e, h2=h2: e.tensor_reduce(
                        out=mrow[0:1, 18 + h2:19 + h2],
                        in_=mrow[0:1, 8 + h2 * 4:8 + h2 * 4 + 4],
                        op=ALU.max, axis=AXL.X), reads=rd, writes=((tag, "mk", h2),))
                    tk.emit("vector", lambda e, h2=h2: e.tensor_tensor(
                        out=mrow[0:1, 20 + h2:21 + h2], in0=mrow[0:1, 16 + h2:17 + h2],
                        in1=mrow[0:1, 18 + h2:19 + h2], op=ALU.mult),
                        reads=((tag, "mq", h2), (tag, "mk", h2)),
                        writes=((tag, "mm2", h2),))
                    tk.emit("scalar", lambda e, h2=h2: e.activation(
                        out=mrow[0:1, 22 + h2:23 + h2], in_=mrow[0:1, 20 + h2:21 + h2],
                        func=ACT.Sqrt), reads=((tag, "mm2", h2),),
                        writes=((tag, "ms", h2),))
                    tk.emit("vector", lambda e, h2=h2: e.tensor_scalar(
                        out=mrow[0:1, 24 + 16 * h2:25 + 16 * h2],
                        in0=mrow[0:1, 22 + h2:23 + h2],
                        scalar1=-0.125, op0=ALU.mult, scalar2=None),
                        reads=((tag, "ms", h2),), writes=((tag, "mneg", h2),))
                    self.bcast(lambda c0, cw, h2=h2:
                               mrow[0:1, 24 + 16 * h2:24 + 16 * h2 + cw],
                               ((tag, "mneg", h2),),
                               lambda c0, cw, h2=h2: mng[:, h2, c0:c0 + cw],
                               (tag, "mnegb", h2), 16)

                for w in range(NW):
                    for h2 in range(2):
                        hp = h2 * 64
                        obk, obr = self.ps[6 + h2], ("ps", 6 + h2)
                        kcs = list(range(4 * w + 4))
                        first = True
                        for c16 in kcs:
                            sbk, sbr = self.bank()
                            self.mm([(lambda c16=c16, hp=hp: kt[hp:hp + 64,
                                                               c16 * 128:(c16 + 1) * 128],
                                      lambda w=w, hp=hp: qt[hp:hp + 64,
                                                            w * WIN:(w + 1) * WIN])],
                                    sbk[0:128, 0:WIN], sbr,
                                    reads=((tag, "kt"), (tag, "qt")))
                            ei = c16 % 2
                            tk.emit("scalar", lambda e, sbk=sbk, ei=ei, h2=h2:
                                    e.activation(
                                        out=es[:, ei, :], in_=sbk[:, 0:WIN],
                                        func=ACT.Exp,
                                        bias=mng[:, h2, 0:1], scale=0.125),
                                    reads=(sbr, (tag, "mnegb", h2)),
                                    writes=((tag, "es", ei),))
                            off = w * WIN - c16 * 128
                            if off < 127:
                                j = (-off) // 128
                                tk.emit("vector", lambda e, ei=ei, j=j: e.tensor_tensor(
                                    out=es[:, ei, :], in0=es[:, ei, :],
                                    in1=cmask[:, j, :], op=ALU.mult),
                                    reads=((tag, "es", ei), (tag, "cmask")),
                                    writes=((tag, "es", ei),))
                            self.mm([(lambda c16=c16, h2=h2: vg[:, c16,
                                                               h2 * 65:h2 * 65 + 65],
                                      lambda ei=ei: es[:, ei, :])],
                                    obk[0:65, 0:WIN], obr,
                                    reads=((tag, "es", ei), (tag, "v", c16),
                                           (tag, "vones", h2)),
                                    start_clear=first, stop=(c16 == kcs[-1]))
                            first = False
                        tk.emit("vector", lambda e, obk=obk: e.reciprocal(
                            out=rr[64:65, 0, :], in_=obk[64:65, 0:WIN]),
                            reads=(obr, (tag, "zpl")), writes=((tag, "zpl"),))
                        self.bcast(lambda c0, cw: rr[64:65, 0, c0:c0 + cw],
                                   ((tag, "zpl"),),
                                   lambda c0, cw: rr[:, 1, c0:c0 + cw],
                                   (tag, "rcpb"), WIN, base=64)
                        tk.emit("vector", lambda e, obk=obk, h2=h2:
                                e.tensor_tensor(
                                    out=aoh[0:64, h2, :],
                                    in0=obk[0:64, 0:WIN], in1=rr[0:64, 1, :],
                                    op=ALU.mult),
                                reads=(obr, (tag, "rcpb")),
                                writes=((tag, "ao", h2),))
                    for dd in range(NCH):
                        bkk, br = self.bank()
                        self.mm([(lambda dd=dd, obi=obi:
                                  wbuf[obi][0:64, dd * 128:(dd + 1) * 128],
                                  lambda: aoh[0:64, 0, :]),
                                 (lambda dd=dd: woB[0:64, dd * 128:(dd + 1) * 128],
                                  lambda: aoh[0:64, 1, :])],
                                bkk[:, 0:WIN], br,
                                reads=((tag, "ao", 0), (tag, "ao", 1), owreg,
                                       (tag, "woB")))
                        if g == 0:
                            tk.emit("vector", lambda e, bkk=bkk, dd=dd, w=w:
                                    e.tensor_scalar(
                                        out=yac[:, dd, w * WIN:(w + 1) * WIN],
                                        in0=bkk[:, 0:WIN],
                                        scalar1=self.vecs[:, bo, dd:dd + 1],
                                        op0=ALU.add, scalar2=None),
                                    reads=(br, ("vecs", "l1aav")),
                                    writes=((tag, "y", dd, w),))
                        else:
                            tk.emit("vector", lambda e, bkk=bkk, dd=dd, w=w:
                                    e.tensor_tensor(
                                        out=yac[:, dd, w * WIN:(w + 1) * WIN],
                                        in0=yac[:, dd, w * WIN:(w + 1) * WIN],
                                        in1=bkk[:, 0:WIN], op=ALU.add),
                                    reads=(br, (tag, "y", dd, w)),
                                    writes=((tag, "y", dd, w),))
            for dd in range(NCH):
                for w in range(NW):
                    tk.emit("vector", lambda e, dd=dd, w=w: e.tensor_tensor(
                        out=xT[:, dd, w * WIN:(w + 1) * WIN],
                        in0=xT[:, dd, w * WIN:(w + 1) * WIN],
                        in1=yac[:, dd, w * WIN:(w + 1) * WIN], op=ALU.add),
                        reads=((tag, "y", dd, w), ("x", dd, w)),
                        writes=(("x", dd, w),))

    def emit_l1ab(self, tag):
        import math
        nc, tk, xT = self.nc, self.tk, self.xT
        wq_h = self.win("l1ab_wqT", (D, 128))
        wk_h = self.win("l1ab_wkT", (D, 128))
        wv_h = self.win("l1ab_wvT", (D, 128))
        wo_h = self.win("l1ab_woT", (128, D))
        tg_h = self.win("l1ab_trig", (128, 2, 16, 64))

        SC = 1.0 / math.sqrt(128.0)
        SC_HI = float(np.float32(SC)); SC_LO = float(np.float32(SC - SC_HI))

        with ExitStack() as lctx:
            with ExitStack() as sctx:
                self.ln_stats(sctx, tag)
                tk.barrier()
            qv = self.sb(f"{tag}_q", [128, 16, 128], ctx=lctx)
            kv = self.sb(f"{tag}_k", [128, 16, 128], ctx=lctx)
            vv = self.sb(f"{tag}_v", [128, 16, 128], ctx=lctx)
            cs_ = self.sb(f"{tag}_cs", [128, 16, 64], ctx=lctx)
            sn = self.sb(f"{tag}_sn", [128, 16, 64], ctx=lctx)
            qtf = self.sb(f"{tag}_qtf", [128, T], ctx=lctx)
            ktf = self.sb(f"{tag}_ktf", [128, T], ctx=lctx)
            ao = self.sb(f"{tag}_ao", [128, T], ctx=lctx)
            es = self.sb(f"{tag}_es", [128, 3, WIN], ctx=lctx)
            wq_s = self.sb(f"{tag}_wq", [128, NCH, 128], ctx=lctx)
            wk_s = self.sb(f"{tag}_wk", [128, NCH, 128], ctx=lctx)
            wv_s = self.sb(f"{tag}_wv", [128, NCH, 128], ctx=lctx)
            wo_s = self.sb(f"{tag}_wo", [128, D], ctx=lctx)
            rr = self.sb(f"{tag}_rr", [128, 6, WIN], ctx=lctx)
            mrow = self.sb(f"{tag}_mrow", [128, 32], ctx=lctx)
            tk.emit("vector", lambda e: e.memset(mrow[:], 0.0),
                    writes=((tag, "mneg"),))
            tca = self.sb(f"{tag}_tca", [128, WIN], ctx=lctx)
            ones_t = self.sb(f"{tag}_ones", [128, 1], ctx=lctx)
            tk.emit("gpsimd", lambda e: e.memset(ones_t[:], 1.0),
                    writes=((tag, "ones"),))
            rc, nac = self.col_stats(lctx, tag)

            for wsb, wh, ch in ((wq_s, wq_h, "q"), (wk_s, wk_h, "k"),
                                (wv_s, wv_h, "v")):
                w_r = wh.rearrange("(c p) d -> p c d", p=128)
                tk.emit("sync", lambda e, wsb=wsb, w_r=w_r: e.dma_start(
                    out=wsb[:], in_=w_r[:]), writes=((tag, "w", ch),),
                    dma=f"{tag}_w{ch}")
            tk.emit("sync", lambda e: e.dma_start(out=wo_s[:], in_=wo_h[:]),
                    writes=((tag, "w", "o"),), dma=f"{tag}_wo")

            tk.emit("sync", lambda e: e.dma_start(out=sn[:], in_=tg_h[:, 0]),
                    writes=((tag, "sn"),), dma=f"{tag}_tg")
            tk.emit("sync", lambda e: e.dma_start(out=cs_[:], in_=tg_h[:, 1]),
                    writes=((tag, "cs"),), dma=f"{tag}_tg")

            xr_all = tuple(("x", c, w) for c in range(NCH) for w in range(NW))
            for pi_, (wsb, ch) in enumerate(((wq_s, "q"), (wk_s, "k"), (wv_s, "v"))):
                self.mm([(lambda: ones_t[:, :],
                          lambda c=c, wsb=wsb: wsb[:, c, :]) for c in range(NCH)],
                        self.ps[7][0:1, 0:128], ("ps", 7),
                        reads=((tag, "w", ch), (tag, "ones")))
                tk.emit("vector", lambda e: e.tensor_copy(
                    out=rr[0:1, 0, 0:128], in_=self.ps[7][0:1, 0:128]),
                    reads=(("ps", 7),), writes=((tag, "wsrow"),))
                self.bcast(lambda c0, cw: rr[0:1, 0, c0:c0 + cw],
                           ((tag, "wsrow"),),
                           lambda c0, cw, pi_=pi_: rr[:, 2 + pi_, c0:c0 + cw],
                           (tag, "wsb", pi_), 128)

            for pi_, (dst, wsb, ch, dreg) in enumerate(
                    ((qv, wq_s, "q", (tag, "qv")), (kv, wk_s, "k", (tag, "kv")),
                     (vv, wv_s, "v", (tag, "vv")))):
                for tc in range(16):
                    bkk, br = self.bank()
                    self.mm([(lambda c=c, tc=tc: xT[:, c, tc * 128:(tc + 1) * 128],
                              lambda c=c, wsb=wsb: wsb[:, c, :])
                             for c in range(NCH)],
                            bkk[:, 0:128], br, reads=xr_all + ((tag, "w", ch),))
                    tk.emit("vector", lambda e, tc=tc, pi_=pi_: e.tensor_scalar(
                        out=tca[:, 0:128], in0=rr[:, 2 + pi_, 0:128],
                        scalar1=nac[:, tc:tc + 1], op0=ALU.mult, scalar2=None),
                        reads=((tag, "wsb", pi_), (tag, "nac")),
                        writes=((tag, "tca"),))
                    tk.emit("vector", lambda e, bkk=bkk, tc=tc, dst=dst, dreg=dreg:
                            e.scalar_tensor_tensor(
                                out=dst[:, tc, :], in0=bkk[:, 0:128],
                                scalar=rc[:, tc:tc + 1], op0=ALU.mult,
                                in1=tca[:, 0:128], op1=ALU.add),
                            reads=(br, (tag, "rc"), (tag, "tca")),
                            writes=(dreg,))

            for src_, sreg in ((qv, (tag, "qv")), (kv, (tag, "kv"))):
                for tc in range(16):
                    tk.emit("vector", lambda e, src_=src_, tc=tc: e.tensor_tensor(
                        out=tca[:, 0:64], in0=src_[:, tc, 0:128:2],
                        in1=cs_[:, tc, :], op=ALU.mult),
                        reads=(sreg, (tag, "cs")), writes=((tag, "tca"),))
                    tk.emit("vector", lambda e, src_=src_, tc=tc: e.tensor_tensor(
                        out=tca[:, 64:128], in0=src_[:, tc, 1:128:2],
                        in1=sn[:, tc, :], op=ALU.mult),
                        reads=(sreg, (tag, "sn")), writes=((tag, "tca"),))
                    tk.emit("vector", lambda e, src_=src_, tc=tc: e.tensor_tensor(
                        out=tca[:, 128:192], in0=src_[:, tc, 0:128:2],
                        in1=sn[:, tc, :], op=ALU.mult),
                        reads=(sreg, (tag, "sn")), writes=((tag, "tca"),))
                    tk.emit("vector", lambda e, src_=src_, tc=tc: e.tensor_tensor(
                        out=tca[:, 192:256], in0=src_[:, tc, 1:128:2],
                        in1=cs_[:, tc, :], op=ALU.mult),
                        reads=(sreg, (tag, "cs")), writes=((tag, "tca"),))
                    tk.emit("vector", lambda e, src_=src_, tc=tc: e.tensor_tensor(
                        out=src_[:, tc, 0:128:2], in0=tca[:, 0:64],
                        in1=tca[:, 64:128], op=ALU.subtract),
                        reads=((tag, "tca"), (tag, "tca")), writes=(sreg,))
                    tk.emit("vector", lambda e, src_=src_, tc=tc: e.tensor_tensor(
                        out=src_[:, tc, 1:128:2], in0=tca[:, 128:192],
                        in1=tca[:, 192:256], op=ALU.add),
                        reads=((tag, "tca"), (tag, "tca")), writes=(sreg,))

            for src_, sreg, dst, dreg, scale2 in (
                    (qv, (tag, "qv"), qtf, (tag, "qtf"), None),
                    (kv, (tag, "kv"), ktf, (tag, "ktf"), (SC_HI, SC_LO))):
                for tc in range(16):
                    bkk, br = self.bank()
                    tk.emit("tensor", lambda e, src_=src_, tc=tc, bkk=bkk:
                            e.transpose(bkk[:, 0:128], in_=src_[:, tc, :],
                                        identity=self.ident[:]),
                            reads=(sreg, "ident"), writes=(br,))
                    if scale2 is None:
                        tk.emit("vector", lambda e, bkk=bkk, tc=tc, dst=dst:
                                e.tensor_copy(
                                    out=dst[:, tc * 128:(tc + 1) * 128],
                                    in_=bkk[:, 0:128]),
                                reads=(br,), writes=(dreg,))
                    else:
                        tk.emit("vector", lambda e, bkk=bkk, tc=tc, dst=dst:
                                e.tensor_scalar(
                                    out=tca[:, 0:128], in0=bkk[:, 0:128],
                                    scalar1=scale2[0], op0=ALU.mult, scalar2=None),
                                reads=(br,), writes=((tag, "tca"),))
                        tk.emit("vector", lambda e, bkk=bkk, tc=tc, dst=dst:
                                e.tensor_scalar(
                                    out=tca[:, 128:256], in0=bkk[:, 0:128],
                                    scalar1=scale2[1], op0=ALU.mult, scalar2=None),
                                reads=(br,), writes=((tag, "tca"),))
                        tk.emit("vector", lambda e, tc=tc, dst=dst, dreg=dreg:
                                e.tensor_tensor(
                                    out=dst[:, tc * 128:(tc + 1) * 128],
                                    in0=tca[:, 0:128], in1=tca[:, 128:256],
                                    op=ALU.add),
                                reads=((tag, "tca"), (tag, "tca")), writes=(dreg,))

            for mi, (src_, sreg) in enumerate(((qtf, (tag, "qtf")),
                                               (ktf, (tag, "ktf")))):
                for w in range(NW):
                    tk.emit("scalar", lambda e, src_=src_, w=w: e.activation(
                        out=tca[:], in_=src_[:, w * WIN:(w + 1) * WIN],
                        func=ACT.Square), reads=(sreg,), writes=((tag, "tca"),))
                    bkk, br = self.bank()
                    self.mm([(lambda: ones_t[:, :], lambda: tca[:])],
                            bkk[0:1, 0:WIN], br, reads=((tag, "tca"), (tag, "ones")))
                    tk.emit("vector", lambda e, bkk=bkk, mi=mi, w=w: e.tensor_reduce(
                        out=mrow[0:1, mi * 4 + w:mi * 4 + w + 1],
                        in_=bkk[0:1, 0:WIN], op=ALU.max, axis=AXL.X),
                        reads=(br,), writes=((tag, "mr", mi, w),))
            rd = tuple((tag, "mr", mi, w) for mi in range(2) for w in range(NW))
            tk.emit("vector", lambda e: e.tensor_reduce(
                out=mrow[0:1, 8:9], in_=mrow[0:1, 0:4], op=ALU.max, axis=AXL.X),
                reads=rd, writes=((tag, "mq"),))
            tk.emit("vector", lambda e: e.tensor_reduce(
                out=mrow[0:1, 9:10], in_=mrow[0:1, 4:8], op=ALU.max, axis=AXL.X),
                reads=rd, writes=((tag, "mk"),))
            tk.emit("vector", lambda e: e.tensor_tensor(
                out=mrow[0:1, 10:11], in0=mrow[0:1, 8:9], in1=mrow[0:1, 9:10],
                op=ALU.mult), reads=((tag, "mq"), (tag, "mk")),
                writes=((tag, "mm2"),))
            tk.emit("scalar", lambda e: e.activation(
                out=mrow[0:1, 11:12], in_=mrow[0:1, 10:11], func=ACT.Sqrt),
                reads=((tag, "mm2"),), writes=((tag, "ms"),))
            tk.emit("vector", lambda e: e.tensor_scalar(
                out=mrow[0:1, 12:13], in0=mrow[0:1, 11:12], scalar1=-1.0,
                op0=ALU.mult, scalar2=None), reads=((tag, "ms"),),
                writes=((tag, "mneg"),))
            self.bcast(lambda c0, cw: mrow[0:1, 12:12 + cw], ((tag, "mneg"),),
                       lambda c0, cw: rr[:, 5, c0:c0 + cw], (tag, "mnegb"), 16)

            for w in range(NW):
                obk, obr = self.ps[6], ("ps", 6)
                smb, smr = self.ps[7], ("ps", 7)
                for c16 in range(16):
                    sbk, sbr = self.bank()
                    self.mm([(lambda c16=c16: ktf[:, c16 * 128:(c16 + 1) * 128],
                              lambda w=w: qtf[:, w * WIN:(w + 1) * WIN])],
                            sbk[0:128, 0:WIN], sbr,
                            reads=((tag, "ktf"), (tag, "qtf")))
                    ei = c16 % 3
                    tk.emit("scalar", lambda e, sbk=sbk, ei=ei: e.activation(
                        out=es[:, ei, :], in_=sbk[:, 0:WIN], func=ACT.Exp,
                        bias=rr[:, 5, 0:1], scale=1.0),
                        reads=(sbr, (tag, "mnegb")), writes=((tag, "es", ei),))
                    self.mm([(lambda c16=c16: vv[:, c16, :],
                              lambda ei=ei: es[:, ei, :])],
                            obk[0:128, 0:WIN], obr,
                            reads=((tag, "es", ei), (tag, "vv")),
                            start_clear=(c16 == 0), stop=(c16 == 15))
                    self.mm([(lambda: ones_t[:, :], lambda ei=ei: es[:, ei, :])],
                            smb[0:1, 0:WIN], smr,
                            reads=((tag, "es", ei), (tag, "ones")),
                            start_clear=(c16 == 0), stop=(c16 == 15))
                tk.emit("vector", lambda e, smb=smb: e.reciprocal(
                    out=rr[0:1, 0, :], in_=smb[0:1, 0:WIN]),
                    reads=(smr,), writes=((tag, "rcp"),))
                self.bcast(lambda c0, cw: rr[0:1, 0, c0:c0 + cw], ((tag, "rcp"),),
                           lambda c0, cw: rr[:, 1, c0:c0 + cw], (tag, "rcpb"), WIN)
                tk.emit("vector", lambda e, obk=obk, w=w: e.tensor_tensor(
                    out=ao[:, w * WIN:(w + 1) * WIN], in0=obk[:, 0:WIN],
                    in1=rr[:, 1, :], op=ALU.mult),
                    reads=(obr, (tag, "rcpb")), writes=((tag, "ao", w),))

            aor = tuple((tag, "ao", w) for w in range(NW))
            for dd in range(NCH):
                for w in range(NW):
                    bkk, br = self.bank()
                    self.mm([(lambda dd=dd: wo_s[:, dd * 128:(dd + 1) * 128],
                              lambda w=w: ao[:, w * WIN:(w + 1) * WIN])],
                            bkk[:, 0:WIN], br, reads=aor + ((tag, "w", "o"),))
                    tk.emit("vector", lambda e, bkk=bkk, dd=dd, w=w: e.tensor_tensor(
                        out=xT[:, dd, w * WIN:(w + 1) * WIN],
                        in0=xT[:, dd, w * WIN:(w + 1) * WIN],
                        in1=bkk[:, 0:WIN], op=ALU.add),
                        reads=(br, ("x", dd, w)), writes=(("x", dd, w),))

    def emit_l0ab(self, tag):
        nc, tk, xT = self.nc, self.tk, self.xT
        WN, NWN = 256, 8
        wq_h = self.win("l0ab_wq", (D, D))
        wk_h = self.win("l0ab_wk", (D, D))
        wv_h = self.win("l0ab_wv", (D, D))
        wb_h = self.win("l0ab_wb", (128, NCH))
        if "l0ab_g" not in self._vec_slot:
            self.load_vecs(["l0ab_g", "l0ab_b"],
                           self.win("l0ab_vec", (128, 2, NCH)), "l0abv")
        g_s, b_s = self.vec("l0ab_g"), self.vec("l0ab_b")

        with ExitStack() as lctx:
            with ExitStack() as sctx:
                self.ln_stats(sctx, tag)
                tk.barrier()
            S_ = self.sb(f"{tag}_S", [128, NCH, D], ctx=lctx)
            qtw = self.sb(f"{tag}_qt", [128, NCH, WN], ctx=lctx)
            ktw = self.sb(f"{tag}_kt", [128, NCH, WN], ctx=lctx)
            vtw = self.sb(f"{tag}_vt", [128, 2, D], ctx=lctx)
            u_ = self.sb(f"{tag}_u", [128, D], ctx=lctx)
            osb = self.sb(f"{tag}_osb", [128, NCH, 128], ctx=lctx)
            ktok = self.sb(f"{tag}_ktok", [128, NCH, 128], ctx=lctx)
            wbuf = [self.sb(f"{tag}_wba", [128, NCH, 256], ctx=lctx),
                    self.sb(f"{tag}_wbb", [128, NCH, 256], ctx=lctx)]
            wbcol = self.sb(f"{tag}_wbcol", [128, NCH], ctx=lctx)
            m0 = self.sb(f"{tag}_m0", [128, 8, 128], ctx=lctx)
            twoI = self.sb(f"{tag}_twoI", [128, 128], ctx=lctx)
            rows = self.sb(f"{tag}_rows", [128, 6, WN], ctx=lctx)
            bcol = self.sb(f"{tag}_bcol", [128, 2], ctx=lctx)
            wvb = self.sb(f"{tag}_wvb", [128, D], ctx=lctx)
            nws = self.sb(f"{tag}_nws", [128, 4], ctx=lctx)
            tca = self.sb(f"{tag}_tca", [128, D], ctx=lctx)
            rq2 = self.sb(f"{tag}_rq2", [128, T], ctx=lctx)
            ones_t = self.sb(f"{tag}_ones", [128, 1], ctx=lctx)
            tk.emit("gpsimd", lambda e: e.memset(ones_t[:], 1.0),
                    writes=((tag, "ones"),))
            tk.emit("gpsimd", lambda e: e.memset(S_[:], 0.0), writes=((tag, "S"),))
            tk.emit("vector", lambda e: e.tensor_scalar(
                out=twoI[:], in0=self.ident[:], scalar1=2.0, op0=ALU.mult,
                scalar2=None), reads=("ident",), writes=((tag, "twoI"),))
            tk.emit("vector", lambda e: e.tensor_scalar(
                out=rq2[:], in0=self.R_b[:], scalar1=0.03125, op0=ALU.mult,
                scalar2=None), reads=("R_b",), writes=((tag, "rq2"),))
            rc, nac = self.col_stats(lctx, tag)
            tk.emit("sync", lambda e: e.dma_start(out=wbcol[:], in_=wb_h[:]),
                    writes=((tag, "wbcol"),), dma=f"{tag}_wb")

            w_rs = {0: wq_h.rearrange("(c p) d -> p c d", p=128),
                    1: wk_h.rearrange("(c p) d -> p c d", p=128),
                    2: wv_h.rearrange("(c p) d -> p c d", p=128)}
            xr_all = tuple(("x", c, w) for c in range(NCH) for w in range(NW))

            def wload(seq_i, wi, quarter):
                bi = seq_i % 2
                tk.emit("sync", lambda e, wi=wi, quarter=quarter, bi=bi:
                        e.dma_start(out=wbuf[bi][:],
                                    in_=w_rs[wi][:, :, quarter * 256:
                                                 (quarter + 1) * 256]),
                        writes=((tag, "wb", bi),), dma=f"{tag}_w{bi}")
                return bi, (tag, "wb", bi)

            seq_i = 0
            for w in range(NWN):
                t0 = w * WN
                xw = tuple(("x", c, t0 // WIN) for c in range(NCH))
                bkk, br = self.bank()
                self.mm([(lambda c=c: wbcol[:, c:c + 1],
                          lambda c=c, t0=t0: xT[:, c, t0:t0 + WN])
                         for c in range(NCH)],
                        bkk[0:1, 0:WN], br, reads=xw + ((tag, "wbcol"),))
                if w == 0:
                    bk2, br2 = self.bank()
                    self.mm([(lambda c=c: wbcol[:, c:c + 1],
                              lambda: ones_t[:, :]) for c in range(NCH)],
                            bk2[0:1, 0:1], br2, reads=((tag, "wbcol"), (tag, "ones")))
                    tk.emit("vector", lambda e, bk2=bk2: e.tensor_scalar(
                        out=nws[0:1, 2:3], in0=bk2[0:1, 0:1], scalar1=-1.0,
                        op0=ALU.mult, scalar2=None),
                        reads=(br2,), writes=((tag, "nws", 2),))
                tk.emit("vector", lambda e, bkk=bkk, t0=t0: e.tensor_tensor(
                    out=rows[0:1, 0, :], in0=bkk[0:1, 0:WN],
                    in1=self.R_b[0:1, t0:t0 + WN], op=ALU.mult),
                    reads=(br, "R_b"), writes=((tag, "brow"),))
                tk.emit("vector", lambda e, t0=t0: e.scalar_tensor_tensor(
                    out=rows[0:1, 0, :], in0=self.A_b[0:1, t0:t0 + WN],
                    scalar=nws[0:1, 2:3], op0=ALU.mult,
                    in1=rows[0:1, 0, :], op1=ALU.add),
                    reads=((tag, "brow"), (tag, "nws", 2), "A_b"),
                    writes=((tag, "brow"),))
                tk.emit("scalar", lambda e: e.activation(
                    out=rows[0:1, 0, :], in_=rows[0:1, 0, :], func=ACT.Sigmoid),
                    reads=((tag, "brow"),), writes=((tag, "brow"),))

                for wi, dst, dreg in ((0, qtw, (tag, "qt")), (1, ktw, (tag, "kt"))):
                    for quarter in range(4):
                        bi, wreg = wload(seq_i, wi, quarter)
                        seq_i += 1
                        for dh in range(2):
                            dd = quarter * 2 + dh
                            bkk, br = self.bank()
                            self.mm([(lambda c=c, dh=dh, bi=bi:
                                      wbuf[bi][:, c, dh * 128:(dh + 1) * 128],
                                      lambda: ones_t[:, :]) for c in range(NCH)],
                                    bkk[:, 0:1], br, reads=(wreg, (tag, "ones")))
                            sc1 = -0.03125 if wi == 0 else -1.0
                            tk.emit("vector", lambda e, bkk=bkk, wi=wi, sc1=sc1:
                                    e.tensor_scalar(
                                        out=nws[:, wi:wi + 1], in0=bkk[:, 0:1],
                                        scalar1=sc1, op0=ALU.mult, scalar2=None),
                                    reads=(br,), writes=((tag, "nws", wi),))
                            bkk, br = self.bank()
                            self.mm([(lambda c=c, dh=dh, bi=bi:
                                      wbuf[bi][:, c, dh * 128:(dh + 1) * 128],
                                      lambda c=c, t0=t0: xT[:, c, t0:t0 + WN])
                                     for c in range(NCH)],
                                    bkk[:, 0:WN], br, reads=xw + (wreg,))
                            Rb = rq2 if wi == 0 else self.R_b
                            Rrg = (tag, "rq2") if wi == 0 else "R_b"
                            tk.emit("vector", lambda e, bkk=bkk, t0=t0, Rb=Rb:
                                    e.tensor_tensor(
                                        out=tca[:, 0:WN], in0=bkk[:, 0:WN],
                                        in1=Rb[:, t0:t0 + WN], op=ALU.mult),
                                    reads=(br, Rrg), writes=((tag, "tca"),))
                            tk.emit("vector", lambda e, t0=t0, wi=wi, dst=dst,
                                    dd=dd, dreg=dreg: e.scalar_tensor_tensor(
                                        out=dst[:, dd, :],
                                        in0=self.A_b[:, t0:t0 + WN],
                                        scalar=nws[:, wi:wi + 1], op0=ALU.mult,
                                        in1=tca[:, 0:WN], op1=ALU.add),
                                    reads=((tag, "tca"), (tag, "nws", wi), "A_b"),
                                    writes=(dreg, (dreg, dd)))
                ktr = tuple(((tag, "kt"), dd) for dd in range(NCH))
                for dd in range(NCH):
                    tk.emit("scalar", lambda e, dd=dd: e.activation(
                        out=rows[:, 1, :], in_=ktw[:, dd, :], func=ACT.Square),
                        reads=((tag, "kt"),), writes=((tag, "ksq"),))
                    if dd == 0:
                        tk.emit("vector", lambda e: e.tensor_copy(
                            out=rows[:, 2, :], in_=rows[:, 1, :]),
                            reads=((tag, "ksq"),), writes=((tag, "kacc"),))
                    else:
                        tk.emit("vector", lambda e: e.tensor_tensor(
                            out=rows[:, 2, :], in0=rows[:, 2, :],
                            in1=rows[:, 1, :], op=ALU.add),
                            reads=((tag, "ksq"), (tag, "kacc")),
                            writes=((tag, "kacc"),))
                bkk, brq = self.bank()
                self.mm([(lambda: self.ones2[:, 0:1], lambda: rows[:, 2, :])],
                        bkk[0:1, 0:WN], brq, reads=((tag, "kacc"), "ones2"))
                tk.emit("scalar", lambda e, bkk=bkk: e.activation(
                    out=rows[0:1, 1, :], in_=bkk[0:1, 0:WN], func=ACT.Sqrt),
                    reads=(brq,), writes=((tag, "ksq"),))
                tk.emit("vector", lambda e: e.tensor_scalar(
                    out=rows[0:1, 1, :], in0=rows[0:1, 1, :], scalar1=1e-12,
                    op0=ALU.max, scalar2=None),
                    reads=((tag, "ksq"),), writes=((tag, "ksq"),))
                tk.emit("vector", lambda e: e.reciprocal(
                    out=rows[0:1, 1, :], in_=rows[0:1, 1, :]),
                    reads=((tag, "ksq"),), writes=((tag, "ksq"),))
                self.bcast(lambda c0, cw: rows[0:1, 1, c0:c0 + cw], ((tag, "ksq"),),
                           lambda c0, cw: rows[:, 2, c0:c0 + cw], (tag, "kacc"), WN)
                for dd in range(NCH):
                    tk.emit("vector", lambda e, dd=dd: e.tensor_tensor(
                        out=ktw[:, dd, :], in0=ktw[:, dd, :], in1=rows[:, 2, :],
                        op=ALU.mult), reads=((tag, "kacc"), (tag, "kt")),
                        writes=((tag, "kt"),))
                for quarter in range(4):
                    bi, wreg = wload(seq_i, 2, quarter)
                    seq_i += 1
                    if w == 0:
                        bkk, br = self.bank()
                        self.mm([(lambda: ones_t[:, :],
                                  lambda c=c, bi=bi: wbuf[bi][:, c, :])
                                 for c in range(NCH)],
                                bkk[0:1, 0:256], br, reads=(wreg, (tag, "ones")))
                        tk.emit("vector", lambda e, bkk=bkk, quarter=quarter:
                                e.tensor_copy(
                                    out=rows[0:1, 3, 0:256], in_=bkk[0:1, 0:256]),
                                reads=(br,), writes=((tag, "wvrow"),))
                        self.bcast(lambda c0, cw: rows[0:1, 3, c0:c0 + cw],
                                   ((tag, "wvrow"),),
                                   lambda c0, cw, quarter=quarter:
                                   wvb[:, quarter * 256 + c0:quarter * 256 + c0 + cw],
                                   (tag, "wvb"), 256)
                    for cc in range(2):
                        tc = (t0 + cc * 128) // 128
                        bkk, br = self.bank()
                        self.mm([(lambda c=c, tc=tc: xT[:, c,
                                                        tc * 128:(tc + 1) * 128],
                                  lambda c=c, bi=bi: wbuf[bi][:, c, :])
                                 for c in range(NCH)],
                                bkk[:, 0:256], br, reads=xw + (wreg,))
                        tk.emit("vector", lambda e, tc=tc, quarter=quarter:
                                e.tensor_scalar(
                                    out=tca[:, 0:256],
                                    in0=wvb[:, quarter * 256:(quarter + 1) * 256],
                                    scalar1=nac[:, tc:tc + 1], op0=ALU.mult,
                                    scalar2=None),
                                reads=((tag, "wvb"), (tag, "nac")),
                                writes=((tag, "tca"),))
                        tk.emit("vector", lambda e, bkk=bkk, tc=tc:
                                e.scalar_tensor_tensor(
                                    out=tca[:, 256:512], in0=bkk[:, 0:256],
                                    scalar=rc[:, tc:tc + 1], op0=ALU.mult,
                                    in1=tca[:, 0:256], op1=ALU.add),
                                reads=(br, (tag, "rc"), (tag, "tca")),
                                writes=((tag, "tca"),))
                        tk.emit("scalar", lambda e, cc=cc, quarter=quarter:
                                e.activation(
                                    out=vtw[:, cc, quarter * 256:(quarter + 1) * 256],
                                    in_=tca[:, 256:512], func=ACT.Silu),
                                reads=((tag, "tca"),),
                                writes=((tag, "v", cc),))

                for cc in range(2):
                    ci = w * 2 + cc
                    qsl = lambda c, cc=cc: qtw[:, c, cc * 128:(cc + 1) * 128]
                    ksl = lambda c, cc=cc: ktw[:, c, cc * 128:(cc + 1) * 128]
                    tk.emit("tensor", lambda e, cc=cc: e.transpose(
                        self.ps[7][:, 0:1],
                        in_=rows[0:1, 0, cc * 128:(cc + 1) * 128],
                        identity=self.ident[0:1, 0:1]),
                        reads=((tag, "brow"), "ident"), writes=(("ps", 7),))
                    tk.emit("vector", lambda e, cc=cc: e.tensor_copy(
                        out=bcol[:, cc:cc + 1], in_=self.ps[7][:, 0:1]),
                        reads=(("ps", 7),), writes=((tag, "bcol"),))
                    self.bcast(lambda c0, cw, cc=cc:
                               rows[0:1, 0, cc * 128 + c0:cc * 128 + c0 + cw],
                               ((tag, "brow"),),
                               lambda c0, cw: m0[:, 6, c0:c0 + cw],
                               (tag, "brb"), 128)
                    bkk, br = self.bank()
                    self.mm([(lambda c=c, ksl=ksl: ksl(c),
                              lambda c=c, ksl=ksl: ksl(c)) for c in range(NCH)],
                            bkk[:, 0:128], br, reads=((tag, "kt"),))
                    tk.emit("vector", lambda e, bkk=bkk: e.tensor_tensor(
                        out=m0[:, 7, :], in0=bkk[:, 0:128], in1=self.stril[:],
                        op=ALU.mult), reads=(br, "stril"), writes=((tag, "m7"),))
                    tk.emit("vector", lambda e, cc=cc: e.tensor_scalar(
                        out=m0[:, 0, :], in0=m0[:, 7, :],
                        scalar1=bcol[:, cc:cc + 1], op0=ALU.mult, scalar2=None),
                        reads=((tag, "m7"), (tag, "bcol")), writes=((tag, "m0"),))
                    tk.emit("vector", lambda e, bkk=bkk: e.tensor_tensor(
                        out=m0[:, 7, :], in0=bkk[:, 0:128], in1=self.striu[:],
                        op=ALU.mult), reads=(br, "striu"), writes=((tag, "m7"),))
                    tk.emit("vector", lambda e: e.tensor_tensor(
                        out=m0[:, 1, :], in0=m0[:, 7, :], in1=m0[:, 6, :],
                        op=ALU.mult), reads=((tag, "m7"), (tag, "brb")),
                        writes=((tag, "m1"),))
                    tk.emit("vector", lambda e: e.tensor_tensor(
                        out=m0[:, 2, :], in0=self.ident[:], in1=m0[:, 0, :],
                        op=ALU.subtract), reads=((tag, "m0"), "ident"),
                        writes=((tag, "m2"),))
                    tk.emit("vector", lambda e: e.tensor_tensor(
                        out=m0[:, 3, :], in0=self.ident[:], in1=m0[:, 1, :],
                        op=ALU.subtract), reads=((tag, "m1"), "ident"),
                        writes=((tag, "m3"),))
                    tk.emit("vector", lambda e: e.tensor_tensor(
                        out=m0[:, 1, :], in0=m0[:, 1, :], in1=self.ident[:],
                        op=ALU.add), reads=((tag, "m1"), "ident"),
                        writes=((tag, "m1"),))
                    for it in range(6):
                        bky, bry = self.bank()
                        self.mm([(lambda: m0[:, 1, :], lambda: m0[:, 2, :])],
                                bky[:, 0:128], bry,
                                reads=((tag, "m1"), (tag, "m2")))
                        tk.emit("vector", lambda e, bky=bky: e.tensor_tensor(
                            out=m0[:, 4, :], in0=twoI[:], in1=bky[:, 0:128],
                            op=ALU.subtract),
                            reads=(bry, (tag, "twoI")), writes=((tag, "m4"),))
                        bkx, brx = self.bank()
                        self.mm([(lambda: m0[:, 3, :], lambda: m0[:, 4, :])],
                                bkx[:, 0:128], brx,
                                reads=((tag, "m3"), (tag, "m4")))
                        bkxt, brxt = self.bank()
                        self.mm([(lambda: m0[:, 4, :], lambda: m0[:, 3, :])],
                                bkxt[:, 0:128], brxt,
                                reads=((tag, "m3"), (tag, "m4")))
                        tk.emit("vector", lambda e, bkx=bkx: e.tensor_copy(
                            out=m0[:, 2, :], in_=bkx[:, 0:128]),
                            reads=(brx,), writes=((tag, "m2"),))
                        tk.emit("vector", lambda e, bkxt=bkxt: e.tensor_copy(
                            out=m0[:, 3, :], in_=bkxt[:, 0:128]),
                            reads=(brxt,), writes=((tag, "m3"),))
                    for dh in range(2):
                        bkk, br = self.bank()
                        if ci > 0:
                            self.mm([(lambda c=c, ksl=ksl: ksl(c),
                                      lambda c=c, dh=dh: S_[:, c,
                                                            dh * 512:(dh + 1) * 512])
                                     for c in range(NCH)],
                                    bkk[:, 0:512], br,
                                    reads=((tag, "kt"), (tag, "S")))
                            tk.emit("vector", lambda e, bkk=bkk, cc=cc, dh=dh:
                                    e.tensor_tensor(
                                        out=tca[:, dh * 512:(dh + 1) * 512],
                                        in0=vtw[:, cc, dh * 512:(dh + 1) * 512],
                                        in1=bkk[:, 0:512], op=ALU.subtract),
                                    reads=(br, (tag, "v", cc)),
                                    writes=((tag, "tca"),))
                            tk.emit("vector", lambda e, cc=cc, dh=dh:
                                    e.tensor_scalar(
                                        out=tca[:, dh * 512:(dh + 1) * 512],
                                        in0=tca[:, dh * 512:(dh + 1) * 512],
                                        scalar1=bcol[:, cc:cc + 1], op0=ALU.mult,
                                        scalar2=None),
                                    reads=((tag, "tca"), (tag, "bcol")),
                                    writes=((tag, "tca"),))
                        else:
                            tk.emit("vector", lambda e, cc=cc, dh=dh:
                                    e.tensor_scalar(
                                        out=tca[:, dh * 512:(dh + 1) * 512],
                                        in0=vtw[:, cc, dh * 512:(dh + 1) * 512],
                                        scalar1=bcol[:, cc:cc + 1], op0=ALU.mult,
                                        scalar2=None),
                                    reads=((tag, "v", cc), (tag, "bcol")),
                                    writes=((tag, "tca"),))
                    for dh in range(2):
                        bkk, br = self.bank()
                        self.mm([(lambda: m0[:, 3, :],
                                  lambda dh=dh: tca[:, dh * 512:(dh + 1) * 512])],
                                bkk[:, 0:512], br,
                                reads=((tag, "m3"), (tag, "tca")))
                        tk.emit("vector", lambda e, bkk=bkk, dh=dh: e.tensor_copy(
                            out=u_[:, dh * 512:(dh + 1) * 512], in_=bkk[:, 0:512]),
                            reads=(br,), writes=((tag, "u"),))
                    bkk, br = self.bank()
                    self.mm([(lambda c=c, ksl=ksl: ksl(c),
                              lambda c=c, qsl=qsl: qsl(c)) for c in range(NCH)],
                            bkk[:, 0:128], br, reads=((tag, "kt"), (tag, "qt")))
                    tk.emit("vector", lambda e, bkk=bkk: e.tensor_tensor(
                        out=m0[:, 5, :], in0=bkk[:, 0:128], in1=self.triuI[:],
                        op=ALU.mult), reads=(br, "triuI"), writes=((tag, "m5"),))
                    for dv in range(NCH):
                        bkk, br = self.bank()
                        steps = []
                        if ci > 0:
                            steps += [(lambda c=c, dv=dv: S_[:, c,
                                                             dv * 128:(dv + 1) * 128],
                                       lambda c=c, qsl=qsl: qsl(c))
                                      for c in range(NCH)]
                        steps += [(lambda dv=dv: u_[:, dv * 128:(dv + 1) * 128],
                                   lambda: m0[:, 5, :])]
                        self.mm(steps, bkk[:, 0:128], br,
                                reads=((tag, "S"), (tag, "qt"), (tag, "u"),
                                       (tag, "m5")))
                        tk.emit("vector", lambda e, bkk=bkk, dv=dv: e.tensor_copy(
                            out=osb[:, dv, :], in_=bkk[:, 0:128]),
                            reads=(br,), writes=((tag, "osb", dv),))
                    for c in range(NCH):
                        bkk, br = self.bank()
                        tk.emit("tensor", lambda e, c=c, ksl=ksl, bkk=bkk:
                                e.transpose(bkk[:, 0:128], in_=ksl(c),
                                            identity=self.ident[:]),
                                reads=((tag, "kt"), "ident"), writes=(br,))
                        tk.emit("vector", lambda e, bkk=bkk, c=c: e.tensor_copy(
                            out=ktok[:, c, :], in_=bkk[:, 0:128]),
                            reads=(br,), writes=((tag, "ktok"),))
                    for dk in range(NCH):
                        for dh in range(2):
                            bkk, br = self.bank()
                            self.mm([(lambda dk=dk: ktok[:, dk, :],
                                      lambda dh=dh: u_[:, dh * 512:(dh + 1) * 512])],
                                    bkk[:, 0:512], br,
                                    reads=((tag, "ktok"), (tag, "u")))
                            tk.emit("vector", lambda e, bkk=bkk, dk=dk, dh=dh:
                                    e.tensor_tensor(
                                        out=S_[:, dk, dh * 512:(dh + 1) * 512],
                                        in0=S_[:, dk, dh * 512:(dh + 1) * 512],
                                        in1=bkk[:, 0:512], op=ALU.add),
                                    reads=(br, (tag, "S")), writes=((tag, "S"),))
                    for dv in range(NCH):
                        tk.emit("scalar", lambda e, dv=dv: e.activation(
                            out=rows[:, 1, 0:128], in_=osb[:, dv, :],
                            func=ACT.Square),
                            reads=((tag, "osb", dv),), writes=((tag, "ost"),))
                        if dv == 0:
                            tk.emit("vector", lambda e: e.tensor_copy(
                                out=rows[:, 2, 0:128], in_=rows[:, 1, 0:128]),
                                reads=((tag, "ost"),), writes=((tag, "osa"),))
                            tk.emit("vector", lambda e: e.tensor_copy(
                                out=rows[:, 3, 0:128], in_=osb[:, 0, :]),
                                reads=((tag, "osb", 0),), writes=((tag, "osx"),))
                        else:
                            tk.emit("vector", lambda e: e.tensor_tensor(
                                out=rows[:, 2, 0:128], in0=rows[:, 2, 0:128],
                                in1=rows[:, 1, 0:128], op=ALU.add),
                                reads=((tag, "ost"), (tag, "osa")),
                                writes=((tag, "osa"),))
                            tk.emit("vector", lambda e, dv=dv: e.tensor_tensor(
                                out=rows[:, 3, 0:128], in0=rows[:, 3, 0:128],
                                in1=osb[:, dv, :], op=ALU.add),
                                reads=((tag, "osb", dv), (tag, "osx")),
                                writes=((tag, "osx"),))
                    bsq, brsq = self.bank()
                    self.mm([(lambda: self.ones2[:, 0:1],
                              lambda: rows[:, 2, 0:128])],
                            bsq[0:1, 0:128], brsq, reads=((tag, "osa"), "ones2"))
                    bsx, brsx = self.bank()
                    self.mm([(lambda: self.ones2[:, 0:1],
                              lambda: rows[:, 3, 0:128])],
                            bsx[0:1, 0:128], brsx, reads=((tag, "osx"), "ones2"))
                    tk.emit("vector", lambda e, bsx=bsx: e.tensor_scalar(
                        out=rows[0:1, 3, 0:128], in0=bsx[0:1, 0:128],
                        scalar1=1.0 / D, op0=ALU.mult, scalar2=None),
                        reads=(brsx,), writes=((tag, "osx"),))
                    tk.emit("vector", lambda e, bsq=bsq: e.tensor_scalar(
                        out=rows[0:1, 1, 0:128], in0=bsq[0:1, 0:128],
                        scalar1=1.0 / D, op0=ALU.mult, scalar2=1e-5, op1=ALU.add),
                        reads=(brsq,), writes=((tag, "ost"),))
                    tk.emit("vector", lambda e: e.tensor_tensor(
                        out=rows[0:1, 2, 0:128], in0=rows[0:1, 3, 0:128],
                        in1=rows[0:1, 3, 0:128], op=ALU.mult),
                        reads=((tag, "osx"),), writes=((tag, "osa"),))
                    tk.emit("vector", lambda e: e.tensor_tensor(
                        out=rows[0:1, 1, 0:128], in0=rows[0:1, 1, 0:128],
                        in1=rows[0:1, 2, 0:128], op=ALU.subtract),
                        reads=((tag, "ost"), (tag, "osa")), writes=((tag, "ost"),))
                    tk.emit("scalar", lambda e: e.activation(
                        out=rows[0:1, 1, 0:128], in_=rows[0:1, 1, 0:128],
                        func=ACT.Sqrt), reads=((tag, "ost"),),
                        writes=((tag, "ost"),))
                    tk.emit("vector", lambda e: e.reciprocal(
                        out=rows[0:1, 2, 0:128], in_=rows[0:1, 1, 0:128]),
                        reads=((tag, "ost"),), writes=((tag, "osa"),))
                    tk.emit("vector", lambda e: e.tensor_tensor(
                        out=rows[0:1, 3, 0:128], in0=rows[0:1, 3, 0:128],
                        in1=rows[0:1, 2, 0:128], op=ALU.mult),
                        reads=((tag, "osx"), (tag, "osa")), writes=((tag, "osx"),))
                    self.bcast(lambda c0, cw: rows[0:1, 2, c0:c0 + cw],
                               ((tag, "osa"),),
                               lambda c0, cw: rows[:, 4, c0:c0 + cw],
                               (tag, "oR"), 128)
                    self.bcast(lambda c0, cw: rows[0:1, 3, c0:c0 + cw],
                               ((tag, "osx"),),
                               lambda c0, cw: rows[:, 5, c0:c0 + cw],
                               (tag, "oA"), 128)
                    for dv in range(NCH):
                        tk.emit("vector", lambda e, dv=dv: e.tensor_tensor(
                            out=tca[:, 0:128], in0=osb[:, dv, :],
                            in1=rows[:, 4, 0:128], op=ALU.mult),
                            reads=((tag, "osb", dv), (tag, "oR")),
                            writes=((tag, "tca"),))
                        tk.emit("vector", lambda e: e.tensor_tensor(
                            out=tca[:, 0:128], in0=tca[:, 0:128],
                            in1=rows[:, 5, 0:128], op=ALU.subtract),
                            reads=((tag, "tca"), (tag, "oA")),
                            writes=((tag, "tca"),))
                        tk.emit("vector", lambda e, dv=dv: e.tensor_scalar(
                            out=tca[:, 0:128], in0=tca[:, 0:128],
                            scalar1=self.vecs[:, g_s, dv:dv + 1], op0=ALU.mult,
                            scalar2=self.vecs[:, b_s, dv:dv + 1], op1=ALU.add),
                            reads=((tag, "tca"), ("vecs", "l0abv")),
                            writes=((tag, "tca"),))
                        tk.emit("vector", lambda e, dv=dv, ci=ci: e.tensor_tensor(
                            out=xT[:, dv, ci * 128:(ci + 1) * 128],
                            in0=xT[:, dv, ci * 128:(ci + 1) * 128],
                            in1=tca[:, 0:128], op=ALU.add),
                            reads=((tag, "tca"), ("x", dv, ci // 4)),
                            writes=(("x", dv, ci // 4),))

    def emit_l0aa(self, tag):
        nc, tk, xT = self.nc, self.tk, self.xT
        wq_h = self.win("l0aa_wqT", (D, D))
        wk_h = self.win("l0aa_wkT", (D, D))
        wv_h = self.win("l0aa_wvT", (D, D))
        wo_h = self.win("l0aa_woT", (D, D))
        if "l0aa_bq" not in self._vec_slot:
            self.load_vecs(["l0aa_bq", "l0aa_bk", "l0aa_bv", "l0aa_bo"],
                           self.win("l0aa_bias", (128, 4, NCH)), "l0aav")
        bq, bk_, bv, bo = (self.vec("l0aa_bq"), self.vec("l0aa_bk"),
                           self.vec("l0aa_bv"), self.vec("l0aa_bo"))

        with ExitStack() as lctx:
            with ExitStack() as sctx:
                self.ln_stats(sctx, tag)
                tk.barrier()
            yac = self.sb(f"{tag}_yac", [128, NCH, T], ctx=lctx)
            qg = self.sb(f"{tag}_qg", [128, 16, 128], ctx=lctx)
            ktg = self.sb(f"{tag}_ktg", [128, T], ctx=lctx)
            vgg = self.sb(f"{tag}_vgg", [128, 16, 128], ctx=lctx)
            rb_ = self.sb(f"{tag}_rb", [128, T], ctx=lctx)
            rb2 = self.sb(f"{tag}_rb2", [128, T], ctx=lctx)
            tmp = self.sb(f"{tag}_tmp", [128, 2, WIN], ctx=lctx)
            gtok = self.sb(f"{tag}_gtok", [128, 16, 20], ctx=lctx)
            atok = self.sb(f"{tag}_atok", [128, 16, 10], ctx=lctx)
            wsl = self.sb(f"{tag}_wsl", [128, D], ctx=lctx)
            qsg = self.sb(f"{tag}_qsg", [128, 20], ctx=lctx)
            outs2 = self.sb(f"{tag}_outs", [128, 128], ctx=lctx)
            qstat = self.sb(f"{tag}_qstat", [128, 16, 16], ctx=lctx)
            bct = self.sb(f"{tag}_bct", [128, 2, 128], ctx=lctx)
            cols = self.sb(f"{tag}_cols", [128, 8], ctx=lctx)
            vcolT = self.sb(f"{tag}_vcolT", [128, 2], ctx=lctx)
            v16 = self.sb(f"{tag}_v16", [128, 16], ctx=lctx)
            bnst = self.sb(f"{tag}_bnst", [128, 56], ctx=lctx)
            ones_t = self.sb(f"{tag}_ones", [128, 1], ctx=lctx)
            tk.emit("gpsimd", lambda e: e.memset(ones_t[:], 1.0),
                    writes=((tag, "ones"),))
            tk.emit("vector", lambda e: e.memset(bnst[:, 28:39], 0.0),
                    writes=((tag, "validrow"),))
            rc, nac = self.col_stats(lctx, tag)

            w_rs = {0: wq_h.rearrange("(c p) d -> p c d", p=128),
                    1: wk_h.rearrange("(c p) d -> p c d", p=128),
                    2: wv_h.rearrange("(c p) d -> p c d", p=128),
                    3: wo_h.rearrange("(g p) d -> p g d", p=128)}
            xr_all = tuple(("x", c, w) for c in range(NCH) for w in range(NW))
            wsl3 = lambda: wsl[:].rearrange("p (c d) -> p c d", d=128)

            def wload(g, i):
                src = (w_rs[i][:, :, g * 128:(g + 1) * 128] if i < 3
                       else w_rs[3][:, g, :])
                tk.emit("sync", lambda e, src=src, i=i: e.dma_start(
                    out=(wsl[:].rearrange("p (c d) -> p c d", d=128)
                         if i < 3 else wsl[:]), in_=src),
                    writes=((tag, "wsl"),), dma=f"{tag}_w")
                return (tag, "wsl")

            def row_bcast(src_fn, src_reads, plane):
                tk.emit("vector", lambda e: e.tensor_copy(
                    out=rb2[0:1, 1024:1152], in_=src_fn()),
                    reads=src_reads + ((tag, "gt1"),), writes=((tag, "gt1"),))
                self.bcast(lambda c0, cw: rb2[0:1, 1024 + c0:1024 + c0 + cw],
                           ((tag, "gt1"),),
                           lambda c0, cw, plane=plane: bct[:, plane, c0:c0 + cw],
                           (tag, "bct", plane), 128)

            st = lambda j: qstat[:, :, 2 * j:2 * j + 2]

            for g in range(8):
                wreg = wload(g, 0)
                bkk, br = self.bank()
                self.mm([(lambda: ones_t[:, :], lambda c=c: wsl3()[:, c, :])
                         for c in range(NCH)], bkk[0:1, 0:128], br,
                        reads=(wreg, (tag, "ones")))
                row_bcast(lambda bkk=bkk: bkk[0:1, 0:128], (br,), 0)
                tk.emit("tensor", lambda e, g=g: e.transpose(
                    self.ps[7][0:1, 0:128], in_=self.vecs[:, bq, g:g + 1],
                    identity=self.ident[:]),
                    reads=(("vecs", "l0aav"), "ident"), writes=(("ps", 7),))
                row_bcast(lambda: self.ps[7][0:1, 0:128], (("ps", 7),), 1)
                for tc in range(16):
                    bkk, br = self.bank()
                    self.mm([(lambda c=c, tc=tc: xT[:, c, tc * 128:(tc + 1) * 128],
                              lambda c=c: wsl3()[:, c, :]) for c in range(NCH)],
                            bkk[:, 0:128], br, reads=xr_all + (wreg,))
                    tk.emit("vector", lambda e, tc=tc: e.scalar_tensor_tensor(
                        out=tmp[:, 0, 0:128], in0=bct[:, 0, :],
                        scalar=nac[:, tc:tc + 1], op0=ALU.mult,
                        in1=bct[:, 1, :], op1=ALU.add),
                        reads=((tag, "bct", 0), (tag, "bct", 1), (tag, "nac")),
                        writes=((tag, "tmp0"),))
                    tk.emit("vector", lambda e, bkk=bkk, tc=tc: e.scalar_tensor_tensor(
                        out=qg[:, tc, :], in0=bkk[:, 0:128],
                        scalar=rc[:, tc:tc + 1], op0=ALU.mult,
                        in1=tmp[:, 0, 0:128], op1=ALU.add),
                        reads=(br, (tag, "rc"), (tag, "tmp0")),
                        writes=((tag, "qg"),))
                wreg = wload(g, 1)
                bkk, br = self.bank()
                self.mm([(lambda c=c: wsl3()[:, c, :], lambda: ones_t[:, :])
                         for c in range(NCH)], bkk[:, 0:1], br,
                        reads=(wreg, (tag, "ones")))
                tk.emit("vector", lambda e, bkk=bkk: e.tensor_scalar(
                    out=cols[:, 0:1], in0=bkk[:, 0:1], scalar1=-1.0, op0=ALU.mult,
                    scalar2=None), reads=(br,), writes=((tag, "nwsk"),))
                for w in range(NW):
                    bkk, br = self.bank()
                    self.mm([(lambda c=c: wsl3()[:, c, :],
                              lambda c=c, w=w: xT[:, c, w * WIN:(w + 1) * WIN])
                             for c in range(NCH)],
                            bkk[:, 0:WIN], br, reads=xr_all + (wreg,))
                    tk.emit("vector", lambda e, bkk=bkk, w=w: e.tensor_tensor(
                        out=tmp[:, 0, :], in0=bkk[:, 0:WIN],
                        in1=self.R_b[:, w * WIN:(w + 1) * WIN], op=ALU.mult),
                        reads=(br, "R_b"), writes=((tag, "tmp0"),))
                    tk.emit("vector", lambda e, w=w: e.scalar_tensor_tensor(
                        out=tmp[:, 0, :], in0=self.A_b[:, w * WIN:(w + 1) * WIN],
                        scalar=cols[:, 0:1], op0=ALU.mult,
                        in1=tmp[:, 0, :], op1=ALU.add),
                        reads=((tag, "tmp0"), (tag, "nwsk"), "A_b"),
                        writes=((tag, "tmp0"),))
                    tk.emit("vector", lambda e, w=w, g=g: e.tensor_scalar(
                        out=ktg[:, w * WIN:(w + 1) * WIN], in0=tmp[:, 0, :],
                        scalar1=self.vecs[:, bk_, g:g + 1], op0=ALU.add,
                        scalar2=None),
                        reads=((tag, "tmp0"), ("vecs", "l0aav")),
                        writes=((tag, "ktg"),))
                wreg = wload(g, 2)
                bkk, br = self.bank()
                self.mm([(lambda: ones_t[:, :], lambda c=c: wsl3()[:, c, :])
                         for c in range(NCH)], bkk[0:1, 0:128], br,
                        reads=(wreg, (tag, "ones")))
                row_bcast(lambda bkk=bkk: bkk[0:1, 0:128], (br,), 0)
                tk.emit("tensor", lambda e, g=g: e.transpose(
                    self.ps[7][0:1, 0:128], in_=self.vecs[:, bv, g:g + 1],
                    identity=self.ident[:]),
                    reads=(("vecs", "l0aav"), "ident"), writes=(("ps", 7),))
                row_bcast(lambda: self.ps[7][0:1, 0:128], (("ps", 7),), 1)
                for tc in range(16):
                    bkk, br = self.bank()
                    self.mm([(lambda c=c, tc=tc: xT[:, c, tc * 128:(tc + 1) * 128],
                              lambda c=c: wsl3()[:, c, :]) for c in range(NCH)],
                            bkk[:, 0:128], br, reads=xr_all + (wreg,))
                    tk.emit("vector", lambda e, tc=tc: e.scalar_tensor_tensor(
                        out=tmp[:, 0, 0:128], in0=bct[:, 0, :],
                        scalar=nac[:, tc:tc + 1], op0=ALU.mult,
                        in1=bct[:, 1, :], op1=ALU.add),
                        reads=((tag, "bct", 0), (tag, "bct", 1), (tag, "nac")),
                        writes=((tag, "tmp0"),))
                    tk.emit("vector", lambda e, bkk=bkk, tc=tc: e.scalar_tensor_tensor(
                        out=vgg[:, tc, :], in0=bkk[:, 0:128],
                        scalar=rc[:, tc:tc + 1], op0=ALU.mult,
                        in1=tmp[:, 0, 0:128], op1=ALU.add),
                        reads=(br, (tag, "rc"), (tag, "tmp0")),
                        writes=((tag, "vgg"),))
                woreg = wload(g, 3)

                for qf in range(4):
                    qv4 = lambda qf=qf: qg[:, qf * 4:(qf + 1) * 4, :].rearrange(
                        "p a (h d) -> p a h d", d=64)
                    st4 = lambda j, qf=qf: qstat[:, qf * 4:(qf + 1) * 4,
                                                 2 * j:2 * j + 2]
                    t3 = lambda pl: tmp[:, pl, :].rearrange(
                        "p (a h d) -> p a h d", a=4, d=64)
                    tk.emit("vector", lambda e, qv4=qv4, st4=st4: e.tensor_reduce(
                        out=st4(0), in_=qv4(), op=ALU.max, axis=AXL.X),
                        reads=((tag, "qg"),), writes=((tag, "qs0"),))
                    tk.emit("vector", lambda e, qv4=qv4, st4=st4, t3=t3:
                            e.tensor_tensor(
                                out=t3(0), in0=qv4(),
                                in1=st4(0).to_broadcast([128, 4, 2, 64]),
                                op=ALU.subtract),
                            reads=((tag, "qg"), (tag, "qs0")),
                            writes=((tag, "tmp0"),))
                    tk.emit("scalar", lambda e: e.activation(
                        out=tmp[:, 1, :], in_=tmp[:, 0, :], func=ACT.Exp),
                        reads=((tag, "tmp0"),), writes=((tag, "tmp1"),))
                    tk.emit("vector", lambda e, st4=st4, t3=t3: e.tensor_reduce(
                        out=st4(4), in_=t3(1), op=ALU.add, axis=AXL.X),
                        reads=((tag, "tmp1"),), writes=((tag, "qs4"),))
                    tk.emit("scalar", lambda e, st4=st4: e.activation(
                        out=st4(4), in_=st4(4), func=ACT.Ln),
                        reads=((tag, "qs4"),), writes=((tag, "qs4"),))
                    tk.emit("vector", lambda e, st4=st4, t3=t3: e.tensor_tensor(
                        out=t3(0), in0=t3(0),
                        in1=st4(4).to_broadcast([128, 4, 2, 64]), op=ALU.subtract),
                        reads=((tag, "tmp0"), (tag, "qs4")), writes=((tag, "tmp0"),))
                    tk.emit("scalar", lambda e: e.activation(
                        out=tmp[:, 0, :], in_=tmp[:, 0, :], func=ACT.Exp),
                        reads=((tag, "tmp0"),), writes=((tag, "tmp0"),))
                    tk.emit("vector", lambda e: e.tensor_scalar(
                        out=tmp[:, 1, :], in0=tmp[:, 0, :], scalar1=1e-9,
                        op0=ALU.add, scalar2=None),
                        reads=((tag, "tmp0"),), writes=((tag, "tmp1"),))
                    tk.emit("scalar", lambda e: e.activation(
                        out=tmp[:, 1, :], in_=tmp[:, 1, :], func=ACT.Ln),
                        reads=((tag, "tmp1"),), writes=((tag, "tmp1"),))
                    tk.emit("vector", lambda e: e.tensor_tensor(
                        out=tmp[:, 1, :], in0=tmp[:, 1, :], in1=tmp[:, 0, :],
                        op=ALU.mult),
                        reads=((tag, "tmp0"), (tag, "tmp1")),
                        writes=((tag, "tmp1"),))
                    tk.emit("vector", lambda e, st4=st4, t3=t3: e.tensor_reduce(
                        out=st4(3), in_=t3(1), op=ALU.add, axis=AXL.X,
                        negate=True),
                        reads=((tag, "tmp1"),), writes=((tag, "qs3"),))
                    tk.emit("scalar", lambda e, qf=qf: e.activation(
                        out=tmp[:, 0, :], in_=qg[:, qf * 4:(qf + 1) * 4, :],
                        func=ACT.Square),
                        reads=((tag, "qg"),), writes=((tag, "tmp0"),))
                    tk.emit("vector", lambda e, st4=st4, t3=t3: e.tensor_reduce(
                        out=st4(1), in_=t3(0), op=ALU.add, axis=AXL.X),
                        reads=((tag, "tmp0"),), writes=((tag, "qs1"),))
                    tk.emit("vector", lambda e, qv4=qv4, st4=st4: e.tensor_reduce(
                        out=st4(2), in_=qv4(), op=ALU.add, axis=AXL.X),
                        reads=((tag, "qg"),), writes=((tag, "qs2"),))
                tk.emit("scalar", lambda e: e.activation(
                    out=st(5), in_=st(1), func=ACT.Sqrt),
                    reads=((tag, "qs1"),), writes=((tag, "qs5"),))
                tk.emit("vector", lambda e: e.tensor_tensor(
                    out=st(6), in0=st(2), in1=st(2), op=ALU.mult),
                    reads=((tag, "qs2"),), writes=((tag, "qs6"),))
                tk.emit("vector", lambda e: e.tensor_scalar(
                    out=st(6), in0=st(6), scalar1=1.0 / 64.0, op0=ALU.mult,
                    scalar2=None), reads=((tag, "qs6"),), writes=((tag, "qs6"),))
                tk.emit("vector", lambda e: e.tensor_tensor(
                    out=st(6), in0=st(1), in1=st(6), op=ALU.subtract),
                    reads=((tag, "qs1"), (tag, "qs6")), writes=((tag, "qs6"),))
                tk.emit("vector", lambda e: e.tensor_scalar(
                    out=st(7), in0=st(5), scalar1=0.5, op0=ALU.mult, scalar2=None),
                    reads=((tag, "qs5"),), writes=((tag, "qs7"),))
                tk.emit("vector", lambda e: e.scalar_tensor_tensor(
                    out=st(7), in0=st(3), scalar=0.3, op0=ALU.mult,
                    in1=st(7), op1=ALU.add),
                    reads=((tag, "qs3"), (tag, "qs7")), writes=((tag, "qs7"),))
                tk.emit("vector", lambda e: e.scalar_tensor_tensor(
                    out=st(7), in0=st(6), scalar=0.2 / 63.0, op0=ALU.mult,
                    in1=st(7), op1=ALU.add),
                    reads=((tag, "qs6"), (tag, "qs7")), writes=((tag, "qs7"),))
                for tc in range(16):
                    tk.emit("tensor", lambda e, tc=tc: e.transpose(
                        self.ps[7][0:2, 0:128], in_=st(7)[:, tc, :],
                        identity=self.ident[:]),
                        reads=((tag, "qs7"), "ident"), writes=(("ps", 7),))
                    tk.emit("vector", lambda e, tc=tc: e.tensor_copy(
                        out=rb_[0:2, tc * 128:(tc + 1) * 128],
                        in_=self.ps[7][0:2, 0:128]),
                        reads=(("ps", 7),), writes=((tag, "r0blk"),))
                if g == 0:
                    for i4 in range(4):
                        tk.emit("vector", lambda e, i4=i4: e.bn_stats(
                            out=bnst[0:1, i4 * 6:i4 * 6 + 6],
                            in_=rb_[0:1, i4 * 512:(i4 + 1) * 512]),
                            reads=((tag, "r0blk"),), writes=((tag, "bn"),))
                    tk.emit("vector", lambda e: e.bn_aggr(
                        out=bnst[0:1, 24:26], in_=bnst[0:1, 0:24]),
                        reads=((tag, "bn"),), writes=((tag, "bn"),))
                    tk.emit("vector", lambda e: e.tensor_scalar(
                        out=bnst[0:1, 26:27], in0=bnst[0:1, 25:26],
                        scalar1=2048.0 / 2047.0, op0=ALU.mult, scalar2=None),
                        reads=((tag, "bn"),), writes=((tag, "rr"),))
                    tk.emit("scalar", lambda e: e.activation(
                        out=bnst[0:1, 26:27], in_=bnst[0:1, 26:27], func=ACT.Sqrt),
                        reads=((tag, "rr"),), writes=((tag, "rr"),))
                    tk.emit("vector", lambda e: e.tensor_scalar(
                        out=bnst[0:1, 27:28], in0=bnst[0:1, 24:25],
                        scalar1=1e-6, op0=ALU.add, scalar2=None),
                        reads=((tag, "bn"),), writes=((tag, "den"),))
                    tk.emit("vector", lambda e: e.reciprocal(
                        out=bnst[0:1, 27:28], in_=bnst[0:1, 27:28]),
                        reads=((tag, "den"),), writes=((tag, "den"),))
                    tk.emit("vector", lambda e: e.tensor_tensor(
                        out=bnst[0:1, 26:27], in0=bnst[0:1, 26:27],
                        in1=bnst[0:1, 27:28], op=ALU.mult),
                        reads=((tag, "rr"), (tag, "den")), writes=((tag, "rr"),))
                    tk.emit("vector", lambda e: e.tensor_scalar(
                        out=bnst[0:1, 26:27], in0=bnst[0:1, 26:27],
                        scalar1=10.0, op0=ALU.mult, scalar2=None),
                        reads=((tag, "rr"),), writes=((tag, "rr"),))
                    tk.emit("vector", lambda e: e.tensor_scalar(
                        out=bnst[0:1, 28:38], in0=self.jrow[0:1, 0:10],
                        scalar1=0.5, op0=ALU.add, scalar2=None),
                        reads=("jrow",), writes=((tag, "validrow"),))
                    tk.emit("vector", lambda e: e.tensor_scalar(
                        out=bnst[0:1, 28:38], in0=bnst[0:1, 28:38],
                        scalar1=bnst[0:1, 26:27], op0=ALU.is_le, scalar2=None),
                        reads=((tag, "validrow"), (tag, "rr")),
                        writes=((tag, "validrow"),))
                    tk.emit("vector", lambda e: e.tensor_scalar(
                        out=bnst[0:1, 38:39], in0=self.jrow[0:1, 0:10],
                        scalar1=2.5, op0=ALU.is_le, scalar2=None)
                        if False else e.tensor_scalar(
                        out=bnst[0:1, 40:50], in0=self.jrow[0:1, 0:10],
                        scalar1=2.5, op0=ALU.is_le, scalar2=None),
                        reads=("jrow",), writes=((tag, "j012"),))
                    tk.emit("vector", lambda e: e.tensor_tensor(
                        out=bnst[0:1, 28:38], in0=bnst[0:1, 28:38],
                        in1=bnst[0:1, 40:50], op=ALU.max),
                        reads=((tag, "validrow"), (tag, "j012")),
                        writes=((tag, "validrow"),))
                    tk.emit("tensor", lambda e: e.transpose(
                        self.ps[7][0:10, 0:1], in_=bnst[0:1, 28:38],
                        identity=self.ident[0:1, 0:1]),
                        reads=((tag, "validrow"), "ident"), writes=(("ps", 7),))
                    tk.emit("vector", lambda e: e.tensor_copy(
                        out=cols[0:10, 3:4], in_=self.ps[7][0:10, 0:1]),
                        reads=(("ps", 7),), writes=((tag, "validcol"),))
                tk.emit("vector", lambda e: e.max(out=v16[0:2, 0:8],
                                                  in_=rb_[0:2, :]),
                        reads=((tag, "r0blk"),), writes=((tag, "v16"),))
                tk.emit("vector", lambda e: e.match_replace(
                    out=rb2[0:2, :], in_to_replace=v16[0:2, 0:8],
                    in_values=rb_[0:2, :], imm_value=-3.0e38),
                    reads=((tag, "r0blk"), (tag, "v16")), writes=((tag, "gt1"),))
                tk.emit("vector", lambda e: e.max(out=v16[0:2, 8:16],
                                                  in_=rb2[0:2, :]),
                        reads=((tag, "gt1"),), writes=((tag, "v16"),))
                tk.emit("tensor", lambda e: e.transpose(
                    self.ps[7][0:16, 0:2], in_=v16[0:2, 0:16],
                    identity=self.ident[0:2, 0:2]),
                    reads=((tag, "v16"), "ident"), writes=(("ps", 7),))
                tk.emit("vector", lambda e: e.tensor_copy(
                    out=vcolT[0:16, :], in_=self.ps[7][0:16, 0:2]),
                    reads=(("ps", 7),), writes=((tag, "vcolT"),))
                for w in range(NW):
                    banks = []
                    for h2 in range(2):
                        bkk, br = self.bank()
                        self.mm([(lambda h2=h2: self.patg[0:2, h2 * 10:h2 * 10 + 10],
                                  lambda w=w: rb_[0:2, w * WIN:(w + 1) * WIN])],
                                bkk[0:10, 0:WIN], br,
                                reads=("patg", (tag, "r0blk")))
                        banks.append((bkk, br))
                    for h2, dstb, dreg in ((0, rb_, (tag, "r0blk")),
                                           (1, rb2, (tag, "gt1"))):
                        bkk, br = banks[h2]
                        tk.emit("vector", lambda e, bkk=bkk, w=w, h2=h2,
                                dstb=dstb, dreg=dreg: e.tensor_scalar(
                                    out=dstb[0:10, w * WIN:(w + 1) * WIN],
                                    in0=bkk[0:10, 0:WIN],
                                    scalar1=vcolT[0:10, h2:h2 + 1],
                                    op0=ALU.is_equal,
                                    scalar2=cols[0:10, 3:4], op1=ALU.mult),
                                reads=(br, (tag, "vcolT"), (tag, "validcol")),
                                writes=(dreg,))
                for h2, srcb, sreg in ((0, rb_, (tag, "r0blk")),
                                       (1, rb2, (tag, "gt1"))):
                    for tc in range(16):
                        bkk, br = self.bank()
                        tk.emit("tensor", lambda e, tc=tc, srcb=srcb, bkk=bkk:
                                e.transpose(
                                    bkk[:, 0:10],
                                    in_=srcb[0:10, tc * 128:(tc + 1) * 128],
                                    identity=self.ident[0:10, 0:10]),
                                reads=(sreg, "ident"), writes=(br,))
                        tk.emit("vector", lambda e, bkk=bkk, tc=tc, h2=h2:
                                e.tensor_copy(
                                    out=gtok[:, tc, h2 * 10:(h2 + 1) * 10],
                                    in_=bkk[:, 0:10]),
                                reads=(br,), writes=((tag, "gtok"),))
                bkk, br = self.bank()
                self.mm([(lambda tc=tc: qg[:, tc, :],
                          lambda tc=tc: gtok[:, tc, :]) for tc in range(16)],
                        bkk[:, 0:20], br, reads=((tag, "qg"), (tag, "gtok")))
                tk.emit("vector", lambda e, bkk=bkk: e.tensor_copy(
                    out=qsg[:], in_=bkk[:, 0:20]),
                    reads=(br,), writes=((tag, "qsg"),))
                for h2 in range(2):
                    hp = h2 * 64
                    for w in range(NW):
                        bkk, br = self.bank()
                        self.mm([(lambda h2=h2, hp=hp: qsg[hp:hp + 64,
                                                           h2 * 10:h2 * 10 + 10],
                                  lambda w=w, hp=hp: ktg[hp:hp + 64,
                                                         w * WIN:(w + 1) * WIN])],
                                bkk[32:42, 0:WIN], br,
                                reads=((tag, "qsg"), (tag, "ktg")),
                                tile_position=(hp, 32))
                        tk.emit("vector", lambda e, bkk=bkk, w=w: e.tensor_copy(
                            out=rb_[32:42, w * WIN:(w + 1) * WIN],
                            in_=bkk[32:42, 0:WIN]),
                            reads=(br,), writes=((tag, "sc32"),))
                    tk.emit("vector", lambda e: e.tensor_scalar(
                        out=cols[32:42, 5:6], in0=rb_[32:42, :],
                        scalar1=0.0, op0=ALU.bypass, scalar2=None)
                        if False else e.tensor_reduce(
                        out=cols[32:42, 5:6], in_=rb_[32:42, :], op=ALU.max,
                        axis=AXL.X, negate=True),
                        reads=((tag, "sc32"),), writes=((tag, "negmax"),))
                    tk.emit("vector", lambda e: e.tensor_scalar(
                        out=cols[32:42, 5:6], in0=cols[32:42, 5:6],
                        scalar1=0.125, op0=ALU.mult, scalar2=None),
                        reads=((tag, "negmax"),), writes=((tag, "negmax"),))
                    tk.emit("scalar", lambda e: e.activation(
                        out=rb_[32:42, :], in_=rb_[32:42, :], func=ACT.Exp,
                        bias=cols[32:42, 5:6], scale=0.125,
                        accum_out=cols[32:42, 4:5]),
                        reads=((tag, "sc32"), (tag, "negmax")),
                        writes=((tag, "sc32"), (tag, "sums")))
                    tk.emit("vector", lambda e: e.reciprocal(
                        out=cols[32:42, 4:5], in_=cols[32:42, 4:5]),
                        reads=((tag, "sums"),), writes=((tag, "sums"),))
                    tk.emit("vector", lambda e: e.tensor_scalar(
                        out=rb_[32:42, :], in0=rb_[32:42, :],
                        scalar1=cols[32:42, 4:5], op0=ALU.mult, scalar2=None),
                        reads=((tag, "sc32"), (tag, "sums")),
                        writes=((tag, "sc32"),))
                    for tc in range(16):
                        bkk, br = self.bank()
                        tk.emit("tensor", lambda e, tc=tc, bkk=bkk: e.transpose(
                            bkk[:, 0:10], in_=rb_[32:42, tc * 128:(tc + 1) * 128],
                            identity=self.ident[32:42, 32:42]),
                            reads=((tag, "sc32"), "ident"), writes=(br,))
                        tk.emit("vector", lambda e, bkk=bkk, tc=tc: e.tensor_copy(
                            out=atok[:, tc, :], in_=bkk[:, 0:10]),
                            reads=(br,), writes=((tag, "atok"),))
                    bkk, br = self.bank()
                    self.mm([(lambda tc=tc: atok[:, tc, :],
                              lambda tc=tc, hp=hp: vgg[:, tc, hp:hp + 64])
                             for tc in range(16)],
                            bkk[0:10, 0:64], br,
                            reads=((tag, "atok"), (tag, "vgg")))
                    tk.emit("vector", lambda e, bkk=bkk, hp=hp: e.tensor_copy(
                        out=outs2[0:10, hp:hp + 64], in_=bkk[0:10, 0:64]),
                        reads=(br,), writes=((tag, "outs"),))
                for w in range(NW):
                    bkk, br = self.bank()
                    self.mm([(lambda: outs2[0:10, 0:64],
                              lambda w=w: rb_[0:10, w * WIN:(w + 1) * WIN])],
                            bkk[0:64, 0:WIN], br,
                            reads=((tag, "outs"), (tag, "r0blk")))
                    self.mm([(lambda: outs2[0:10, 64:128],
                              lambda w=w: rb2[0:10, w * WIN:(w + 1) * WIN])],
                            bkk[64:128, 0:WIN], br,
                            reads=((tag, "outs"), (tag, "gt1")),
                            tile_position=(0, 64))
                    tk.emit("vector", lambda e, bkk=bkk: e.tensor_copy(
                        out=tmp[:, 0, :], in_=bkk[:, 0:WIN]),
                        reads=(br,), writes=((tag, "tmp0"),))
                    for dd in range(NCH):
                        bkk2, br2 = self.bank()
                        self.mm([(lambda dd=dd: wsl[:, dd * 128:(dd + 1) * 128],
                                  lambda: tmp[:, 0, :])],
                                bkk2[:, 0:WIN], br2,
                                reads=((tag, "tmp0"), woreg))
                        if g == 0:
                            tk.emit("vector", lambda e, bkk2=bkk2, dd=dd, w=w:
                                    e.tensor_scalar(
                                        out=yac[:, dd, w * WIN:(w + 1) * WIN],
                                        in0=bkk2[:, 0:WIN],
                                        scalar1=self.vecs[:, bo, dd:dd + 1],
                                        op0=ALU.add, scalar2=None),
                                    reads=(br2, ("vecs", "l0aav")),
                                    writes=((tag, "y", dd, w),))
                        else:
                            tk.emit("vector", lambda e, bkk2=bkk2, dd=dd, w=w:
                                    e.tensor_tensor(
                                        out=yac[:, dd, w * WIN:(w + 1) * WIN],
                                        in0=yac[:, dd, w * WIN:(w + 1) * WIN],
                                        in1=bkk2[:, 0:WIN], op=ALU.add),
                                    reads=(br2, (tag, "y", dd, w)),
                                    writes=((tag, "y", dd, w),))
            for dd in range(NCH):
                for w in range(NW):
                    tk.emit("vector", lambda e, dd=dd, w=w: e.tensor_tensor(
                        out=xT[:, dd, w * WIN:(w + 1) * WIN],
                        in0=xT[:, dd, w * WIN:(w + 1) * WIN],
                        in1=yac[:, dd, w * WIN:(w + 1) * WIN], op=ALU.add),
                        reads=((tag, "y", dd, w), ("x", dd, w)),
                        writes=(("x", dd, w),))





NCH_ = D // 128
def prep_weights(inputs):
    g = lambda k: np.asarray(inputs[k])
    W = {}
    W["ffn0_wrT"] = np.ascontiguousarray(g("L0_ffn_W_r").T.astype(np.float32))
    W["ffn0_wkT"] = np.ascontiguousarray(g("L0_ffn_W_k").T.astype(np.float32))
    W["ffn0_wvT"] = np.ascontiguousarray(g("L0_ffn_W_v").T.astype(np.float32))
    def vecpack(*vs):
        a = np.stack([np.asarray(v, dtype=np.float32) for v in vs])
        return np.ascontiguousarray(a.reshape(len(vs), NCH_, 128).transpose(2, 0, 1))
    W["ffn0_mu"] = vecpack(g("L0_ffn_mu_r"), g("L0_ffn_mu_k"))
    W["l1ffn_wqT"] = np.ascontiguousarray(
        g("L1_ffn_weight_q").T.astype(ml_dtypes.bfloat16))
    W["l1ffn_vec"] = vecpack(g("L1_ffn_weight_scale"), g("L1_ffn_bias"))
    W["l1aa_wq"] = np.ascontiguousarray(g("L1_attn_a_W_q").astype(np.float32))
    W["l1aa_wk"] = np.ascontiguousarray(g("L1_attn_a_W_k").astype(np.float32))
    W["l1aa_wv"] = np.ascontiguousarray(g("L1_attn_a_W_v").astype(np.float32))
    W["l1aa_wo"] = np.ascontiguousarray(g("L1_attn_a_W_o").astype(np.float32))
    W["l1aa_bias"] = vecpack(g("L1_attn_a_b_q"), g("L1_attn_a_b_k"),
                             g("L1_attn_a_b_v"), g("L1_attn_a_b_o"))
    W["l0aa_wqT"] = np.ascontiguousarray(g("L0_attn_a_Wq").T.astype(np.float32))
    W["l0aa_wkT"] = np.ascontiguousarray(g("L0_attn_a_Wk").T.astype(np.float32))
    W["l0aa_wvT"] = np.ascontiguousarray(g("L0_attn_a_Wv").T.astype(np.float32))
    W["l0aa_woT"] = np.ascontiguousarray(g("L0_attn_a_Wo").T.astype(np.float32))
    W["l0aa_bias"] = vecpack(g("L0_attn_a_bq"), g("L0_attn_a_bk"),
                             g("L0_attn_a_bv"), g("L0_attn_a_bo"))
    W["l0ab_wq"] = np.ascontiguousarray(g("L0_attn_b_Wq").astype(np.float32))
    W["l0ab_wk"] = np.ascontiguousarray(g("L0_attn_b_Wk").astype(np.float32))
    W["l0ab_wv"] = np.ascontiguousarray(g("L0_attn_b_Wv").astype(np.float32))
    W["l0ab_wb"] = np.ascontiguousarray(
        g("L0_attn_b_Wb").astype(np.float32).reshape(NCH_, 128).T)
    W["l0ab_vec"] = vecpack(g("L0_attn_b_ln_g"), g("L0_attn_b_ln_b"))
    W["l1ab_wqT"] = np.ascontiguousarray(g("L1_attn_b_wq").T.astype(np.float32))
    W["l1ab_wkT"] = np.ascontiguousarray(g("L1_attn_b_wk").T.astype(np.float32))
    W["l1ab_wvT"] = np.ascontiguousarray(g("L1_attn_b_wv").T.astype(np.float32))
    W["l1ab_woT"] = np.ascontiguousarray(g("L1_attn_b_wo").T.astype(np.float32))
    W["l1ab_trig"] = _rope_trig_table()
    return W


def _rope_trig_table():
    hd, seq = 128, 2048
    w = np.arange(0, hd, 2).astype(np.float32) / np.float32(hd)
    inv = np.float32(1.0) / np.power(10000.0, w.astype(np.float64)).astype(np.float32)
    t = np.arange(seq - 1, dtype=np.float32)
    ang = (t[:, None] * inv[None, :]).astype(np.float32)
    flat = ang.ravel()
    try:
        import ctypes
        import ctypes.util
        name = ctypes.util.find_library("m") or "libm.so.6"
        libm = ctypes.CDLL(name)
        libm.sinf.restype = ctypes.c_float
        libm.sinf.argtypes = [ctypes.c_float]
        libm.cosf.restype = ctypes.c_float
        libm.cosf.argtypes = [ctypes.c_float]
        sint = np.fromiter((libm.sinf(float(v)) for v in flat), np.float32,
                           flat.size).reshape(ang.shape)
        cost = np.fromiter((libm.cosf(float(v)) for v in flat), np.float32,
                           flat.size).reshape(ang.shape)
    except Exception:
        sint = np.sin(flat.astype(np.float64)).astype(np.float32).reshape(ang.shape)
        cost = np.cos(flat.astype(np.float64)).astype(np.float32).reshape(ang.shape)
    sfull = np.zeros((seq, 64), np.float32)
    sfull[1:] = sint
    cfull = np.ones((seq, 64), np.float32)
    cfull[1:] = cost
    trig = np.empty((128, 2, 16, 64), np.float32)
    trig[:, 0] = sfull.reshape(16, 128, 64).transpose(1, 0, 2)
    trig[:, 1] = cfull.reshape(16, 128, 64).transpose(1, 0, 2)
    return np.ascontiguousarray(trig)


LAYERS = CHAIN


def build_nc(layers=LAYERS):
    return Builder(set(layers))


def kernel(**inputs):
    x = np.ascontiguousarray(np.asarray(inputs["L0_attn_a_x"], dtype=np.float32))
    b = Builder(set(LAYERS))
    nc = b.build()
    W = prep_weights(inputs)
    need = set(b.weights.keys())
    in_maps = [dict({k: W[k] for k in need}, x=np.ascontiguousarray(x[i]))
               for i in range(B)]
    res = run_bass_kernel_spmd(nc, in_maps, core_ids=list(range(B)))
    out = np.stack([np.asarray(r["y"]) for r in res.results], axis=0)
    return out.astype(np.float32)


# version 13
# speedup vs baseline: 1.0586x; 1.0586x over previous
import numpy as np
import ml_dtypes
from contextlib import ExitStack

import concourse.bass as bass
import concourse.mybir as mybir
from concourse.bass_utils import run_bass_kernel_spmd
from concourse.bass_isa import ReduceOp
from concourse import library_config

F32 = mybir.dt.float32
BF16 = mybir.dt.bfloat16
I32 = mybir.dt.int32
ALU = mybir.AluOpType
ACT = mybir.ActivationFunctionType
AXL = mybir.AxisListType

B, T, D = 8, 2048, 1024
NCH = D // 128
NW = 4
WIN = T // NW

CHAIN = ("l0aa", "l0ffn", "l0ab", "l0ffn2", "l1aa", "l1ffn", "l1ab", "l1ffn2")


class Tracker:

    ENGINES = ("tensor", "vector", "scalar", "gpsimd", "sync")

    def __init__(self, nc, ctx):
        self.nc = nc
        self.ctx = ctx
        self.ops = {e: [] for e in self.ENGINES}
        self.sem = {e: ctx.enter_context(nc.semaphore(f"pg_{e}")) for e in self.ENGINES}
        self.count = {e: 0 for e in self.ENGINES}
        self.dch = {}
        self.seen = {e: {} for e in self.ENGINES}
        self.last_w = {}
        self.readers = {}
        self.nwait = 0

    def _dch(self, key, eng):
        if key not in self.dch:
            sem = self.ctx.enter_context(self.nc.semaphore(f"d{len(self.dch)}"))
            self.dch[key] = [sem, 0, eng]
        ch = self.dch[key]
        assert ch[2] == eng, f"dma channel {key} used from {eng} and {ch[2]}"
        return ch

    def _wait(self, eng, src, val):
        if val <= 0 or self.seen[eng].get(src, 0) >= val:
            return
        if src == eng and val > self.count[eng]:
            return
        self.seen[eng][src] = val
        sem = self.dch[src[4:]][0] if src.startswith("dch:") else self.sem[src]
        self.nwait += 1
        self.ops[eng].append(lambda e, sem=sem, v=val: e.wait_ge(sem, v))

    def emit(self, eng, fn, reads=(), writes=(), dma=None, inc=True):
        deps = {}
        def dep(src, val):
            if deps.get(src, 0) < val:
                deps[src] = val
        for r in reads:
            w = self.last_w.get(r)
            if w:
                dep(w[0], w[1])
        for wr in writes:
            w = self.last_w.get(wr)
            if w:
                dep(w[0], w[1])
            for (src, val) in self.readers.get(wr, ()):
                dep(src, val)
        for src, val in deps.items():
            self._wait(eng, src, val)
        lbl = f"{eng[0]}{self.count[eng] + 1}_w{writes[:1]}_r{reads[:1]}"
        def run(e, fn=fn, lbl=lbl):
            with self.nc.named_scope(lbl):
                return fn(e)
        if dma is not None:
            ch = self._dch(dma, eng)
            ch[1] += 16
            self.ops[eng].append(lambda e, run=run, sem=ch[0]: run(e).then_inc(sem, 16))
            tag, cnt = "dch:" + dma, ch[1]
        elif inc:
            self.count[eng] += 1
            self.ops[eng].append(lambda e, run=run, sem=self.sem[eng]:
                                 run(e).then_inc(sem, 1))
            tag, cnt = eng, self.count[eng]
        else:
            self.ops[eng].append(lambda e, run=run: run(e))
            tag, cnt = eng, self.count[eng] + 1
        for r in reads:
            rl = self.readers.setdefault(r, [])
            if (tag, cnt) not in rl:
                rl.append((tag, cnt))
        for wr in writes:
            self.last_w[wr] = (tag, cnt)
            self.readers[wr] = []
        return tag, cnt

    def raw(self, eng, fn):
        self.ops[eng].append(fn)

    def barrier(self):
        engines = ("tensor", "vector", "scalar", "gpsimd")
        snap = dict(self.count)
        for e in engines + ("sync",):
            for f in engines:
                if e != f:
                    self._wait(e, f, snap[f])

    def finalize(self, eng, keys):
        for k in keys:
            ch = self.dch[k]
            self.ops[eng].append(lambda e, sem=ch[0], v=ch[1]: e.wait_ge(sem, v))
        for r, lst in self.readers.items():
            for (tag, cnt) in lst:
                if not tag.startswith("dch:"):
                    assert cnt <= self.count[tag], (r, tag, cnt, self.count[tag])


class Builder:
    def __init__(self, layers):
        self.layers = layers
        self.nc = bass.Bass("TRN2", target_bir_lowering=False)
        self.weights = {}
        self.ctx = ExitStack()

    def win(self, name, shape, dt=F32):
        if name not in self.weights:
            self.weights[name] = (shape, dt,
                                  self.nc.dram_tensor(name, list(shape), dt,
                                                      kind="ExternalInput").ap())
        return self.weights[name][2]

    def sb(self, name, shape, dt=F32, ctx=None):
        return (ctx or self.ctx).enter_context(self.nc.sbuf_tensor(name, shape, dt))

    def bank(self):
        i = self._pb
        self._pb = (self._pb + 1) % 6
        return self.ps[i], ("ps", i)

    def mm(self, steps, out_ap, out_region, reads, start_clear=True, stop=True,
           tile_position=None):
        n = len(steps)
        for i, (l, r) in enumerate(steps):
            self.tk.emit(
                "tensor",
                lambda e, l=l, r=r, s=(start_clear and i == 0),
                t=(stop and i == n - 1):
                    e.matmul(out_ap, lhsT=l(), rhs=r(), start=s, stop=t,
                             tile_position=tile_position),
                reads=reads if i == 0 else (),
                writes=(out_region,) if i in (0, n - 1) else (),
                inc=(i == n - 1))

    def mm3(self, steps, out_ap, out_region, reads, start_clear=True,
            stop=True):
        flat = []
        for (lh, ll, rh, rl) in steps:
            flat += [(lh, rh), (lh, rl), (ll, rh)]
        self.mm(flat, out_ap, out_region, reads, start_clear, stop)

    def split_x(self, t_fn, t_reads, hi_ap, lo_ap, hreg, lreg):
        self.tk.emit("scalar", lambda e: e.activation(
            out=hi_ap, in_=t_fn(), func=ACT.Copy),
            reads=t_reads, writes=(hreg,))
        self.tk.emit("vector", lambda e: e.tensor_tensor(
            out=lo_ap, in0=t_fn(), in1=hi_ap, op=ALU.subtract),
            reads=t_reads + (hreg,), writes=(lreg,))

    def build(self):
        nc = self.nc
        x_h = nc.dram_tensor("x", [T, D], F32, kind="ExternalInput").ap()
        y_h = nc.dram_tensor("y", [T, D], F32, kind="ExternalOutput").ap()

        with self.ctx:
            tk = self.tk = Tracker(nc, self.ctx)
            self.ps = [self.sb_psum(f"ps{i}") for i in range(8)]
            self._pb = 0

            xT = self.xT = self.sb("xT", [128, NCH, T])
            self.ident = self.sb("ident", [128, 128])
            self.R_b = self.sb("R_b", [128, T])
            self.A_b = self.sb("A_b", [128, T])
            self.vec_names = []
            self.vecs = None

            tk.emit("vector", lambda e: e.memset(self.ones2[:], 1.0),
                    writes=("ones2",))
            self.firow = self.sb("firow", [128, WIN])
            self.ones2 = self.sb("ones2", [128, 128])
            self.patg = self.sb("patg", [128, 40])
            self.pcol = self.sb("pcol", [128, 1])
            self.jrow = self.sb("jrow", [128, 64])
            self.tm1 = self.sb("tm1", [128, 16])
            self.stril = self.sb("stril", [128, 128])
            self.striu = self.sb("striu", [128, 128])
            self.trilI = self.sb("trilI", [128, 128])
            self.triuI = self.sb("triuI", [128, 128])
            with ExitStack() as cctx:
                cmaski = self.sb("cmaski", [128, 4, WIN], I32, ctx=cctx)
                identi = self.sb("identi", [128, 128], I32, ctx=cctx)
                tk.emit("gpsimd", lambda e: e.iota(identi[:], pattern=[[1, 128]],
                                                   base=0, channel_multiplier=-1),
                        writes=("identi",))
                tk.emit("gpsimd", lambda e: e.tensor_scalar(
                    out=self.ident[:], in0=identi[:], scalar1=0, op0=ALU.is_equal,
                    scalar2=None), reads=("identi",), writes=("ident",))
                tk.emit("gpsimd", lambda e: e.iota(
                    cmaski[:, 0, :], pattern=[[1, WIN]], base=0,
                    channel_multiplier=0), writes=("cmaski",))
                tk.emit("gpsimd", lambda e: e.tensor_copy(
                    out=self.firow[:], in_=cmaski[:, 0, :]),
                    reads=("cmaski",), writes=("firow",))
                tk.emit("gpsimd", lambda e: e.iota(
                    cmaski[:, 0, 0:1], pattern=[[1, 1]], base=0,
                    channel_multiplier=1), writes=("cmaski",))
                tk.emit("gpsimd", lambda e: e.tensor_copy(
                    out=self.pcol[:], in_=cmaski[:, 0, 0:1]),
                    reads=("cmaski",), writes=("pcol",))
                tk.emit("gpsimd", lambda e: e.iota(
                    cmaski[:, 0, 0:64], pattern=[[1, 64]], base=0,
                    channel_multiplier=0), writes=("cmaski",))
                tk.emit("gpsimd", lambda e: e.tensor_copy(
                    out=self.jrow[:], in_=cmaski[:, 0, 0:64]),
                    reads=("cmaski",), writes=("jrow",))
                tk.emit("gpsimd", lambda e: e.iota(
                    cmaski[:, 0, 0:16], pattern=[[128, 16]], base=-1,
                    channel_multiplier=1), writes=("cmaski",))
                tk.emit("gpsimd", lambda e: e.tensor_copy(
                    out=self.tm1[:], in_=cmaski[:, 0, 0:16]),
                    reads=("cmaski",), writes=("tm1",))
                tk.emit("gpsimd", lambda e: e.iota(
                    cmaski[:, 1, 0:20], pattern=[[1, 20]], base=0,
                    channel_multiplier=-10), writes=("cmaski2",))
                tk.emit("gpsimd", lambda e: e.tensor_copy(
                    out=self.patg[:, 0:20], in_=cmaski[:, 1, 0:20]),
                    reads=("cmaski2",), writes=("patg",))
                tk.emit("gpsimd", lambda e: e.tensor_scalar(
                    out=self.patg[:, 20:40], in0=self.patg[:, 0:20], scalar1=0.0,
                    op0=ALU.is_ge, scalar2=None),
                    reads=("patg",), writes=("patg",))
                tk.emit("gpsimd", lambda e: e.tensor_scalar(
                    out=self.patg[:, 0:20], in0=self.patg[:, 0:20], scalar1=9.5,
                    op0=ALU.is_le, scalar2=None),
                    reads=("patg",), writes=("patg",))
                tk.emit("gpsimd", lambda e: e.tensor_tensor(
                    out=self.patg[:, 0:20], in0=self.patg[:, 0:20],
                    in1=self.patg[:, 20:40], op=ALU.mult),
                    reads=("patg",), writes=("patg",))
                tk.emit("gpsimd", lambda e: e.iota(
                    cmaski[:, 0, 0:128], pattern=[[1, 128]], base=0,
                    channel_multiplier=-1), writes=("cmaski",))
                tk.emit("gpsimd", lambda e: e.tensor_scalar(
                    out=self.stril[:], in0=cmaski[:, 0, 0:128], scalar1=0,
                    op0=ALU.is_lt, scalar2=None), reads=("cmaski",),
                    writes=("stril",))
                tk.emit("gpsimd", lambda e: e.tensor_scalar(
                    out=self.striu[:], in0=cmaski[:, 0, 0:128], scalar1=0,
                    op0=ALU.is_gt, scalar2=None), reads=("cmaski",),
                    writes=("striu",))
                tk.emit("gpsimd", lambda e: e.tensor_scalar(
                    out=self.trilI[:], in0=cmaski[:, 0, 0:128], scalar1=0,
                    op0=ALU.is_le, scalar2=None), reads=("cmaski",),
                    writes=("trilI",))
                tk.emit("gpsimd", lambda e: e.tensor_scalar(
                    out=self.triuI[:], in0=cmaski[:, 0, 0:128], scalar1=0,
                    op0=ALU.is_ge, scalar2=None), reads=("cmaski",),
                    writes=("triuI",))
            tk.barrier()

            with ExitStack() as lctx:
                ytile = self.sb("ytile", [128, 8, D], ctx=lctx)
                x_r = x_h.rearrange("(a p) d -> p a d", p=128)
                for g in range(4):
                    tk.emit("sync", lambda e, g=g: e.dma_start(
                        out=ytile[:, (g % 2) * 4:(g % 2) * 4 + 4, :],
                        in_=x_r[:, g * 4:(g + 1) * 4, :]),
                        writes=(("ytg", g % 2),), dma=f"xin{g % 2}")
                    for tc in range(g * 4, g * 4 + 4):
                        for cc in range(NCH):
                            bk, br = self.bank()
                            tk.emit("tensor", lambda e, tc=tc, cc=cc, bk=bk, g=g: e.transpose(
                                bk[:, 0:128],
                                in_=ytile[:, (g % 2) * 4 + tc - g * 4, cc * 128:(cc + 1) * 128],
                                identity=self.ident[:]),
                                reads=(("ytg", g % 2), "ident"), writes=(br,))
                            tk.emit("vector", lambda e, tc=tc, cc=cc, bk=bk: e.tensor_copy(
                                out=xT[:, cc, tc * 128:(tc + 1) * 128], in_=bk[:, 0:128]),
                                reads=(br,), writes=(("x", cc, tc // 4),))
                tk.barrier()

            self.vecs = self.sb("vecs", [128, 16, NCH])
            self._vec_slot = {}
            for li, name in enumerate(CHAIN):
                if name not in self.layers:
                    continue
                getattr(self, "emit_" + name.rstrip("2"))(name)
                tk.barrier()

            with ExitStack() as lctx:
                ytile2 = self.sb("ytile2", [128, 8, D], ctx=lctx)
                y_r = y_h.rearrange("(a p) d -> p a d", p=128)
                for g in range(4):
                    h = g % 2
                    for tc in range(g * 4, g * 4 + 4):
                        for cc in range(NCH):
                            bk, br = self.bank()
                            tk.emit("tensor", lambda e, tc=tc, cc=cc, bk=bk: e.transpose(
                                bk[:, 0:128], in_=xT[:, cc, tc * 128:(tc + 1) * 128],
                                identity=self.ident[:]),
                                reads=(("x", cc, tc // 4), "ident"), writes=(br,))
                            tk.emit("vector", lambda e, tc=tc, cc=cc, bk=bk, h=h, g=g: e.tensor_copy(
                                out=ytile2[:, h * 4 + tc - g * 4, cc * 128:(cc + 1) * 128],
                                in_=bk[:, 0:128]),
                                reads=(br,), writes=(("yo", h),))
                    tk.emit("scalar", lambda e, g=g, h=h: e.dma_start(
                        out=y_r[:, g * 4:(g + 1) * 4, :],
                        in_=ytile2[:, h * 4:h * 4 + 4, :]),
                        reads=(("yo", h),), writes=(("yhbm", g),), dma=f"yout{g % 2}")
                tk.finalize("scalar", [f"yout{g}" for g in range(2)])

            blk = self.ctx.enter_context(nc.Block())
            for eng in Tracker.ENGINES:
                opl = tk.ops[eng]
                if opl:
                    getattr(blk, eng)(lambda e, opl=opl: [f(e) for f in opl])
        return nc

    def sb_psum(self, name):
        return self.ctx.enter_context(self.nc.psum_tensor(name, [128, 512], F32))

    def vec(self, name):
        return self._vec_slot[name]

    def load_vecs(self, names, dram_ap, tag):
        base = len(self._vec_slot)
        for i, n in enumerate(names):
            self._vec_slot[n] = base + i
        self.tk.emit("sync", lambda e: e.dma_start(
            out=self.vecs[:, base:base + len(names), :], in_=dram_ap[:]),
            writes=(("vecs", tag),), dma=f"vec_{tag}")

    def bcast(self, src_fn, src_reads, dst_fn, dst_region, n, base=0):
        tk = self.tk
        for c0 in range(0, n, 512):
            cw = min(512, n - c0)
            bkk, br = self.bank()
            self.mm([(lambda base=base: self.ones2[base:base + 1, 0:128],
                      lambda c0=c0, cw=cw: src_fn(c0, cw))],
                    bkk[:, 0:cw], br, reads=src_reads + (("ones2"),))
            tk.emit("vector", lambda e, bkk=bkk, c0=c0, cw=cw: e.tensor_copy(
                out=dst_fn(c0, cw), in_=bkk[:, 0:cw]),
                reads=(br,), writes=(dst_region,))

    def ln_stats(self, lctx, tag):
        tk, xT = self.tk, self.xT
        acc = self.sb(f"lnacc_{tag}", [128, 2, T], ctx=lctx)
        lnr = self.sb(f"lnrw_{tag}", [128, 2, 512], ctx=lctx)
        tmp = self.sb(f"lntmp_{tag}", [128, T], ctx=lctx)
        xr = lambda c: tuple(("x", c, w) for w in range(NW))
        for c in range(NCH):
            tk.emit("scalar", lambda e, c=c: e.activation(
                out=tmp[:], in_=xT[:, c, :], func=ACT.Square),
                reads=xr(c), writes=(("lntmp", tag),))
            if c == 0:
                tk.emit("vector", lambda e: e.tensor_copy(out=acc[:, 0, :], in_=tmp[:]),
                        reads=(("lntmp", tag),), writes=(("lnacc0", tag),))
                tk.emit("vector", lambda e: e.tensor_copy(out=acc[:, 1, :], in_=xT[:, 0, :]),
                        reads=xr(0), writes=(("lnacc1", tag),))
            else:
                tk.emit("vector", lambda e, c=c: e.tensor_tensor(
                    out=acc[:, 0, :], in0=acc[:, 0, :], in1=tmp[:], op=ALU.add),
                    reads=(("lntmp", tag), ("lnacc0", tag)), writes=(("lnacc0", tag),))
                tk.emit("vector", lambda e, c=c: e.tensor_tensor(
                    out=acc[:, 1, :], in0=acc[:, 1, :], in1=xT[:, c, :], op=ALU.add),
                    reads=xr(c) + (("lnacc1", tag),), writes=(("lnacc1", tag),))
        for w in range(NW):
            sl = slice(w * WIN, (w + 1) * WIN)
            b2, r2_ = self.bank()
            self.mm([(lambda: self.ones2[:, 0:1],
                      lambda w=w: acc[:, 0, w * WIN:(w + 1) * WIN])],
                    b2[0:1, 0:WIN], r2_, reads=(("lnacc0", tag), "ones2"))
            b1, r1_ = self.bank()
            self.mm([(lambda: self.ones2[:, 0:1],
                      lambda w=w: acc[:, 1, w * WIN:(w + 1) * WIN])],
                    b1[0:1, 0:WIN], r1_, reads=(("lnacc1", tag), "ones2"))
            tk.emit("vector", lambda e, b1=b1, sl=sl: e.tensor_scalar(
                out=self.A_b[0:1, sl], in0=b1[0:1, 0:WIN], scalar1=1.0 / D,
                op0=ALU.mult, scalar2=None), reads=(r1_,), writes=("A_b",))
            tk.emit("vector", lambda e, b2=b2, sl=sl: e.tensor_scalar(
                out=lnr[0:1, 0, :], in0=b2[0:1, 0:WIN], scalar1=1.0 / D,
                op0=ALU.mult, scalar2=1e-5, op1=ALU.add),
                reads=(r2_,), writes=(("lnr0", tag),))
            tk.emit("vector", lambda e, sl=sl: e.tensor_tensor(
                out=lnr[0:1, 1, :], in0=self.A_b[0:1, sl], in1=self.A_b[0:1, sl],
                op=ALU.mult), reads=("A_b",), writes=(("lnr1", tag),))
            tk.emit("vector", lambda e: e.tensor_tensor(
                out=lnr[0:1, 0, :], in0=lnr[0:1, 0, :], in1=lnr[0:1, 1, :],
                op=ALU.subtract),
                reads=(("lnr0", tag), ("lnr1", tag)), writes=(("lnr0", tag),))
            tk.emit("scalar", lambda e: e.activation(
                out=lnr[0:1, 0, :], in_=lnr[0:1, 0, :], func=ACT.Sqrt),
                reads=(("lnr0", tag),), writes=(("lnr0", tag),))
            tk.emit("vector", lambda e, sl=sl: e.reciprocal(
                out=self.R_b[0:1, sl], in_=lnr[0:1, 0, :]),
                reads=(("lnr0", tag),), writes=("R_b",))
            tk.emit("vector", lambda e, sl=sl: e.tensor_tensor(
                out=self.A_b[0:1, sl], in0=self.A_b[0:1, sl],
                in1=self.R_b[0:1, sl], op=ALU.mult),
                reads=("A_b", "R_b"), writes=("A_b",))
        self.bcast(lambda c0, cw: self.R_b[0:1, c0:c0 + cw], ("R_b",),
                   lambda c0, cw: self.R_b[:, c0:c0 + cw], "R_b", T)
        self.bcast(lambda c0, cw: self.A_b[0:1, c0:c0 + cw], ("A_b",),
                   lambda c0, cw: self.A_b[:, c0:c0 + cw], "A_b", T)

    def normalize(self, out_ap, c, t0, n, out_region, tag=None):
        tk, xT = self.tk, self.xT
        w0, w1 = t0 // WIN, (t0 + n - 1) // WIN
        xreads = tuple(("x", c, w) for w in range(w0, w1 + 1))
        tk.emit("vector", lambda e: e.tensor_tensor(
            out=out_ap, in0=xT[:, c, t0:t0 + n], in1=self.R_b[:, t0:t0 + n],
            op=ALU.mult), reads=xreads + ("R_b",), writes=(out_region,))
        tk.emit("vector", lambda e: e.tensor_tensor(
            out=out_ap, in0=out_ap, in1=self.A_b[:, t0:t0 + n], op=ALU.subtract),
            reads=(out_region, "A_b"), writes=(out_region,))

    def emit_l0ffn(self, tag):
        nc, tk, xT = self.nc, self.tk, self.xT
        wr_h = self.win("ffn0_wrT", (D, D))
        wk_h = self.win("ffn0_wkT", (D, D))
        wv_h = self.win("ffn0_wvT", (D, D))
        wr_hi = self.win("ffn0_wrT_hi", (D, D), BF16)
        wk_hi = self.win("ffn0_wkT_hi", (D, D), BF16)
        wv_hi = self.win("ffn0_wvT_hi", (D, D), BF16)
        if "mu_r" not in self._vec_slot:
            self.load_vecs(["mu_r", "mu_k"], self.win("ffn0_mu", (128, 2, NCH)), "ffnmu")
        mu_r, mu_k = self.vec("mu_r"), self.vec("mu_k")
        QW = 256

        with ExitStack() as lctx:
            with ExitStack() as sctx:
                self.ln_stats(sctx, tag)
                tk.barrier()
            wbuf = [self.sb(f"{tag}_wf{i}", [128, NCH, QW], ctx=lctx)
                    for i in range(2)]
            whib = [self.sb(f"{tag}_wh{i}", [128, NCH, QW], BF16, ctx=lctx)
                    for i in range(2)]
            wlob = [self.sb(f"{tag}_wl{i}", [128, NCH, QW], BF16, ctx=lctx)
                    for i in range(2)]
            xn = self.sb(f"{tag}_xn", [128, NCH, WIN + 1], ctx=lctx)
            xrh = self.sb(f"{tag}_xrh", [128, NCH, WIN], BF16, ctx=lctx)
            xrl = self.sb(f"{tag}_xrl", [128, NCH, WIN], BF16, ctx=lctx)
            xkh = self.sb(f"{tag}_xkh", [128, NCH, WIN], BF16, ctx=lctx)
            xkl = self.sb(f"{tag}_xkl", [128, NCH, WIN], BF16, ctx=lctx)
            kah = self.sb(f"{tag}_kah", [128, NCH, WIN], BF16, ctx=lctx)
            kal = self.sb(f"{tag}_kal", [128, NCH, WIN], BF16, ctx=lctx)
            r_ = self.sb(f"{tag}_r", [128, NCH, WIN], ctx=lctx)
            tmp = self.sb(f"{tag}_tmp", [128, WIN], ctx=lctx)
            spl = self.sb(f"{tag}_spl", [128, 2, WIN], ctx=lctx)

            seq = []
            for w in range(NW):
                for h_ in [(wr_h, wr_hi, q) for q in range(4)] + \
                          [(wk_h, wk_hi, q) for q in range(4)] + \
                          [(wv_h, wv_hi, q) for q in range(4)]:
                    seq.append(h_)

            def load(i):
                if i >= len(seq):
                    return
                h, hhi, q = seq[i]
                bi = i % 2
                h_r = h.rearrange("(c p) d -> p c d", p=128)
                hi_r = hhi.rearrange("(c p) d -> p c d", p=128)
                tk.emit("sync", lambda e, h_r=h_r, bi=bi, q=q: e.dma_start(
                    out=wbuf[bi][:], in_=h_r[:, :, q * QW:(q + 1) * QW]),
                    writes=((tag, "wf", bi),), dma=f"{tag}_wa{bi}")
                tk.emit("sync", lambda e, hi_r=hi_r, bi=bi, q=q: e.dma_start(
                    out=whib[bi][:], in_=hi_r[:, :, q * QW:(q + 1) * QW]),
                    writes=((tag, "wh", bi),), dma=f"{tag}_wb{bi}")
                tk.emit("gpsimd", lambda e, bi=bi: e.tensor_tensor(
                    out=wlob[bi][:, :, :].rearrange("p c d -> p (c d)"),
                    in0=wbuf[bi][:, :, :].rearrange("p c d -> p (c d)"),
                    in1=whib[bi][:, :, :].rearrange("p c d -> p (c d)"),
                    op=ALU.subtract),
                    reads=((tag, "wf", bi), (tag, "wh", bi)),
                    writes=((tag, "wl", bi),))
            load(0)
            load(1)

            for w in range(NW):
                t0 = w * WIN
                if w == 0:
                    tk.emit("gpsimd", lambda e: e.memset(xn[:, :, 0:1], 0.0),
                            writes=((tag, "xnh"),))
                else:
                    tk.emit("vector", lambda e: e.tensor_copy(
                        out=xn[:, :, 0:1], in_=xn[:, :, WIN:WIN + 1]),
                        reads=tuple((tag, "xn", c) for c in range(NCH)),
                        writes=((tag, "xnh"),))
                for c in range(NCH):
                    self.normalize(xn[:, c, 1:WIN + 1], c, t0, WIN, (tag, "xn", c))
                for c in range(NCH):
                    rds = ((tag, "xn", c), (tag, "xnh"))
                    tk.emit("vector", lambda e, c=c: e.tensor_tensor(
                        out=spl[:, 0, :], in0=xn[:, c, 1:WIN + 1],
                        in1=xn[:, c, 0:WIN], op=ALU.subtract),
                        reads=rds, writes=((tag, "sp0"),))
                    tk.emit("vector", lambda e, c=c: e.scalar_tensor_tensor(
                        out=spl[:, 1, :], in0=spl[:, 0, :],
                        scalar=self.vecs[:, mu_k, c:c + 1], op0=ALU.mult,
                        in1=xn[:, c, 0:WIN], op1=ALU.add),
                        reads=((tag, "sp0"), ("vecs", "ffnmu")) + rds,
                        writes=((tag, "sp1"),))
                    self.split_x(lambda: spl[:, 1, :], ((tag, "sp1"),),
                                 xkh[:, c, :], xkl[:, c, :],
                                 (tag, "xkh", c), (tag, "xkl", c))
                    tk.emit("vector", lambda e, c=c: e.scalar_tensor_tensor(
                        out=spl[:, 1, :], in0=spl[:, 0, :],
                        scalar=self.vecs[:, mu_r, c:c + 1], op0=ALU.mult,
                        in1=xn[:, c, 0:WIN], op1=ALU.add),
                        reads=((tag, "sp0"), ("vecs", "ffnmu"),
                               (tag, "xkh", c), (tag, "xkl", c)) + rds,
                        writes=((tag, "sp1"),))
                    self.split_x(lambda: spl[:, 1, :], ((tag, "sp1"),),
                                 xrh[:, c, :], xrl[:, c, :],
                                 (tag, "xrh", c), (tag, "xrl", c))

                for ph in range(12):
                    gi = w * 12 + ph
                    bi = gi % 2
                    q = ph % 4
                    kind = ph // 4
                    srch, srcl = [(xrh, xrl), (xkh, xkl), (kah, kal)][kind]
                    hreads = tuple((tag, ["xrh", "xkh", "kah"][kind], c)
                                   for c in range(NCH))
                    lreads = tuple((tag, ["xrl", "xkl", "kal"][kind], c)
                                   for c in range(NCH))
                    for dj in range(2):
                        dd = q * 2 + dj
                        bk, br = self.bank()
                        self.mm3(
                            [(lambda c=c, dj=dj, bi=bi:
                              whib[bi][:, c, dj * 128:(dj + 1) * 128],
                              lambda c=c, dj=dj, bi=bi:
                              wlob[bi][:, c, dj * 128:(dj + 1) * 128],
                              lambda c=c, srch=srch: srch[:, c, :],
                              lambda c=c, srcl=srcl: srcl[:, c, :])
                             for c in range(NCH)],
                            bk[:, 0:WIN], br,
                            reads=hreads + lreads + ((tag, "wh", bi),
                                                     (tag, "wl", bi)))
                        if kind == 0:
                            tk.emit("scalar", lambda e, dd=dd, bk=bk:
                                    e.activation(out=r_[:, dd, :],
                                                 in_=bk[:, 0:WIN],
                                                 func=ACT.Sigmoid),
                                    reads=(br,), writes=((tag, "r", dd),))
                        elif kind == 1:
                            tk.emit("scalar", lambda e, dd=dd, bk=bk:
                                    e.activation(out=tmp[:], in_=bk[:, 0:WIN],
                                                 func=ACT.Relu),
                                    reads=(br,), writes=((tag, "tmp"),))
                            tk.emit("scalar", lambda e, dd=dd: e.activation(
                                out=tmp[:], in_=tmp[:], func=ACT.Square),
                                reads=((tag, "tmp"),), writes=((tag, "tmp"),))
                            self.split_x(lambda: tmp[:], ((tag, "tmp"),),
                                         kah[:, dd, :], kal[:, dd, :],
                                         (tag, "kah", dd), (tag, "kal", dd))
                        else:
                            tk.emit("vector", lambda e, bk=bk, dd=dd:
                                    e.tensor_tensor(
                                        out=tmp[:], in0=bk[:, 0:WIN],
                                        in1=r_[:, dd, :], op=ALU.mult),
                                    reads=(br, (tag, "r", dd)),
                                    writes=((tag, "tmp"),))
                            tk.emit("vector", lambda e, dd=dd, w=w:
                                    e.tensor_tensor(
                                        out=xT[:, dd, w * WIN:(w + 1) * WIN],
                                        in0=xT[:, dd, w * WIN:(w + 1) * WIN],
                                        in1=tmp[:], op=ALU.add),
                                    reads=((tag, "tmp"), ("x", dd, w)),
                                    writes=(("x", dd, w),))
                    load(gi + 2)

    def emit_l1ffn(self, tag):
        nc, tk, xT = self.nc, self.tk, self.xT
        wq_h = self.win("l1ffn_wqT", (D, D), BF16)
        if "l1ffn_ws" not in self._vec_slot:
            self.load_vecs(["l1ffn_ws", "l1ffn_bias"],
                           self.win("l1ffn_vec", (128, 2, NCH)), "l1ffnv")
        ws, bia = self.vec("l1ffn_ws"), self.vec("l1ffn_bias")

        with ExitStack() as lctx:
            with ExitStack() as sctx:
                self.ln_stats(sctx, tag)
                tk.barrier()
            acc = self.sb(f"{tag}_acc", [128, T], ctx=lctx)
            rep = self.sb(f"{tag}_rep", [128, T], ctx=lctx)
            xsb = self.sb(f"{tag}_xsb", [128, T], ctx=lctx)
            r2 = self.sb(f"{tag}_r2", [128, T], ctx=lctx)
            a2 = self.sb(f"{tag}_a2", [128, T], ctx=lctx)
            tmp = self.sb(f"{tag}_tmp", [128, 2, T], ctx=lctx)
            xq = self.sb(f"{tag}_xq", [128, NCH, T], BF16, ctx=lctx)
            wqb = self.sb(f"{tag}_wqb", [128, NCH, D], BF16, ctx=lctx)

            wq_r = wq_h.rearrange("(c p) d -> p c d", p=128)
            tk.emit("sync", lambda e: e.dma_start(out=wqb[:], in_=wq_r[:]),
                    writes=((tag, "wqb"),), dma=f"{tag}_w")

            for c in range(NCH):
                self.normalize(tmp[:, 0, :], c, 0, T, (tag, "t0"))
                if c == 0:
                    tk.emit("scalar", lambda e: e.activation(
                        out=acc[:], in_=tmp[:, 0, :], func=ACT.Abs),
                        reads=((tag, "t0"),), writes=((tag, "acc"),))
                else:
                    tk.emit("scalar", lambda e: e.activation(
                        out=tmp[:, 1, :], in_=tmp[:, 0, :], func=ACT.Abs),
                        reads=((tag, "t0"),), writes=((tag, "t1"),))
                    tk.emit("vector", lambda e: e.tensor_tensor(
                        out=acc[:], in0=acc[:], in1=tmp[:, 1, :], op=ALU.max),
                        reads=((tag, "t1"), (tag, "acc")), writes=((tag, "acc"),))
            qcl = self.sb(f"{tag}_qcl", [128, 3, 16], ctx=lctx)
            for b in range(16):
                bkk, br = self.bank()
                tk.emit("tensor", lambda e, b=b, bkk=bkk: e.transpose(
                    bkk[:, 0:128], in_=acc[:, b * 128:(b + 1) * 128],
                    identity=self.ident[:]),
                    reads=((tag, "acc"), "ident"), writes=(br,))
                tk.emit("vector", lambda e, b=b, bkk=bkk: e.tensor_reduce(
                    out=qcl[:, 0, b:b + 1], in_=bkk[:, 0:128], op=ALU.max,
                    axis=AXL.X), reads=(br,), writes=((tag, "amc"),))
            tk.emit("vector", lambda e: e.tensor_scalar(
                out=qcl[:, 1, :], in0=qcl[:, 0, :], scalar1=1e-6, op0=ALU.max,
                scalar2=1.0 / 127.0, op1=ALU.mult),
                reads=((tag, "amc"),), writes=((tag, "xsc"),))
            tk.emit("vector", lambda e: e.reciprocal(
                out=qcl[:, 2, :], in_=qcl[:, 1, :]),
                reads=((tag, "xsc"),), writes=((tag, "rbc"),))
            for b in range(16):
                for j, reg in ((1, (tag, "xsr")), (2, (tag, "rbr"))):
                    bkk, br = self.bank()
                    tk.emit("tensor", lambda e, b=b, j=j, bkk=bkk: e.transpose(
                        bkk[0:1, 0:128], in_=qcl[:, j, b:b + 1],
                        identity=self.ident[:]),
                        reads=((tag, "xsc"), (tag, "rbc"), "ident"), writes=(br,))
                    dst = xsb if j == 1 else rep
                    tk.emit("vector", lambda e, b=b, bkk=bkk, dst=dst, reg=reg:
                            e.tensor_copy(out=dst[0:1, b * 128:(b + 1) * 128],
                                          in_=bkk[0:1, 0:128]),
                            reads=(br,), writes=(reg,))
            self.bcast(lambda c0, cw: xsb[0:1, c0:c0 + cw], ((tag, "xsr"),),
                       lambda c0, cw: xsb[:, c0:c0 + cw], (tag, "xsb"), T)
            self.bcast(lambda c0, cw: rep[0:1, c0:c0 + cw], ((tag, "rbr"),),
                       lambda c0, cw: rep[:, c0:c0 + cw], (tag, "rb"), T)
            tk.emit("vector", lambda e: e.tensor_tensor(
                out=r2[:], in0=self.R_b[:], in1=rep[:], op=ALU.mult),
                reads=("R_b", (tag, "rb")), writes=((tag, "r2"),))
            tk.emit("vector", lambda e: e.tensor_tensor(
                out=a2[:], in0=self.A_b[:], in1=rep[:], op=ALU.mult),
                reads=("A_b", (tag, "rb")), writes=((tag, "a2"),))
            tk.emit("vector", lambda e: e.tensor_scalar(
                out=a2[:], in0=a2[:], scalar1=128.5, op0=ALU.subtract,
                scalar2=None), reads=((tag, "a2"),), writes=((tag, "a2"),))
            for c in range(NCH):
                xreads = tuple(("x", c, w) for w in range(NW))
                tk.emit("vector", lambda e, c=c: e.tensor_tensor(
                    out=tmp[:, 0, :], in0=xT[:, c, :], in1=r2[:], op=ALU.mult),
                    reads=xreads + ((tag, "r2"),), writes=((tag, "t0"),))
                tk.emit("vector", lambda e: e.tensor_tensor(
                    out=tmp[:, 0, :], in0=tmp[:, 0, :], in1=a2[:], op=ALU.subtract),
                    reads=((tag, "t0"), (tag, "a2")), writes=((tag, "t0"),))
                tk.emit("vector", lambda e: e.tensor_copy(
                    out=tmp[:, 1, :], in_=tmp[:, 0, :]),
                    reads=((tag, "t0"),), writes=((tag, "t1"),))
                for fb in (128.0, 64.0, 32.0, 16.0, 8.0, 4.0, 2.0, 1.0):
                    tk.emit("vector", lambda e, fb=fb: e.tensor_scalar(
                        out=acc[:], in0=tmp[:, 1, :], scalar1=fb, op0=ALU.is_ge,
                        scalar2=None), reads=((tag, "t1"),),
                        writes=((tag, "acc"),))
                    tk.emit("vector", lambda e, fb=fb: e.scalar_tensor_tensor(
                        out=tmp[:, 1, :], in0=acc[:], scalar=-fb, op0=ALU.mult,
                        in1=tmp[:, 1, :], op1=ALU.add),
                        reads=((tag, "acc"), (tag, "t1")), writes=((tag, "t1"),))
                tk.emit("vector", lambda e, c=c: e.scalar_tensor_tensor(
                    out=xq[:, c, :], in0=tmp[:, 0, :], scalar=-128.0, op0=ALU.add,
                    in1=tmp[:, 1, :], op1=ALU.subtract),
                    reads=((tag, "t0"), (tag, "t1")), writes=((tag, "xq", c),))
            for w in range(NW):
                for dd in range(NCH):
                    bk, br = self.bank()
                    self.mm([(lambda c=c, dd=dd: wqb[:, c, dd * 128:(dd + 1) * 128],
                              lambda c=c, w=w: xq[:, c, w * WIN:(w + 1) * WIN])
                             for c in range(NCH)],
                            bk[:, 0:WIN], br,
                            reads=tuple((tag, "xq", c) for c in range(NCH))
                                  + ((tag, "wqb"),))
                    tk.emit("vector", lambda e, bk=bk, dd=dd, w=w: e.scalar_tensor_tensor(
                        out=tmp[:, 0, w * WIN:(w + 1) * WIN], in0=bk[:, 0:WIN],
                        scalar=self.vecs[:, ws, dd:dd + 1], op0=ALU.mult,
                        in1=xsb[:, w * WIN:(w + 1) * WIN], op1=ALU.mult),
                        reads=(br, (tag, "xsb"), ("vecs", "l1ffnv")),
                        writes=((tag, "t0"),))
                    tk.emit("vector", lambda e, dd=dd, w=w: e.scalar_tensor_tensor(
                        out=xT[:, dd, w * WIN:(w + 1) * WIN],
                        in0=tmp[:, 0, w * WIN:(w + 1) * WIN],
                        scalar=self.vecs[:, bia, dd:dd + 1], op0=ALU.add,
                        in1=xT[:, dd, w * WIN:(w + 1) * WIN], op1=ALU.add),
                        reads=((tag, "t0"), ("vecs", "l1ffnv"), ("x", dd, w)),
                        writes=(("x", dd, w),))

    def col_stats(self, lctx, tag):
        tk = self.tk
        rc = self.sb(f"{tag}_rc", [128, 16], ctx=lctx)
        nac = self.sb(f"{tag}_nac", [128, 16], ctx=lctx)
        for tc in range(16):
            tk.emit("tensor", lambda e, tc=tc: e.transpose(
                self.ps[7][:, 0:1], in_=self.R_b[0:1, tc * 128:(tc + 1) * 128],
                identity=self.ident[0:1, 0:1]),
                reads=("R_b", "ident"), writes=(("ps", 7),))
            tk.emit("vector", lambda e, tc=tc: e.tensor_copy(
                out=rc[:, tc:tc + 1], in_=self.ps[7][:, 0:1]),
                reads=(("ps", 7),), writes=((tag, "rc"),))
            tk.emit("tensor", lambda e, tc=tc: e.transpose(
                self.ps[7][:, 0:1], in_=self.A_b[0:1, tc * 128:(tc + 1) * 128],
                identity=self.ident[0:1, 0:1]),
                reads=("A_b", "ident"), writes=(("ps", 7),))
            tk.emit("vector", lambda e, tc=tc: e.tensor_scalar(
                out=nac[:, tc:tc + 1], in0=self.ps[7][:, 0:1], scalar1=-1.0,
                op0=ALU.mult, scalar2=None),
                reads=(("ps", 7),), writes=((tag, "nac"),))
        return rc, nac

    def emit_l1aa(self, tag):
        nc, tk, xT = self.nc, self.tk, self.xT
        wq_h = self.win("l1aa_wq", (D, D))
        wk_h = self.win("l1aa_wk", (D, D))
        wv_h = self.win("l1aa_wv", (D, D))
        wo_h = self.win("l1aa_wo", (D, D))
        if "l1aa_bq" not in self._vec_slot:
            self.load_vecs(["l1aa_bq", "l1aa_bk", "l1aa_bv", "l1aa_bo"],
                           self.win("l1aa_bias", (128, 4, NCH)), "l1aav")
        bq, bk_, bv, bo = (self.vec("l1aa_bq"), self.vec("l1aa_bk"),
                           self.vec("l1aa_bv"), self.vec("l1aa_bo"))

        with ExitStack() as lctx:
            with ExitStack() as sctx:
                self.ln_stats(sctx, tag)
                tk.barrier()
            yac = self.sb(f"{tag}_yac", [128, NCH, T], ctx=lctx)
            qt = self.sb(f"{tag}_qt", [128, T], ctx=lctx)
            kt = self.sb(f"{tag}_kt", [128, T], ctx=lctx)
            vg = self.sb(f"{tag}_vg", [128, 16, 130], ctx=lctx)
            cmask = self.sb(f"{tag}_cmask", [128, 4, WIN], ctx=lctx)
            for j in range(4):
                tk.emit("vector", lambda e, j=j: e.tensor_scalar(
                    out=cmask[:, j, 0:1], in0=self.pcol[:], scalar1=float(128 * j),
                    op0=ALU.add, scalar2=None),
                    reads=("pcol",), writes=((tag, "cth", j),))
                tk.emit("vector", lambda e, j=j: e.tensor_scalar(
                    out=cmask[:, j, :], in0=self.firow[:],
                    scalar1=cmask[:, j, 0:1], op0=ALU.is_ge, scalar2=None),
                    reads=("firow", (tag, "cth", j)), writes=((tag, "cmask"),))
            es = self.sb(f"{tag}_es", [128, 2, WIN], ctx=lctx)
            sq = es[:, 0, :]
            wbuf = [self.sb(f"{tag}_wba", [128, D], ctx=lctx),
                    self.sb(f"{tag}_wbb", [128, D], ctx=lctx)]
            aoh = self.sb(f"{tag}_aoh", [128, 2, WIN], ctx=lctx)
            woB = self.sb(f"{tag}_woB", [128, D], ctx=lctx)
            rr = self.sb(f"{tag}_rr", [128, 2, WIN], ctx=lctx)
            wvbt = self.sb(f"{tag}_wvbt", [128, 128], ctx=lctx)
            bvbt = self.sb(f"{tag}_bvbt", [128, 128], ctx=lctx)
            mng = self.sb(f"{tag}_mng", [128, 2, 16], ctx=lctx)
            mrow = self.sb(f"{tag}_mrow", [128, 56], ctx=lctx)
            tk.emit("vector", lambda e: e.memset(mrow[:], 0.0),
                    writes=((tag, "mneg", 0), (tag, "mneg", 1)))
            nws = self.sb(f"{tag}_nws", [128, 4], ctx=lctx)
            tca = es[:, 0, :]
            ones_t = self.sb(f"{tag}_ones", [128, 1], ctx=lctx)
            tk.emit("gpsimd", lambda e: e.memset(ones_t[:], 1.0),
                    writes=((tag, "ones"),))
            rc, nac = self.col_stats(lctx, tag)

            wq_r = wq_h.rearrange("(c p) d -> p c d", p=128)
            wk_r = wk_h.rearrange("(c p) d -> p c d", p=128)
            wv_r = wv_h.rearrange("(c p) d -> p c d", p=128)
            wo_r = wo_h.rearrange("(g p) d -> p g d", p=128)
            xr_all = tuple(("x", c, w) for c in range(NCH) for w in range(NW))

            def wload(g, i):
                bi = (4 * g + i) % 2
                srcs = (wq_r[:, :, g * 128:(g + 1) * 128],
                        wk_r[:, :, g * 128:(g + 1) * 128],
                        wv_r[:, :, g * 128:(g + 1) * 128], wo_r[:, g, :])
                src = srcs[i]
                tk.emit("sync", lambda e, src=src, bi=bi: e.dma_start(
                    out=(wbuf[bi][:].rearrange("p (c d) -> p c d", d=128)
                         if i < 3 else wbuf[bi][:]), in_=src),
                    writes=((tag, "wb", bi),), dma=f"{tag}_w{bi}")
                return bi, (tag, "wb", bi)

            for g in range(8):
                for wi, (dst, bslot, dreg) in enumerate(
                        ((qt, bq, (tag, "qt")), (kt, bk_, (tag, "kt")))):
                    bi, wreg = wload(g, wi)
                    wv3 = lambda bi=bi: wbuf[bi][:].rearrange("p (c d) -> p c d", d=128)
                    self.mm([(lambda c=c, wv3=wv3: wv3()[:, c, :],
                              lambda: ones_t[:, :]) for c in range(NCH)],
                            self.ps[7][:, 0:1], ("ps", 7),
                            reads=(wreg, (tag, "ones")))
                    tk.emit("vector", lambda e, wi=wi: e.tensor_scalar(
                        out=nws[:, wi:wi + 1], in0=self.ps[7][:, 0:1],
                        scalar1=-1.0, op0=ALU.mult, scalar2=None),
                        reads=(("ps", 7),), writes=((tag, "nws", wi),))
                    for w in range(NW):
                        bkk, br = self.bank()
                        self.mm([(lambda c=c, wv3=wv3: wv3()[:, c, :],
                                  lambda c=c, w=w: xT[:, c, w * WIN:(w + 1) * WIN])
                                 for c in range(NCH)],
                                bkk[:, 0:WIN], br, reads=xr_all + (wreg,))
                        tk.emit("vector", lambda e, bkk=bkk, w=w: e.tensor_tensor(
                            out=tca[:], in0=bkk[:, 0:WIN],
                            in1=self.R_b[:, w * WIN:(w + 1) * WIN], op=ALU.mult),
                            reads=(br, "R_b"), writes=((tag, "es", 0),))
                        tk.emit("vector", lambda e, w=w, wi=wi: e.scalar_tensor_tensor(
                            out=tca[:], in0=self.A_b[:, w * WIN:(w + 1) * WIN],
                            scalar=nws[:, wi:wi + 1], op0=ALU.mult,
                            in1=tca[:], op1=ALU.add),
                            reads=((tag, "es", 0), (tag, "nws", wi), "A_b"),
                            writes=((tag, "es", 0),))
                        tk.emit("vector", lambda e, w=w, dst=dst, bslot=bslot, g=g,
                                dreg=dreg: e.tensor_scalar(
                                    out=dst[:, w * WIN:(w + 1) * WIN], in0=tca[:],
                                    scalar1=self.vecs[:, bslot, g:g + 1],
                                    op0=ALU.add, scalar2=None),
                                reads=((tag, "es", 0), ("vecs", "l1aav")),
                                writes=(dreg,))
                bi, wreg = wload(g, 2)
                wv3 = lambda bi=bi: wbuf[bi][:].rearrange("p (c d) -> p c d", d=128)
                self.mm([(lambda: ones_t[:, :],
                          lambda c=c, wv3=wv3: wv3()[:, c, :]) for c in range(NCH)],
                        self.ps[7][0:1, 0:128], ("ps", 7),
                        reads=(wreg, (tag, "ones")))
                tk.emit("vector", lambda e: e.tensor_copy(
                    out=aoh[0:1, 0, 0:128], in_=self.ps[7][0:1, 0:128]),
                    reads=(("ps", 7), (tag, "ao", 0)), writes=((tag, "ao", 0),))
                self.bcast(lambda c0, cw: aoh[0:1, 0, c0:c0 + cw],
                           ((tag, "ao", 0),),
                           lambda c0, cw: wvbt[:, c0:c0 + cw], (tag, "wvb"), 128)
                tk.emit("tensor", lambda e, g=g: e.transpose(
                    self.ps[7][0:1, 0:128], in_=self.vecs[:, bv, g:g + 1],
                    identity=self.ident[:]),
                    reads=(("vecs", "l1aav"), "ident"), writes=(("ps", 7),))
                tk.emit("vector", lambda e: e.tensor_copy(
                    out=aoh[0:1, 1, 0:128], in_=self.ps[7][0:1, 0:128]),
                    reads=(("ps", 7), (tag, "ao", 1)), writes=((tag, "ao", 1),))
                self.bcast(lambda c0, cw: aoh[0:1, 1, c0:c0 + cw],
                           ((tag, "ao", 1),),
                           lambda c0, cw: bvbt[:, c0:c0 + cw], (tag, "bvb"), 128)
                for h2 in range(2):
                    tk.emit("gpsimd", lambda e, h2=h2: e.memset(
                        vg[:, :, h2 * 65 + 64:h2 * 65 + 65], 1.0),
                        writes=((tag, "vones", h2),))
                for tc in range(16):
                    bkk, br = self.bank()
                    self.mm([(lambda c=c, tc=tc: xT[:, c, tc * 128:(tc + 1) * 128],
                              lambda c=c, wv3=wv3: wv3()[:, c, :])
                             for c in range(NCH)],
                            bkk[:, 0:128], br, reads=xr_all + (wreg,))
                    tk.emit("vector", lambda e, tc=tc: e.scalar_tensor_tensor(
                        out=tca[:, 0:128], in0=wvbt[:],
                        scalar=nac[:, tc:tc + 1], op0=ALU.mult,
                        in1=bvbt[:], op1=ALU.add),
                        reads=((tag, "wvb"), (tag, "nac"), (tag, "bvb")),
                        writes=((tag, "es", 0),))
                    for h2 in range(2):
                        tk.emit("vector", lambda e, bkk=bkk, tc=tc, h2=h2:
                                e.scalar_tensor_tensor(
                                    out=vg[:, tc, h2 * 65:h2 * 65 + 64],
                                    in0=bkk[:, h2 * 64:h2 * 64 + 64],
                                    scalar=rc[:, tc:tc + 1], op0=ALU.mult,
                                    in1=tca[:, h2 * 64:h2 * 64 + 64], op1=ALU.add),
                                reads=(br, (tag, "rc"), (tag, "es", 0)),
                                writes=((tag, "v", tc),))
                obi, owreg = wload(g, 3)
                tk.emit("sync", lambda e, g=g: e.dma_start(
                    out=woB[0:64, :], in_=wo_h[g * 128 + 64:g * 128 + 128, :]),
                    writes=((tag, "woB"),), dma=f"{tag}_woB")

                for src_, srg, mi in ((qt, (tag, "qt"), 0), (kt, (tag, "kt"), 1)):
                    for h2 in range(2):
                        for w in range(NW):
                            tk.emit("scalar", lambda e, src_=src_, h2=h2, w=w:
                                    e.activation(
                                        out=sq[h2 * 64:h2 * 64 + 64, :],
                                        in_=src_[h2 * 64:h2 * 64 + 64,
                                                 w * WIN:(w + 1) * WIN],
                                        func=ACT.Square),
                                    reads=(srg,), writes=((tag, "es", 0),))
                            bkk, br = self.bank()
                            self.mm([(lambda h2=h2: ones_t[h2 * 64:h2 * 64 + 64, :],
                                      lambda h2=h2: sq[h2 * 64:h2 * 64 + 64, :])],
                                    bkk[0:1, 0:WIN], br,
                                    reads=((tag, "es", 0), (tag, "ones")))
                            tk.emit("vector", lambda e, bkk=bkk, mi=mi, h2=h2, w=w:
                                    e.tensor_reduce(
                                        out=mrow[0:1, mi * 8 + h2 * 4 + w:
                                                 mi * 8 + h2 * 4 + w + 1],
                                        in_=bkk[0:1, 0:WIN], op=ALU.max, axis=AXL.X),
                                    reads=(br,), writes=((tag, "mr", mi, h2, w),))
                for h2 in range(2):
                    rd = tuple((tag, "mr", mi, h2, w) for mi in range(2)
                               for w in range(NW))
                    tk.emit("vector", lambda e, h2=h2: e.tensor_reduce(
                        out=mrow[0:1, 16 + h2:17 + h2],
                        in_=mrow[0:1, h2 * 4:h2 * 4 + 4],
                        op=ALU.max, axis=AXL.X), reads=rd, writes=((tag, "mq", h2),))
                    tk.emit("vector", lambda e, h2=h2: e.tensor_reduce(
                        out=mrow[0:1, 18 + h2:19 + h2],
                        in_=mrow[0:1, 8 + h2 * 4:8 + h2 * 4 + 4],
                        op=ALU.max, axis=AXL.X), reads=rd, writes=((tag, "mk", h2),))
                    tk.emit("vector", lambda e, h2=h2: e.tensor_tensor(
                        out=mrow[0:1, 20 + h2:21 + h2], in0=mrow[0:1, 16 + h2:17 + h2],
                        in1=mrow[0:1, 18 + h2:19 + h2], op=ALU.mult),
                        reads=((tag, "mq", h2), (tag, "mk", h2)),
                        writes=((tag, "mm2", h2),))
                    tk.emit("scalar", lambda e, h2=h2: e.activation(
                        out=mrow[0:1, 22 + h2:23 + h2], in_=mrow[0:1, 20 + h2:21 + h2],
                        func=ACT.Sqrt), reads=((tag, "mm2", h2),),
                        writes=((tag, "ms", h2),))
                    tk.emit("vector", lambda e, h2=h2: e.tensor_scalar(
                        out=mrow[0:1, 24 + 16 * h2:25 + 16 * h2],
                        in0=mrow[0:1, 22 + h2:23 + h2],
                        scalar1=-0.125, op0=ALU.mult, scalar2=None),
                        reads=((tag, "ms", h2),), writes=((tag, "mneg", h2),))
                    self.bcast(lambda c0, cw, h2=h2:
                               mrow[0:1, 24 + 16 * h2:24 + 16 * h2 + cw],
                               ((tag, "mneg", h2),),
                               lambda c0, cw, h2=h2: mng[:, h2, c0:c0 + cw],
                               (tag, "mnegb", h2), 16)

                for w in range(NW):
                    for h2 in range(2):
                        hp = h2 * 64
                        obk, obr = self.ps[6 + h2], ("ps", 6 + h2)
                        kcs = list(range(4 * w + 4))
                        first = True
                        for c16 in kcs:
                            sbk, sbr = self.bank()
                            self.mm([(lambda c16=c16, hp=hp: kt[hp:hp + 64,
                                                               c16 * 128:(c16 + 1) * 128],
                                      lambda w=w, hp=hp: qt[hp:hp + 64,
                                                            w * WIN:(w + 1) * WIN])],
                                    sbk[0:128, 0:WIN], sbr,
                                    reads=((tag, "kt"), (tag, "qt")))
                            ei = c16 % 2
                            tk.emit("scalar", lambda e, sbk=sbk, ei=ei, h2=h2:
                                    e.activation(
                                        out=es[:, ei, :], in_=sbk[:, 0:WIN],
                                        func=ACT.Exp,
                                        bias=mng[:, h2, 0:1], scale=0.125),
                                    reads=(sbr, (tag, "mnegb", h2)),
                                    writes=((tag, "es", ei),))
                            off = w * WIN - c16 * 128
                            if off < 127:
                                j = (-off) // 128
                                tk.emit("vector", lambda e, ei=ei, j=j: e.tensor_tensor(
                                    out=es[:, ei, :], in0=es[:, ei, :],
                                    in1=cmask[:, j, :], op=ALU.mult),
                                    reads=((tag, "es", ei), (tag, "cmask")),
                                    writes=((tag, "es", ei),))
                            self.mm([(lambda c16=c16, h2=h2: vg[:, c16,
                                                               h2 * 65:h2 * 65 + 65],
                                      lambda ei=ei: es[:, ei, :])],
                                    obk[0:65, 0:WIN], obr,
                                    reads=((tag, "es", ei), (tag, "v", c16),
                                           (tag, "vones", h2)),
                                    start_clear=first, stop=(c16 == kcs[-1]))
                            first = False
                        tk.emit("vector", lambda e, obk=obk: e.reciprocal(
                            out=rr[64:65, 0, :], in_=obk[64:65, 0:WIN]),
                            reads=(obr, (tag, "zpl")), writes=((tag, "zpl"),))
                        self.bcast(lambda c0, cw: rr[64:65, 0, c0:c0 + cw],
                                   ((tag, "zpl"),),
                                   lambda c0, cw: rr[:, 1, c0:c0 + cw],
                                   (tag, "rcpb"), WIN, base=64)
                        tk.emit("vector", lambda e, obk=obk, h2=h2:
                                e.tensor_tensor(
                                    out=aoh[0:64, h2, :],
                                    in0=obk[0:64, 0:WIN], in1=rr[0:64, 1, :],
                                    op=ALU.mult),
                                reads=(obr, (tag, "rcpb")),
                                writes=((tag, "ao", h2),))
                    for dd in range(NCH):
                        bkk, br = self.bank()
                        self.mm([(lambda dd=dd, obi=obi:
                                  wbuf[obi][0:64, dd * 128:(dd + 1) * 128],
                                  lambda: aoh[0:64, 0, :]),
                                 (lambda dd=dd: woB[0:64, dd * 128:(dd + 1) * 128],
                                  lambda: aoh[0:64, 1, :])],
                                bkk[:, 0:WIN], br,
                                reads=((tag, "ao", 0), (tag, "ao", 1), owreg,
                                       (tag, "woB")))
                        if g == 0:
                            tk.emit("vector", lambda e, bkk=bkk, dd=dd, w=w:
                                    e.tensor_scalar(
                                        out=yac[:, dd, w * WIN:(w + 1) * WIN],
                                        in0=bkk[:, 0:WIN],
                                        scalar1=self.vecs[:, bo, dd:dd + 1],
                                        op0=ALU.add, scalar2=None),
                                    reads=(br, ("vecs", "l1aav")),
                                    writes=((tag, "y", dd, w),))
                        else:
                            tk.emit("vector", lambda e, bkk=bkk, dd=dd, w=w:
                                    e.tensor_tensor(
                                        out=yac[:, dd, w * WIN:(w + 1) * WIN],
                                        in0=yac[:, dd, w * WIN:(w + 1) * WIN],
                                        in1=bkk[:, 0:WIN], op=ALU.add),
                                    reads=(br, (tag, "y", dd, w)),
                                    writes=((tag, "y", dd, w),))
            for dd in range(NCH):
                for w in range(NW):
                    tk.emit("vector", lambda e, dd=dd, w=w: e.tensor_tensor(
                        out=xT[:, dd, w * WIN:(w + 1) * WIN],
                        in0=xT[:, dd, w * WIN:(w + 1) * WIN],
                        in1=yac[:, dd, w * WIN:(w + 1) * WIN], op=ALU.add),
                        reads=((tag, "y", dd, w), ("x", dd, w)),
                        writes=(("x", dd, w),))

    def emit_l1ab(self, tag):
        import math
        nc, tk, xT = self.nc, self.tk, self.xT
        wq_h = self.win("l1ab_wqT", (D, 128))
        wk_h = self.win("l1ab_wkT", (D, 128))
        wv_h = self.win("l1ab_wvT", (D, 128))
        wo_h = self.win("l1ab_woT", (128, D))
        tg_h = self.win("l1ab_trig", (128, 2, 16, 64))

        SC = 1.0 / math.sqrt(128.0)
        SC_HI = float(np.float32(SC)); SC_LO = float(np.float32(SC - SC_HI))

        with ExitStack() as lctx:
            with ExitStack() as sctx:
                self.ln_stats(sctx, tag)
                tk.barrier()
            qv = self.sb(f"{tag}_q", [128, 16, 128], ctx=lctx)
            kv = self.sb(f"{tag}_k", [128, 16, 128], ctx=lctx)
            vv = self.sb(f"{tag}_v", [128, 16, 128], ctx=lctx)
            cs_ = self.sb(f"{tag}_cs", [128, 16, 64], ctx=lctx)
            sn = self.sb(f"{tag}_sn", [128, 16, 64], ctx=lctx)
            qtf = self.sb(f"{tag}_qtf", [128, T], ctx=lctx)
            ktf = self.sb(f"{tag}_ktf", [128, T], ctx=lctx)
            ao = self.sb(f"{tag}_ao", [128, T], ctx=lctx)
            es = self.sb(f"{tag}_es", [128, 3, WIN], ctx=lctx)
            wq_s = self.sb(f"{tag}_wq", [128, NCH, 128], ctx=lctx)
            wk_s = self.sb(f"{tag}_wk", [128, NCH, 128], ctx=lctx)
            wv_s = self.sb(f"{tag}_wv", [128, NCH, 128], ctx=lctx)
            wo_s = self.sb(f"{tag}_wo", [128, D], ctx=lctx)
            rr = self.sb(f"{tag}_rr", [128, 6, WIN], ctx=lctx)
            mrow = self.sb(f"{tag}_mrow", [128, 32], ctx=lctx)
            tk.emit("vector", lambda e: e.memset(mrow[:], 0.0),
                    writes=((tag, "mneg"),))
            tca = self.sb(f"{tag}_tca", [128, WIN], ctx=lctx)
            ones_t = self.sb(f"{tag}_ones", [128, 1], ctx=lctx)
            tk.emit("gpsimd", lambda e: e.memset(ones_t[:], 1.0),
                    writes=((tag, "ones"),))
            rc, nac = self.col_stats(lctx, tag)

            for wsb, wh, ch in ((wq_s, wq_h, "q"), (wk_s, wk_h, "k"),
                                (wv_s, wv_h, "v")):
                w_r = wh.rearrange("(c p) d -> p c d", p=128)
                tk.emit("sync", lambda e, wsb=wsb, w_r=w_r: e.dma_start(
                    out=wsb[:], in_=w_r[:]), writes=((tag, "w", ch),),
                    dma=f"{tag}_w{ch}")
            tk.emit("sync", lambda e: e.dma_start(out=wo_s[:], in_=wo_h[:]),
                    writes=((tag, "w", "o"),), dma=f"{tag}_wo")

            tk.emit("sync", lambda e: e.dma_start(out=sn[:], in_=tg_h[:, 0]),
                    writes=((tag, "sn"),), dma=f"{tag}_tg")
            tk.emit("sync", lambda e: e.dma_start(out=cs_[:], in_=tg_h[:, 1]),
                    writes=((tag, "cs"),), dma=f"{tag}_tg")

            xr_all = tuple(("x", c, w) for c in range(NCH) for w in range(NW))
            for pi_, (wsb, ch) in enumerate(((wq_s, "q"), (wk_s, "k"), (wv_s, "v"))):
                self.mm([(lambda: ones_t[:, :],
                          lambda c=c, wsb=wsb: wsb[:, c, :]) for c in range(NCH)],
                        self.ps[7][0:1, 0:128], ("ps", 7),
                        reads=((tag, "w", ch), (tag, "ones")))
                tk.emit("vector", lambda e: e.tensor_copy(
                    out=rr[0:1, 0, 0:128], in_=self.ps[7][0:1, 0:128]),
                    reads=(("ps", 7),), writes=((tag, "wsrow"),))
                self.bcast(lambda c0, cw: rr[0:1, 0, c0:c0 + cw],
                           ((tag, "wsrow"),),
                           lambda c0, cw, pi_=pi_: rr[:, 2 + pi_, c0:c0 + cw],
                           (tag, "wsb", pi_), 128)

            for pi_, (dst, wsb, ch, dreg) in enumerate(
                    ((qv, wq_s, "q", (tag, "qv")), (kv, wk_s, "k", (tag, "kv")),
                     (vv, wv_s, "v", (tag, "vv")))):
                for tc in range(16):
                    bkk, br = self.bank()
                    self.mm([(lambda c=c, tc=tc: xT[:, c, tc * 128:(tc + 1) * 128],
                              lambda c=c, wsb=wsb: wsb[:, c, :])
                             for c in range(NCH)],
                            bkk[:, 0:128], br, reads=xr_all + ((tag, "w", ch),))
                    tk.emit("vector", lambda e, tc=tc, pi_=pi_: e.tensor_scalar(
                        out=tca[:, 0:128], in0=rr[:, 2 + pi_, 0:128],
                        scalar1=nac[:, tc:tc + 1], op0=ALU.mult, scalar2=None),
                        reads=((tag, "wsb", pi_), (tag, "nac")),
                        writes=((tag, "tca"),))
                    tk.emit("vector", lambda e, bkk=bkk, tc=tc, dst=dst, dreg=dreg:
                            e.scalar_tensor_tensor(
                                out=dst[:, tc, :], in0=bkk[:, 0:128],
                                scalar=rc[:, tc:tc + 1], op0=ALU.mult,
                                in1=tca[:, 0:128], op1=ALU.add),
                            reads=(br, (tag, "rc"), (tag, "tca")),
                            writes=(dreg,))

            for src_, sreg in ((qv, (tag, "qv")), (kv, (tag, "kv"))):
                for tc in range(16):
                    tk.emit("vector", lambda e, src_=src_, tc=tc: e.tensor_tensor(
                        out=tca[:, 0:64], in0=src_[:, tc, 0:128:2],
                        in1=cs_[:, tc, :], op=ALU.mult),
                        reads=(sreg, (tag, "cs")), writes=((tag, "tca"),))
                    tk.emit("vector", lambda e, src_=src_, tc=tc: e.tensor_tensor(
                        out=tca[:, 64:128], in0=src_[:, tc, 1:128:2],
                        in1=sn[:, tc, :], op=ALU.mult),
                        reads=(sreg, (tag, "sn")), writes=((tag, "tca"),))
                    tk.emit("vector", lambda e, src_=src_, tc=tc: e.tensor_tensor(
                        out=tca[:, 128:192], in0=src_[:, tc, 0:128:2],
                        in1=sn[:, tc, :], op=ALU.mult),
                        reads=(sreg, (tag, "sn")), writes=((tag, "tca"),))
                    tk.emit("vector", lambda e, src_=src_, tc=tc: e.tensor_tensor(
                        out=tca[:, 192:256], in0=src_[:, tc, 1:128:2],
                        in1=cs_[:, tc, :], op=ALU.mult),
                        reads=(sreg, (tag, "cs")), writes=((tag, "tca"),))
                    tk.emit("vector", lambda e, src_=src_, tc=tc: e.tensor_tensor(
                        out=src_[:, tc, 0:128:2], in0=tca[:, 0:64],
                        in1=tca[:, 64:128], op=ALU.subtract),
                        reads=((tag, "tca"), (tag, "tca")), writes=(sreg,))
                    tk.emit("vector", lambda e, src_=src_, tc=tc: e.tensor_tensor(
                        out=src_[:, tc, 1:128:2], in0=tca[:, 128:192],
                        in1=tca[:, 192:256], op=ALU.add),
                        reads=((tag, "tca"), (tag, "tca")), writes=(sreg,))

            for src_, sreg, dst, dreg, scale2 in (
                    (qv, (tag, "qv"), qtf, (tag, "qtf"), None),
                    (kv, (tag, "kv"), ktf, (tag, "ktf"), (SC_HI, SC_LO))):
                for tc in range(16):
                    bkk, br = self.bank()
                    tk.emit("tensor", lambda e, src_=src_, tc=tc, bkk=bkk:
                            e.transpose(bkk[:, 0:128], in_=src_[:, tc, :],
                                        identity=self.ident[:]),
                            reads=(sreg, "ident"), writes=(br,))
                    if scale2 is None:
                        tk.emit("vector", lambda e, bkk=bkk, tc=tc, dst=dst:
                                e.tensor_copy(
                                    out=dst[:, tc * 128:(tc + 1) * 128],
                                    in_=bkk[:, 0:128]),
                                reads=(br,), writes=(dreg,))
                    else:
                        tk.emit("vector", lambda e, bkk=bkk, tc=tc, dst=dst:
                                e.tensor_scalar(
                                    out=tca[:, 0:128], in0=bkk[:, 0:128],
                                    scalar1=scale2[0], op0=ALU.mult, scalar2=None),
                                reads=(br,), writes=((tag, "tca"),))
                        tk.emit("vector", lambda e, bkk=bkk, tc=tc, dst=dst:
                                e.tensor_scalar(
                                    out=tca[:, 128:256], in0=bkk[:, 0:128],
                                    scalar1=scale2[1], op0=ALU.mult, scalar2=None),
                                reads=(br,), writes=((tag, "tca"),))
                        tk.emit("vector", lambda e, tc=tc, dst=dst, dreg=dreg:
                                e.tensor_tensor(
                                    out=dst[:, tc * 128:(tc + 1) * 128],
                                    in0=tca[:, 0:128], in1=tca[:, 128:256],
                                    op=ALU.add),
                                reads=((tag, "tca"), (tag, "tca")), writes=(dreg,))

            for mi, (src_, sreg) in enumerate(((qtf, (tag, "qtf")),
                                               (ktf, (tag, "ktf")))):
                for w in range(NW):
                    tk.emit("scalar", lambda e, src_=src_, w=w: e.activation(
                        out=tca[:], in_=src_[:, w * WIN:(w + 1) * WIN],
                        func=ACT.Square), reads=(sreg,), writes=((tag, "tca"),))
                    bkk, br = self.bank()
                    self.mm([(lambda: ones_t[:, :], lambda: tca[:])],
                            bkk[0:1, 0:WIN], br, reads=((tag, "tca"), (tag, "ones")))
                    tk.emit("vector", lambda e, bkk=bkk, mi=mi, w=w: e.tensor_reduce(
                        out=mrow[0:1, mi * 4 + w:mi * 4 + w + 1],
                        in_=bkk[0:1, 0:WIN], op=ALU.max, axis=AXL.X),
                        reads=(br,), writes=((tag, "mr", mi, w),))
            rd = tuple((tag, "mr", mi, w) for mi in range(2) for w in range(NW))
            tk.emit("vector", lambda e: e.tensor_reduce(
                out=mrow[0:1, 8:9], in_=mrow[0:1, 0:4], op=ALU.max, axis=AXL.X),
                reads=rd, writes=((tag, "mq"),))
            tk.emit("vector", lambda e: e.tensor_reduce(
                out=mrow[0:1, 9:10], in_=mrow[0:1, 4:8], op=ALU.max, axis=AXL.X),
                reads=rd, writes=((tag, "mk"),))
            tk.emit("vector", lambda e: e.tensor_tensor(
                out=mrow[0:1, 10:11], in0=mrow[0:1, 8:9], in1=mrow[0:1, 9:10],
                op=ALU.mult), reads=((tag, "mq"), (tag, "mk")),
                writes=((tag, "mm2"),))
            tk.emit("scalar", lambda e: e.activation(
                out=mrow[0:1, 11:12], in_=mrow[0:1, 10:11], func=ACT.Sqrt),
                reads=((tag, "mm2"),), writes=((tag, "ms"),))
            tk.emit("vector", lambda e: e.tensor_scalar(
                out=mrow[0:1, 12:13], in0=mrow[0:1, 11:12], scalar1=-1.0,
                op0=ALU.mult, scalar2=None), reads=((tag, "ms"),),
                writes=((tag, "mneg"),))
            self.bcast(lambda c0, cw: mrow[0:1, 12:12 + cw], ((tag, "mneg"),),
                       lambda c0, cw: rr[:, 5, c0:c0 + cw], (tag, "mnegb"), 16)

            for w in range(NW):
                obk, obr = self.ps[6], ("ps", 6)
                smb, smr = self.ps[7], ("ps", 7)
                for c16 in range(16):
                    sbk, sbr = self.bank()
                    self.mm([(lambda c16=c16: ktf[:, c16 * 128:(c16 + 1) * 128],
                              lambda w=w: qtf[:, w * WIN:(w + 1) * WIN])],
                            sbk[0:128, 0:WIN], sbr,
                            reads=((tag, "ktf"), (tag, "qtf")))
                    ei = c16 % 3
                    tk.emit("scalar", lambda e, sbk=sbk, ei=ei: e.activation(
                        out=es[:, ei, :], in_=sbk[:, 0:WIN], func=ACT.Exp,
                        bias=rr[:, 5, 0:1], scale=1.0),
                        reads=(sbr, (tag, "mnegb")), writes=((tag, "es", ei),))
                    self.mm([(lambda c16=c16: vv[:, c16, :],
                              lambda ei=ei: es[:, ei, :])],
                            obk[0:128, 0:WIN], obr,
                            reads=((tag, "es", ei), (tag, "vv")),
                            start_clear=(c16 == 0), stop=(c16 == 15))
                    self.mm([(lambda: ones_t[:, :], lambda ei=ei: es[:, ei, :])],
                            smb[0:1, 0:WIN], smr,
                            reads=((tag, "es", ei), (tag, "ones")),
                            start_clear=(c16 == 0), stop=(c16 == 15))
                tk.emit("vector", lambda e, smb=smb: e.reciprocal(
                    out=rr[0:1, 0, :], in_=smb[0:1, 0:WIN]),
                    reads=(smr,), writes=((tag, "rcp"),))
                self.bcast(lambda c0, cw: rr[0:1, 0, c0:c0 + cw], ((tag, "rcp"),),
                           lambda c0, cw: rr[:, 1, c0:c0 + cw], (tag, "rcpb"), WIN)
                tk.emit("vector", lambda e, obk=obk, w=w: e.tensor_tensor(
                    out=ao[:, w * WIN:(w + 1) * WIN], in0=obk[:, 0:WIN],
                    in1=rr[:, 1, :], op=ALU.mult),
                    reads=(obr, (tag, "rcpb")), writes=((tag, "ao", w),))

            aor = tuple((tag, "ao", w) for w in range(NW))
            for dd in range(NCH):
                for w in range(NW):
                    bkk, br = self.bank()
                    self.mm([(lambda dd=dd: wo_s[:, dd * 128:(dd + 1) * 128],
                              lambda w=w: ao[:, w * WIN:(w + 1) * WIN])],
                            bkk[:, 0:WIN], br, reads=aor + ((tag, "w", "o"),))
                    tk.emit("vector", lambda e, bkk=bkk, dd=dd, w=w: e.tensor_tensor(
                        out=xT[:, dd, w * WIN:(w + 1) * WIN],
                        in0=xT[:, dd, w * WIN:(w + 1) * WIN],
                        in1=bkk[:, 0:WIN], op=ALU.add),
                        reads=(br, ("x", dd, w)), writes=(("x", dd, w),))

    def emit_l0ab(self, tag):
        nc, tk, xT = self.nc, self.tk, self.xT
        WN, NWN = 256, 8
        wq_h = self.win("l0ab_wq", (D, D))
        wk_h = self.win("l0ab_wk", (D, D))
        wv_h = self.win("l0ab_wv", (D, D))
        wb_h = self.win("l0ab_wb", (128, NCH))
        if "l0ab_g" not in self._vec_slot:
            self.load_vecs(["l0ab_g", "l0ab_b"],
                           self.win("l0ab_vec", (128, 2, NCH)), "l0abv")
        g_s, b_s = self.vec("l0ab_g"), self.vec("l0ab_b")

        with ExitStack() as lctx:
            with ExitStack() as sctx:
                self.ln_stats(sctx, tag)
                tk.barrier()
            S_ = self.sb(f"{tag}_S", [128, NCH, D], ctx=lctx)
            qtw = self.sb(f"{tag}_qt", [128, NCH, WN], ctx=lctx)
            ktw = self.sb(f"{tag}_kt", [128, NCH, WN], ctx=lctx)
            vtw = self.sb(f"{tag}_vt", [128, 2, D], ctx=lctx)
            u_ = self.sb(f"{tag}_u", [128, D], ctx=lctx)
            osb = self.sb(f"{tag}_osb", [128, NCH, 128], ctx=lctx)
            ktok = self.sb(f"{tag}_ktok", [128, NCH, 128], ctx=lctx)
            wbuf = [self.sb(f"{tag}_wba", [128, NCH, 256], ctx=lctx),
                    self.sb(f"{tag}_wbb", [128, NCH, 256], ctx=lctx)]
            wbcol = self.sb(f"{tag}_wbcol", [128, NCH], ctx=lctx)
            m0 = self.sb(f"{tag}_m0", [128, 8, 128], ctx=lctx)
            twoI = self.sb(f"{tag}_twoI", [128, 128], ctx=lctx)
            rows = self.sb(f"{tag}_rows", [128, 6, WN], ctx=lctx)
            bcol = self.sb(f"{tag}_bcol", [128, 2], ctx=lctx)
            wvb = self.sb(f"{tag}_wvb", [128, D], ctx=lctx)
            nws = self.sb(f"{tag}_nws", [128, 4], ctx=lctx)
            tca = self.sb(f"{tag}_tca", [128, D], ctx=lctx)
            rq2 = self.sb(f"{tag}_rq2", [128, T], ctx=lctx)
            ones_t = self.sb(f"{tag}_ones", [128, 1], ctx=lctx)
            tk.emit("gpsimd", lambda e: e.memset(ones_t[:], 1.0),
                    writes=((tag, "ones"),))
            tk.emit("gpsimd", lambda e: e.memset(S_[:], 0.0), writes=((tag, "S"),))
            tk.emit("vector", lambda e: e.tensor_scalar(
                out=twoI[:], in0=self.ident[:], scalar1=2.0, op0=ALU.mult,
                scalar2=None), reads=("ident",), writes=((tag, "twoI"),))
            tk.emit("vector", lambda e: e.tensor_scalar(
                out=rq2[:], in0=self.R_b[:], scalar1=0.03125, op0=ALU.mult,
                scalar2=None), reads=("R_b",), writes=((tag, "rq2"),))
            rc, nac = self.col_stats(lctx, tag)
            tk.emit("sync", lambda e: e.dma_start(out=wbcol[:], in_=wb_h[:]),
                    writes=((tag, "wbcol"),), dma=f"{tag}_wb")

            w_rs = {0: wq_h.rearrange("(c p) d -> p c d", p=128),
                    1: wk_h.rearrange("(c p) d -> p c d", p=128),
                    2: wv_h.rearrange("(c p) d -> p c d", p=128)}
            xr_all = tuple(("x", c, w) for c in range(NCH) for w in range(NW))

            def wload(seq_i, wi, quarter):
                bi = seq_i % 2
                tk.emit("sync", lambda e, wi=wi, quarter=quarter, bi=bi:
                        e.dma_start(out=wbuf[bi][:],
                                    in_=w_rs[wi][:, :, quarter * 256:
                                                 (quarter + 1) * 256]),
                        writes=((tag, "wb", bi),), dma=f"{tag}_w{bi}")
                return bi, (tag, "wb", bi)

            seq_i = 0
            for w in range(NWN):
                t0 = w * WN
                xw = tuple(("x", c, t0 // WIN) for c in range(NCH))
                bkk, br = self.bank()
                self.mm([(lambda c=c: wbcol[:, c:c + 1],
                          lambda c=c, t0=t0: xT[:, c, t0:t0 + WN])
                         for c in range(NCH)],
                        bkk[0:1, 0:WN], br, reads=xw + ((tag, "wbcol"),))
                if w == 0:
                    bk2, br2 = self.bank()
                    self.mm([(lambda c=c: wbcol[:, c:c + 1],
                              lambda: ones_t[:, :]) for c in range(NCH)],
                            bk2[0:1, 0:1], br2, reads=((tag, "wbcol"), (tag, "ones")))
                    tk.emit("vector", lambda e, bk2=bk2: e.tensor_scalar(
                        out=nws[0:1, 2:3], in0=bk2[0:1, 0:1], scalar1=-1.0,
                        op0=ALU.mult, scalar2=None),
                        reads=(br2,), writes=((tag, "nws", 2),))
                tk.emit("vector", lambda e, bkk=bkk, t0=t0: e.tensor_tensor(
                    out=rows[0:1, 0, :], in0=bkk[0:1, 0:WN],
                    in1=self.R_b[0:1, t0:t0 + WN], op=ALU.mult),
                    reads=(br, "R_b"), writes=((tag, "brow"),))
                tk.emit("vector", lambda e, t0=t0: e.scalar_tensor_tensor(
                    out=rows[0:1, 0, :], in0=self.A_b[0:1, t0:t0 + WN],
                    scalar=nws[0:1, 2:3], op0=ALU.mult,
                    in1=rows[0:1, 0, :], op1=ALU.add),
                    reads=((tag, "brow"), (tag, "nws", 2), "A_b"),
                    writes=((tag, "brow"),))
                tk.emit("scalar", lambda e: e.activation(
                    out=rows[0:1, 0, :], in_=rows[0:1, 0, :], func=ACT.Sigmoid),
                    reads=((tag, "brow"),), writes=((tag, "brow"),))

                for wi, dst, dreg in ((0, qtw, (tag, "qt")), (1, ktw, (tag, "kt"))):
                    for quarter in range(4):
                        bi, wreg = wload(seq_i, wi, quarter)
                        seq_i += 1
                        for dh in range(2):
                            dd = quarter * 2 + dh
                            bkk, br = self.bank()
                            self.mm([(lambda c=c, dh=dh, bi=bi:
                                      wbuf[bi][:, c, dh * 128:(dh + 1) * 128],
                                      lambda: ones_t[:, :]) for c in range(NCH)],
                                    bkk[:, 0:1], br, reads=(wreg, (tag, "ones")))
                            sc1 = -0.03125 if wi == 0 else -1.0
                            tk.emit("vector", lambda e, bkk=bkk, wi=wi, sc1=sc1:
                                    e.tensor_scalar(
                                        out=nws[:, wi:wi + 1], in0=bkk[:, 0:1],
                                        scalar1=sc1, op0=ALU.mult, scalar2=None),
                                    reads=(br,), writes=((tag, "nws", wi),))
                            bkk, br = self.bank()
                            self.mm([(lambda c=c, dh=dh, bi=bi:
                                      wbuf[bi][:, c, dh * 128:(dh + 1) * 128],
                                      lambda c=c, t0=t0: xT[:, c, t0:t0 + WN])
                                     for c in range(NCH)],
                                    bkk[:, 0:WN], br, reads=xw + (wreg,))
                            Rb = rq2 if wi == 0 else self.R_b
                            Rrg = (tag, "rq2") if wi == 0 else "R_b"
                            tk.emit("vector", lambda e, bkk=bkk, t0=t0, Rb=Rb:
                                    e.tensor_tensor(
                                        out=tca[:, 0:WN], in0=bkk[:, 0:WN],
                                        in1=Rb[:, t0:t0 + WN], op=ALU.mult),
                                    reads=(br, Rrg), writes=((tag, "tca"),))
                            tk.emit("vector", lambda e, t0=t0, wi=wi, dst=dst,
                                    dd=dd, dreg=dreg: e.scalar_tensor_tensor(
                                        out=dst[:, dd, :],
                                        in0=self.A_b[:, t0:t0 + WN],
                                        scalar=nws[:, wi:wi + 1], op0=ALU.mult,
                                        in1=tca[:, 0:WN], op1=ALU.add),
                                    reads=((tag, "tca"), (tag, "nws", wi), "A_b"),
                                    writes=(dreg, (dreg, dd)))
                ktr = tuple(((tag, "kt"), dd) for dd in range(NCH))
                for dd in range(NCH):
                    tk.emit("scalar", lambda e, dd=dd: e.activation(
                        out=rows[:, 1, :], in_=ktw[:, dd, :], func=ACT.Square),
                        reads=((tag, "kt"),), writes=((tag, "ksq"),))
                    if dd == 0:
                        tk.emit("vector", lambda e: e.tensor_copy(
                            out=rows[:, 2, :], in_=rows[:, 1, :]),
                            reads=((tag, "ksq"),), writes=((tag, "kacc"),))
                    else:
                        tk.emit("vector", lambda e: e.tensor_tensor(
                            out=rows[:, 2, :], in0=rows[:, 2, :],
                            in1=rows[:, 1, :], op=ALU.add),
                            reads=((tag, "ksq"), (tag, "kacc")),
                            writes=((tag, "kacc"),))
                bkk, brq = self.bank()
                self.mm([(lambda: self.ones2[:, 0:1], lambda: rows[:, 2, :])],
                        bkk[0:1, 0:WN], brq, reads=((tag, "kacc"), "ones2"))
                tk.emit("scalar", lambda e, bkk=bkk: e.activation(
                    out=rows[0:1, 1, :], in_=bkk[0:1, 0:WN], func=ACT.Sqrt),
                    reads=(brq,), writes=((tag, "ksq"),))
                tk.emit("vector", lambda e: e.tensor_scalar(
                    out=rows[0:1, 1, :], in0=rows[0:1, 1, :], scalar1=1e-12,
                    op0=ALU.max, scalar2=None),
                    reads=((tag, "ksq"),), writes=((tag, "ksq"),))
                tk.emit("vector", lambda e: e.reciprocal(
                    out=rows[0:1, 1, :], in_=rows[0:1, 1, :]),
                    reads=((tag, "ksq"),), writes=((tag, "ksq"),))
                self.bcast(lambda c0, cw: rows[0:1, 1, c0:c0 + cw], ((tag, "ksq"),),
                           lambda c0, cw: rows[:, 2, c0:c0 + cw], (tag, "kacc"), WN)
                for dd in range(NCH):
                    tk.emit("vector", lambda e, dd=dd: e.tensor_tensor(
                        out=ktw[:, dd, :], in0=ktw[:, dd, :], in1=rows[:, 2, :],
                        op=ALU.mult), reads=((tag, "kacc"), (tag, "kt")),
                        writes=((tag, "kt"),))
                for quarter in range(4):
                    bi, wreg = wload(seq_i, 2, quarter)
                    seq_i += 1
                    if w == 0:
                        bkk, br = self.bank()
                        self.mm([(lambda: ones_t[:, :],
                                  lambda c=c, bi=bi: wbuf[bi][:, c, :])
                                 for c in range(NCH)],
                                bkk[0:1, 0:256], br, reads=(wreg, (tag, "ones")))
                        tk.emit("vector", lambda e, bkk=bkk, quarter=quarter:
                                e.tensor_copy(
                                    out=rows[0:1, 3, 0:256], in_=bkk[0:1, 0:256]),
                                reads=(br,), writes=((tag, "wvrow"),))
                        self.bcast(lambda c0, cw: rows[0:1, 3, c0:c0 + cw],
                                   ((tag, "wvrow"),),
                                   lambda c0, cw, quarter=quarter:
                                   wvb[:, quarter * 256 + c0:quarter * 256 + c0 + cw],
                                   (tag, "wvb"), 256)
                    for cc in range(2):
                        tc = (t0 + cc * 128) // 128
                        bkk, br = self.bank()
                        self.mm([(lambda c=c, tc=tc: xT[:, c,
                                                        tc * 128:(tc + 1) * 128],
                                  lambda c=c, bi=bi: wbuf[bi][:, c, :])
                                 for c in range(NCH)],
                                bkk[:, 0:256], br, reads=xw + (wreg,))
                        tk.emit("vector", lambda e, tc=tc, quarter=quarter:
                                e.tensor_scalar(
                                    out=tca[:, 0:256],
                                    in0=wvb[:, quarter * 256:(quarter + 1) * 256],
                                    scalar1=nac[:, tc:tc + 1], op0=ALU.mult,
                                    scalar2=None),
                                reads=((tag, "wvb"), (tag, "nac")),
                                writes=((tag, "tca"),))
                        tk.emit("vector", lambda e, bkk=bkk, tc=tc:
                                e.scalar_tensor_tensor(
                                    out=tca[:, 256:512], in0=bkk[:, 0:256],
                                    scalar=rc[:, tc:tc + 1], op0=ALU.mult,
                                    in1=tca[:, 0:256], op1=ALU.add),
                                reads=(br, (tag, "rc"), (tag, "tca")),
                                writes=((tag, "tca"),))
                        tk.emit("scalar", lambda e, cc=cc, quarter=quarter:
                                e.activation(
                                    out=vtw[:, cc, quarter * 256:(quarter + 1) * 256],
                                    in_=tca[:, 256:512], func=ACT.Silu),
                                reads=((tag, "tca"),),
                                writes=((tag, "v", cc),))

                for cc in range(2):
                    ci = w * 2 + cc
                    qsl = lambda c, cc=cc: qtw[:, c, cc * 128:(cc + 1) * 128]
                    ksl = lambda c, cc=cc: ktw[:, c, cc * 128:(cc + 1) * 128]
                    tk.emit("tensor", lambda e, cc=cc: e.transpose(
                        self.ps[7][:, 0:1],
                        in_=rows[0:1, 0, cc * 128:(cc + 1) * 128],
                        identity=self.ident[0:1, 0:1]),
                        reads=((tag, "brow"), "ident"), writes=(("ps", 7),))
                    tk.emit("vector", lambda e, cc=cc: e.tensor_copy(
                        out=bcol[:, cc:cc + 1], in_=self.ps[7][:, 0:1]),
                        reads=(("ps", 7),), writes=((tag, "bcol"),))
                    self.bcast(lambda c0, cw, cc=cc:
                               rows[0:1, 0, cc * 128 + c0:cc * 128 + c0 + cw],
                               ((tag, "brow"),),
                               lambda c0, cw: m0[:, 6, c0:c0 + cw],
                               (tag, "brb"), 128)
                    bkk, br = self.bank()
                    self.mm([(lambda c=c, ksl=ksl: ksl(c),
                              lambda c=c, ksl=ksl: ksl(c)) for c in range(NCH)],
                            bkk[:, 0:128], br, reads=((tag, "kt"),))
                    tk.emit("vector", lambda e, bkk=bkk: e.tensor_tensor(
                        out=m0[:, 7, :], in0=bkk[:, 0:128], in1=self.stril[:],
                        op=ALU.mult), reads=(br, "stril"), writes=((tag, "m7"),))
                    tk.emit("vector", lambda e, cc=cc: e.tensor_scalar(
                        out=m0[:, 0, :], in0=m0[:, 7, :],
                        scalar1=bcol[:, cc:cc + 1], op0=ALU.mult, scalar2=None),
                        reads=((tag, "m7"), (tag, "bcol")), writes=((tag, "m0"),))
                    tk.emit("vector", lambda e, bkk=bkk: e.tensor_tensor(
                        out=m0[:, 7, :], in0=bkk[:, 0:128], in1=self.striu[:],
                        op=ALU.mult), reads=(br, "striu"), writes=((tag, "m7"),))
                    tk.emit("vector", lambda e: e.tensor_tensor(
                        out=m0[:, 1, :], in0=m0[:, 7, :], in1=m0[:, 6, :],
                        op=ALU.mult), reads=((tag, "m7"), (tag, "brb")),
                        writes=((tag, "m1"),))
                    tk.emit("vector", lambda e: e.tensor_tensor(
                        out=m0[:, 2, :], in0=self.ident[:], in1=m0[:, 0, :],
                        op=ALU.subtract), reads=((tag, "m0"), "ident"),
                        writes=((tag, "m2"),))
                    tk.emit("vector", lambda e: e.tensor_tensor(
                        out=m0[:, 3, :], in0=self.ident[:], in1=m0[:, 1, :],
                        op=ALU.subtract), reads=((tag, "m1"), "ident"),
                        writes=((tag, "m3"),))
                    tk.emit("vector", lambda e: e.tensor_tensor(
                        out=m0[:, 1, :], in0=m0[:, 1, :], in1=self.ident[:],
                        op=ALU.add), reads=((tag, "m1"), "ident"),
                        writes=((tag, "m1"),))
                    for it in range(6):
                        bky, bry = self.bank()
                        self.mm([(lambda: m0[:, 1, :], lambda: m0[:, 2, :])],
                                bky[:, 0:128], bry,
                                reads=((tag, "m1"), (tag, "m2")))
                        tk.emit("vector", lambda e, bky=bky: e.tensor_tensor(
                            out=m0[:, 4, :], in0=twoI[:], in1=bky[:, 0:128],
                            op=ALU.subtract),
                            reads=(bry, (tag, "twoI")), writes=((tag, "m4"),))
                        bkx, brx = self.bank()
                        self.mm([(lambda: m0[:, 3, :], lambda: m0[:, 4, :])],
                                bkx[:, 0:128], brx,
                                reads=((tag, "m3"), (tag, "m4")))
                        bkxt, brxt = self.bank()
                        self.mm([(lambda: m0[:, 4, :], lambda: m0[:, 3, :])],
                                bkxt[:, 0:128], brxt,
                                reads=((tag, "m3"), (tag, "m4")))
                        tk.emit("vector", lambda e, bkx=bkx: e.tensor_copy(
                            out=m0[:, 2, :], in_=bkx[:, 0:128]),
                            reads=(brx,), writes=((tag, "m2"),))
                        tk.emit("vector", lambda e, bkxt=bkxt: e.tensor_copy(
                            out=m0[:, 3, :], in_=bkxt[:, 0:128]),
                            reads=(brxt,), writes=((tag, "m3"),))
                    for dh in range(2):
                        bkk, br = self.bank()
                        if ci > 0:
                            self.mm([(lambda c=c, ksl=ksl: ksl(c),
                                      lambda c=c, dh=dh: S_[:, c,
                                                            dh * 512:(dh + 1) * 512])
                                     for c in range(NCH)],
                                    bkk[:, 0:512], br,
                                    reads=((tag, "kt"), (tag, "S")))
                            tk.emit("vector", lambda e, bkk=bkk, cc=cc, dh=dh:
                                    e.tensor_tensor(
                                        out=tca[:, dh * 512:(dh + 1) * 512],
                                        in0=vtw[:, cc, dh * 512:(dh + 1) * 512],
                                        in1=bkk[:, 0:512], op=ALU.subtract),
                                    reads=(br, (tag, "v", cc)),
                                    writes=((tag, "tca"),))
                            tk.emit("vector", lambda e, cc=cc, dh=dh:
                                    e.tensor_scalar(
                                        out=tca[:, dh * 512:(dh + 1) * 512],
                                        in0=tca[:, dh * 512:(dh + 1) * 512],
                                        scalar1=bcol[:, cc:cc + 1], op0=ALU.mult,
                                        scalar2=None),
                                    reads=((tag, "tca"), (tag, "bcol")),
                                    writes=((tag, "tca"),))
                        else:
                            tk.emit("vector", lambda e, cc=cc, dh=dh:
                                    e.tensor_scalar(
                                        out=tca[:, dh * 512:(dh + 1) * 512],
                                        in0=vtw[:, cc, dh * 512:(dh + 1) * 512],
                                        scalar1=bcol[:, cc:cc + 1], op0=ALU.mult,
                                        scalar2=None),
                                    reads=((tag, "v", cc), (tag, "bcol")),
                                    writes=((tag, "tca"),))
                    for dh in range(2):
                        bkk, br = self.bank()
                        self.mm([(lambda: m0[:, 3, :],
                                  lambda dh=dh: tca[:, dh * 512:(dh + 1) * 512])],
                                bkk[:, 0:512], br,
                                reads=((tag, "m3"), (tag, "tca")))
                        tk.emit("vector", lambda e, bkk=bkk, dh=dh: e.tensor_copy(
                            out=u_[:, dh * 512:(dh + 1) * 512], in_=bkk[:, 0:512]),
                            reads=(br,), writes=((tag, "u"),))
                    bkk, br = self.bank()
                    self.mm([(lambda c=c, ksl=ksl: ksl(c),
                              lambda c=c, qsl=qsl: qsl(c)) for c in range(NCH)],
                            bkk[:, 0:128], br, reads=((tag, "kt"), (tag, "qt")))
                    tk.emit("vector", lambda e, bkk=bkk: e.tensor_tensor(
                        out=m0[:, 5, :], in0=bkk[:, 0:128], in1=self.triuI[:],
                        op=ALU.mult), reads=(br, "triuI"), writes=((tag, "m5"),))
                    for dv in range(NCH):
                        bkk, br = self.bank()
                        steps = []
                        if ci > 0:
                            steps += [(lambda c=c, dv=dv: S_[:, c,
                                                             dv * 128:(dv + 1) * 128],
                                       lambda c=c, qsl=qsl: qsl(c))
                                      for c in range(NCH)]
                        steps += [(lambda dv=dv: u_[:, dv * 128:(dv + 1) * 128],
                                   lambda: m0[:, 5, :])]
                        self.mm(steps, bkk[:, 0:128], br,
                                reads=((tag, "S"), (tag, "qt"), (tag, "u"),
                                       (tag, "m5")))
                        tk.emit("vector", lambda e, bkk=bkk, dv=dv: e.tensor_copy(
                            out=osb[:, dv, :], in_=bkk[:, 0:128]),
                            reads=(br,), writes=((tag, "osb", dv),))
                    for c in range(NCH):
                        bkk, br = self.bank()
                        tk.emit("tensor", lambda e, c=c, ksl=ksl, bkk=bkk:
                                e.transpose(bkk[:, 0:128], in_=ksl(c),
                                            identity=self.ident[:]),
                                reads=((tag, "kt"), "ident"), writes=(br,))
                        tk.emit("vector", lambda e, bkk=bkk, c=c: e.tensor_copy(
                            out=ktok[:, c, :], in_=bkk[:, 0:128]),
                            reads=(br,), writes=((tag, "ktok"),))
                    for dk in range(NCH):
                        for dh in range(2):
                            bkk, br = self.bank()
                            self.mm([(lambda dk=dk: ktok[:, dk, :],
                                      lambda dh=dh: u_[:, dh * 512:(dh + 1) * 512])],
                                    bkk[:, 0:512], br,
                                    reads=((tag, "ktok"), (tag, "u")))
                            tk.emit("vector", lambda e, bkk=bkk, dk=dk, dh=dh:
                                    e.tensor_tensor(
                                        out=S_[:, dk, dh * 512:(dh + 1) * 512],
                                        in0=S_[:, dk, dh * 512:(dh + 1) * 512],
                                        in1=bkk[:, 0:512], op=ALU.add),
                                    reads=(br, (tag, "S")), writes=((tag, "S"),))
                    for dv in range(NCH):
                        tk.emit("scalar", lambda e, dv=dv: e.activation(
                            out=rows[:, 1, 0:128], in_=osb[:, dv, :],
                            func=ACT.Square),
                            reads=((tag, "osb", dv),), writes=((tag, "ost"),))
                        if dv == 0:
                            tk.emit("vector", lambda e: e.tensor_copy(
                                out=rows[:, 2, 0:128], in_=rows[:, 1, 0:128]),
                                reads=((tag, "ost"),), writes=((tag, "osa"),))
                            tk.emit("vector", lambda e: e.tensor_copy(
                                out=rows[:, 3, 0:128], in_=osb[:, 0, :]),
                                reads=((tag, "osb", 0),), writes=((tag, "osx"),))
                        else:
                            tk.emit("vector", lambda e: e.tensor_tensor(
                                out=rows[:, 2, 0:128], in0=rows[:, 2, 0:128],
                                in1=rows[:, 1, 0:128], op=ALU.add),
                                reads=((tag, "ost"), (tag, "osa")),
                                writes=((tag, "osa"),))
                            tk.emit("vector", lambda e, dv=dv: e.tensor_tensor(
                                out=rows[:, 3, 0:128], in0=rows[:, 3, 0:128],
                                in1=osb[:, dv, :], op=ALU.add),
                                reads=((tag, "osb", dv), (tag, "osx")),
                                writes=((tag, "osx"),))
                    bsq, brsq = self.bank()
                    self.mm([(lambda: self.ones2[:, 0:1],
                              lambda: rows[:, 2, 0:128])],
                            bsq[0:1, 0:128], brsq, reads=((tag, "osa"), "ones2"))
                    bsx, brsx = self.bank()
                    self.mm([(lambda: self.ones2[:, 0:1],
                              lambda: rows[:, 3, 0:128])],
                            bsx[0:1, 0:128], brsx, reads=((tag, "osx"), "ones2"))
                    tk.emit("vector", lambda e, bsx=bsx: e.tensor_scalar(
                        out=rows[0:1, 3, 0:128], in0=bsx[0:1, 0:128],
                        scalar1=1.0 / D, op0=ALU.mult, scalar2=None),
                        reads=(brsx,), writes=((tag, "osx"),))
                    tk.emit("vector", lambda e, bsq=bsq: e.tensor_scalar(
                        out=rows[0:1, 1, 0:128], in0=bsq[0:1, 0:128],
                        scalar1=1.0 / D, op0=ALU.mult, scalar2=1e-5, op1=ALU.add),
                        reads=(brsq,), writes=((tag, "ost"),))
                    tk.emit("vector", lambda e: e.tensor_tensor(
                        out=rows[0:1, 2, 0:128], in0=rows[0:1, 3, 0:128],
                        in1=rows[0:1, 3, 0:128], op=ALU.mult),
                        reads=((tag, "osx"),), writes=((tag, "osa"),))
                    tk.emit("vector", lambda e: e.tensor_tensor(
                        out=rows[0:1, 1, 0:128], in0=rows[0:1, 1, 0:128],
                        in1=rows[0:1, 2, 0:128], op=ALU.subtract),
                        reads=((tag, "ost"), (tag, "osa")), writes=((tag, "ost"),))
                    tk.emit("scalar", lambda e: e.activation(
                        out=rows[0:1, 1, 0:128], in_=rows[0:1, 1, 0:128],
                        func=ACT.Sqrt), reads=((tag, "ost"),),
                        writes=((tag, "ost"),))
                    tk.emit("vector", lambda e: e.reciprocal(
                        out=rows[0:1, 2, 0:128], in_=rows[0:1, 1, 0:128]),
                        reads=((tag, "ost"),), writes=((tag, "osa"),))
                    tk.emit("vector", lambda e: e.tensor_tensor(
                        out=rows[0:1, 3, 0:128], in0=rows[0:1, 3, 0:128],
                        in1=rows[0:1, 2, 0:128], op=ALU.mult),
                        reads=((tag, "osx"), (tag, "osa")), writes=((tag, "osx"),))
                    self.bcast(lambda c0, cw: rows[0:1, 2, c0:c0 + cw],
                               ((tag, "osa"),),
                               lambda c0, cw: rows[:, 4, c0:c0 + cw],
                               (tag, "oR"), 128)
                    self.bcast(lambda c0, cw: rows[0:1, 3, c0:c0 + cw],
                               ((tag, "osx"),),
                               lambda c0, cw: rows[:, 5, c0:c0 + cw],
                               (tag, "oA"), 128)
                    for dv in range(NCH):
                        tk.emit("vector", lambda e, dv=dv: e.tensor_tensor(
                            out=tca[:, 0:128], in0=osb[:, dv, :],
                            in1=rows[:, 4, 0:128], op=ALU.mult),
                            reads=((tag, "osb", dv), (tag, "oR")),
                            writes=((tag, "tca"),))
                        tk.emit("vector", lambda e: e.tensor_tensor(
                            out=tca[:, 0:128], in0=tca[:, 0:128],
                            in1=rows[:, 5, 0:128], op=ALU.subtract),
                            reads=((tag, "tca"), (tag, "oA")),
                            writes=((tag, "tca"),))
                        tk.emit("vector", lambda e, dv=dv: e.tensor_scalar(
                            out=tca[:, 0:128], in0=tca[:, 0:128],
                            scalar1=self.vecs[:, g_s, dv:dv + 1], op0=ALU.mult,
                            scalar2=self.vecs[:, b_s, dv:dv + 1], op1=ALU.add),
                            reads=((tag, "tca"), ("vecs", "l0abv")),
                            writes=((tag, "tca"),))
                        tk.emit("vector", lambda e, dv=dv, ci=ci: e.tensor_tensor(
                            out=xT[:, dv, ci * 128:(ci + 1) * 128],
                            in0=xT[:, dv, ci * 128:(ci + 1) * 128],
                            in1=tca[:, 0:128], op=ALU.add),
                            reads=((tag, "tca"), ("x", dv, ci // 4)),
                            writes=(("x", dv, ci // 4),))

    def emit_l0aa(self, tag):
        nc, tk, xT = self.nc, self.tk, self.xT
        wq_h = self.win("l0aa_wqT", (D, D))
        wk_h = self.win("l0aa_wkT", (D, D))
        wv_h = self.win("l0aa_wvT", (D, D))
        wo_h = self.win("l0aa_woT", (D, D))
        if "l0aa_bq" not in self._vec_slot:
            self.load_vecs(["l0aa_bq", "l0aa_bk", "l0aa_bv", "l0aa_bo"],
                           self.win("l0aa_bias", (128, 4, NCH)), "l0aav")
        bq, bk_, bv, bo = (self.vec("l0aa_bq"), self.vec("l0aa_bk"),
                           self.vec("l0aa_bv"), self.vec("l0aa_bo"))

        with ExitStack() as lctx:
            with ExitStack() as sctx:
                self.ln_stats(sctx, tag)
                tk.barrier()
            yac = self.sb(f"{tag}_yac", [128, NCH, T], ctx=lctx)
            qg = self.sb(f"{tag}_qg", [128, 16, 128], ctx=lctx)
            ktg = self.sb(f"{tag}_ktg", [128, T], ctx=lctx)
            vgg = self.sb(f"{tag}_vgg", [128, 16, 128], ctx=lctx)
            rb_ = self.sb(f"{tag}_rb", [128, T], ctx=lctx)
            rb2 = self.sb(f"{tag}_rb2", [128, T], ctx=lctx)
            tmp = self.sb(f"{tag}_tmp", [128, 2, WIN], ctx=lctx)
            gtok = self.sb(f"{tag}_gtok", [128, 16, 20], ctx=lctx)
            atok = self.sb(f"{tag}_atok", [128, 16, 10], ctx=lctx)
            wsl = self.sb(f"{tag}_wsl", [128, D], ctx=lctx)
            qsg = self.sb(f"{tag}_qsg", [128, 20], ctx=lctx)
            outs2 = self.sb(f"{tag}_outs", [128, 128], ctx=lctx)
            qstat = self.sb(f"{tag}_qstat", [128, 16, 16], ctx=lctx)
            bct = self.sb(f"{tag}_bct", [128, 2, 128], ctx=lctx)
            cols = self.sb(f"{tag}_cols", [128, 8], ctx=lctx)
            vcolT = self.sb(f"{tag}_vcolT", [128, 2], ctx=lctx)
            v16 = self.sb(f"{tag}_v16", [128, 16], ctx=lctx)
            bnst = self.sb(f"{tag}_bnst", [128, 56], ctx=lctx)
            ones_t = self.sb(f"{tag}_ones", [128, 1], ctx=lctx)
            tk.emit("gpsimd", lambda e: e.memset(ones_t[:], 1.0),
                    writes=((tag, "ones"),))
            tk.emit("vector", lambda e: e.memset(bnst[:, 28:39], 0.0),
                    writes=((tag, "validrow"),))
            rc, nac = self.col_stats(lctx, tag)

            w_rs = {0: wq_h.rearrange("(c p) d -> p c d", p=128),
                    1: wk_h.rearrange("(c p) d -> p c d", p=128),
                    2: wv_h.rearrange("(c p) d -> p c d", p=128),
                    3: wo_h.rearrange("(g p) d -> p g d", p=128)}
            xr_all = tuple(("x", c, w) for c in range(NCH) for w in range(NW))
            wsl3 = lambda: wsl[:].rearrange("p (c d) -> p c d", d=128)

            def wload(g, i):
                src = (w_rs[i][:, :, g * 128:(g + 1) * 128] if i < 3
                       else w_rs[3][:, g, :])
                tk.emit("sync", lambda e, src=src, i=i: e.dma_start(
                    out=(wsl[:].rearrange("p (c d) -> p c d", d=128)
                         if i < 3 else wsl[:]), in_=src),
                    writes=((tag, "wsl"),), dma=f"{tag}_w")
                return (tag, "wsl")

            def row_bcast(src_fn, src_reads, plane):
                tk.emit("vector", lambda e: e.tensor_copy(
                    out=rb2[0:1, 1024:1152], in_=src_fn()),
                    reads=src_reads + ((tag, "gt1"),), writes=((tag, "gt1"),))
                self.bcast(lambda c0, cw: rb2[0:1, 1024 + c0:1024 + c0 + cw],
                           ((tag, "gt1"),),
                           lambda c0, cw, plane=plane: bct[:, plane, c0:c0 + cw],
                           (tag, "bct", plane), 128)

            st = lambda j: qstat[:, :, 2 * j:2 * j + 2]

            for g in range(8):
                wreg = wload(g, 0)
                bkk, br = self.bank()
                self.mm([(lambda: ones_t[:, :], lambda c=c: wsl3()[:, c, :])
                         for c in range(NCH)], bkk[0:1, 0:128], br,
                        reads=(wreg, (tag, "ones")))
                row_bcast(lambda bkk=bkk: bkk[0:1, 0:128], (br,), 0)
                tk.emit("tensor", lambda e, g=g: e.transpose(
                    self.ps[7][0:1, 0:128], in_=self.vecs[:, bq, g:g + 1],
                    identity=self.ident[:]),
                    reads=(("vecs", "l0aav"), "ident"), writes=(("ps", 7),))
                row_bcast(lambda: self.ps[7][0:1, 0:128], (("ps", 7),), 1)
                for tc in range(16):
                    bkk, br = self.bank()
                    self.mm([(lambda c=c, tc=tc: xT[:, c, tc * 128:(tc + 1) * 128],
                              lambda c=c: wsl3()[:, c, :]) for c in range(NCH)],
                            bkk[:, 0:128], br, reads=xr_all + (wreg,))
                    tk.emit("vector", lambda e, tc=tc: e.scalar_tensor_tensor(
                        out=tmp[:, 0, 0:128], in0=bct[:, 0, :],
                        scalar=nac[:, tc:tc + 1], op0=ALU.mult,
                        in1=bct[:, 1, :], op1=ALU.add),
                        reads=((tag, "bct", 0), (tag, "bct", 1), (tag, "nac")),
                        writes=((tag, "tmp0"),))
                    tk.emit("vector", lambda e, bkk=bkk, tc=tc: e.scalar_tensor_tensor(
                        out=qg[:, tc, :], in0=bkk[:, 0:128],
                        scalar=rc[:, tc:tc + 1], op0=ALU.mult,
                        in1=tmp[:, 0, 0:128], op1=ALU.add),
                        reads=(br, (tag, "rc"), (tag, "tmp0")),
                        writes=((tag, "qg"),))
                wreg = wload(g, 1)
                bkk, br = self.bank()
                self.mm([(lambda c=c: wsl3()[:, c, :], lambda: ones_t[:, :])
                         for c in range(NCH)], bkk[:, 0:1], br,
                        reads=(wreg, (tag, "ones")))
                tk.emit("vector", lambda e, bkk=bkk: e.tensor_scalar(
                    out=cols[:, 0:1], in0=bkk[:, 0:1], scalar1=-1.0, op0=ALU.mult,
                    scalar2=None), reads=(br,), writes=((tag, "nwsk"),))
                for w in range(NW):
                    bkk, br = self.bank()
                    self.mm([(lambda c=c: wsl3()[:, c, :],
                              lambda c=c, w=w: xT[:, c, w * WIN:(w + 1) * WIN])
                             for c in range(NCH)],
                            bkk[:, 0:WIN], br, reads=xr_all + (wreg,))
                    tk.emit("vector", lambda e, bkk=bkk, w=w: e.tensor_tensor(
                        out=tmp[:, 0, :], in0=bkk[:, 0:WIN],
                        in1=self.R_b[:, w * WIN:(w + 1) * WIN], op=ALU.mult),
                        reads=(br, "R_b"), writes=((tag, "tmp0"),))
                    tk.emit("vector", lambda e, w=w: e.scalar_tensor_tensor(
                        out=tmp[:, 0, :], in0=self.A_b[:, w * WIN:(w + 1) * WIN],
                        scalar=cols[:, 0:1], op0=ALU.mult,
                        in1=tmp[:, 0, :], op1=ALU.add),
                        reads=((tag, "tmp0"), (tag, "nwsk"), "A_b"),
                        writes=((tag, "tmp0"),))
                    tk.emit("vector", lambda e, w=w, g=g: e.tensor_scalar(
                        out=ktg[:, w * WIN:(w + 1) * WIN], in0=tmp[:, 0, :],
                        scalar1=self.vecs[:, bk_, g:g + 1], op0=ALU.add,
                        scalar2=None),
                        reads=((tag, "tmp0"), ("vecs", "l0aav")),
                        writes=((tag, "ktg"),))
                wreg = wload(g, 2)
                bkk, br = self.bank()
                self.mm([(lambda: ones_t[:, :], lambda c=c: wsl3()[:, c, :])
                         for c in range(NCH)], bkk[0:1, 0:128], br,
                        reads=(wreg, (tag, "ones")))
                row_bcast(lambda bkk=bkk: bkk[0:1, 0:128], (br,), 0)
                tk.emit("tensor", lambda e, g=g: e.transpose(
                    self.ps[7][0:1, 0:128], in_=self.vecs[:, bv, g:g + 1],
                    identity=self.ident[:]),
                    reads=(("vecs", "l0aav"), "ident"), writes=(("ps", 7),))
                row_bcast(lambda: self.ps[7][0:1, 0:128], (("ps", 7),), 1)
                for tc in range(16):
                    bkk, br = self.bank()
                    self.mm([(lambda c=c, tc=tc: xT[:, c, tc * 128:(tc + 1) * 128],
                              lambda c=c: wsl3()[:, c, :]) for c in range(NCH)],
                            bkk[:, 0:128], br, reads=xr_all + (wreg,))
                    tk.emit("vector", lambda e, tc=tc: e.scalar_tensor_tensor(
                        out=tmp[:, 0, 0:128], in0=bct[:, 0, :],
                        scalar=nac[:, tc:tc + 1], op0=ALU.mult,
                        in1=bct[:, 1, :], op1=ALU.add),
                        reads=((tag, "bct", 0), (tag, "bct", 1), (tag, "nac")),
                        writes=((tag, "tmp0"),))
                    tk.emit("vector", lambda e, bkk=bkk, tc=tc: e.scalar_tensor_tensor(
                        out=vgg[:, tc, :], in0=bkk[:, 0:128],
                        scalar=rc[:, tc:tc + 1], op0=ALU.mult,
                        in1=tmp[:, 0, 0:128], op1=ALU.add),
                        reads=(br, (tag, "rc"), (tag, "tmp0")),
                        writes=((tag, "vgg"),))
                woreg = wload(g, 3)

                for qf in range(4):
                    qv4 = lambda qf=qf: qg[:, qf * 4:(qf + 1) * 4, :].rearrange(
                        "p a (h d) -> p a h d", d=64)
                    st4 = lambda j, qf=qf: qstat[:, qf * 4:(qf + 1) * 4,
                                                 2 * j:2 * j + 2]
                    t3 = lambda pl: tmp[:, pl, :].rearrange(
                        "p (a h d) -> p a h d", a=4, d=64)
                    tk.emit("vector", lambda e, qv4=qv4, st4=st4: e.tensor_reduce(
                        out=st4(0), in_=qv4(), op=ALU.max, axis=AXL.X),
                        reads=((tag, "qg"),), writes=((tag, "qs0"),))
                    tk.emit("vector", lambda e, qv4=qv4, st4=st4, t3=t3:
                            e.tensor_tensor(
                                out=t3(0), in0=qv4(),
                                in1=st4(0).to_broadcast([128, 4, 2, 64]),
                                op=ALU.subtract),
                            reads=((tag, "qg"), (tag, "qs0")),
                            writes=((tag, "tmp0"),))
                    tk.emit("scalar", lambda e: e.activation(
                        out=tmp[:, 1, :], in_=tmp[:, 0, :], func=ACT.Exp),
                        reads=((tag, "tmp0"),), writes=((tag, "tmp1"),))
                    tk.emit("vector", lambda e, st4=st4, t3=t3: e.tensor_reduce(
                        out=st4(4), in_=t3(1), op=ALU.add, axis=AXL.X),
                        reads=((tag, "tmp1"),), writes=((tag, "qs4"),))
                    tk.emit("scalar", lambda e, st4=st4: e.activation(
                        out=st4(4), in_=st4(4), func=ACT.Ln),
                        reads=((tag, "qs4"),), writes=((tag, "qs4"),))
                    tk.emit("vector", lambda e, st4=st4, t3=t3: e.tensor_tensor(
                        out=t3(0), in0=t3(0),
                        in1=st4(4).to_broadcast([128, 4, 2, 64]), op=ALU.subtract),
                        reads=((tag, "tmp0"), (tag, "qs4")), writes=((tag, "tmp0"),))
                    tk.emit("scalar", lambda e: e.activation(
                        out=tmp[:, 0, :], in_=tmp[:, 0, :], func=ACT.Exp),
                        reads=((tag, "tmp0"),), writes=((tag, "tmp0"),))
                    tk.emit("vector", lambda e: e.tensor_scalar(
                        out=tmp[:, 1, :], in0=tmp[:, 0, :], scalar1=1e-9,
                        op0=ALU.add, scalar2=None),
                        reads=((tag, "tmp0"),), writes=((tag, "tmp1"),))
                    tk.emit("scalar", lambda e: e.activation(
                        out=tmp[:, 1, :], in_=tmp[:, 1, :], func=ACT.Ln),
                        reads=((tag, "tmp1"),), writes=((tag, "tmp1"),))
                    tk.emit("vector", lambda e: e.tensor_tensor(
                        out=tmp[:, 1, :], in0=tmp[:, 1, :], in1=tmp[:, 0, :],
                        op=ALU.mult),
                        reads=((tag, "tmp0"), (tag, "tmp1")),
                        writes=((tag, "tmp1"),))
                    tk.emit("vector", lambda e, st4=st4, t3=t3: e.tensor_reduce(
                        out=st4(3), in_=t3(1), op=ALU.add, axis=AXL.X,
                        negate=True),
                        reads=((tag, "tmp1"),), writes=((tag, "qs3"),))
                    tk.emit("scalar", lambda e, qf=qf: e.activation(
                        out=tmp[:, 0, :], in_=qg[:, qf * 4:(qf + 1) * 4, :],
                        func=ACT.Square),
                        reads=((tag, "qg"),), writes=((tag, "tmp0"),))
                    tk.emit("vector", lambda e, st4=st4, t3=t3: e.tensor_reduce(
                        out=st4(1), in_=t3(0), op=ALU.add, axis=AXL.X),
                        reads=((tag, "tmp0"),), writes=((tag, "qs1"),))
                    tk.emit("vector", lambda e, qv4=qv4, st4=st4: e.tensor_reduce(
                        out=st4(2), in_=qv4(), op=ALU.add, axis=AXL.X),
                        reads=((tag, "qg"),), writes=((tag, "qs2"),))
                tk.emit("scalar", lambda e: e.activation(
                    out=st(5), in_=st(1), func=ACT.Sqrt),
                    reads=((tag, "qs1"),), writes=((tag, "qs5"),))
                tk.emit("vector", lambda e: e.tensor_tensor(
                    out=st(6), in0=st(2), in1=st(2), op=ALU.mult),
                    reads=((tag, "qs2"),), writes=((tag, "qs6"),))
                tk.emit("vector", lambda e: e.tensor_scalar(
                    out=st(6), in0=st(6), scalar1=1.0 / 64.0, op0=ALU.mult,
                    scalar2=None), reads=((tag, "qs6"),), writes=((tag, "qs6"),))
                tk.emit("vector", lambda e: e.tensor_tensor(
                    out=st(6), in0=st(1), in1=st(6), op=ALU.subtract),
                    reads=((tag, "qs1"), (tag, "qs6")), writes=((tag, "qs6"),))
                tk.emit("vector", lambda e: e.tensor_scalar(
                    out=st(7), in0=st(5), scalar1=0.5, op0=ALU.mult, scalar2=None),
                    reads=((tag, "qs5"),), writes=((tag, "qs7"),))
                tk.emit("vector", lambda e: e.scalar_tensor_tensor(
                    out=st(7), in0=st(3), scalar=0.3, op0=ALU.mult,
                    in1=st(7), op1=ALU.add),
                    reads=((tag, "qs3"), (tag, "qs7")), writes=((tag, "qs7"),))
                tk.emit("vector", lambda e: e.scalar_tensor_tensor(
                    out=st(7), in0=st(6), scalar=0.2 / 63.0, op0=ALU.mult,
                    in1=st(7), op1=ALU.add),
                    reads=((tag, "qs6"), (tag, "qs7")), writes=((tag, "qs7"),))
                for tc in range(16):
                    tk.emit("tensor", lambda e, tc=tc: e.transpose(
                        self.ps[7][0:2, 0:128], in_=st(7)[:, tc, :],
                        identity=self.ident[:]),
                        reads=((tag, "qs7"), "ident"), writes=(("ps", 7),))
                    tk.emit("vector", lambda e, tc=tc: e.tensor_copy(
                        out=rb_[0:2, tc * 128:(tc + 1) * 128],
                        in_=self.ps[7][0:2, 0:128]),
                        reads=(("ps", 7),), writes=((tag, "r0blk"),))
                if g == 0:
                    for i4 in range(4):
                        tk.emit("vector", lambda e, i4=i4: e.bn_stats(
                            out=bnst[0:1, i4 * 6:i4 * 6 + 6],
                            in_=rb_[0:1, i4 * 512:(i4 + 1) * 512]),
                            reads=((tag, "r0blk"),), writes=((tag, "bn"),))
                    tk.emit("vector", lambda e: e.bn_aggr(
                        out=bnst[0:1, 24:26], in_=bnst[0:1, 0:24]),
                        reads=((tag, "bn"),), writes=((tag, "bn"),))
                    tk.emit("vector", lambda e: e.tensor_scalar(
                        out=bnst[0:1, 26:27], in0=bnst[0:1, 25:26],
                        scalar1=2048.0 / 2047.0, op0=ALU.mult, scalar2=None),
                        reads=((tag, "bn"),), writes=((tag, "rr"),))
                    tk.emit("scalar", lambda e: e.activation(
                        out=bnst[0:1, 26:27], in_=bnst[0:1, 26:27], func=ACT.Sqrt),
                        reads=((tag, "rr"),), writes=((tag, "rr"),))
                    tk.emit("vector", lambda e: e.tensor_scalar(
                        out=bnst[0:1, 27:28], in0=bnst[0:1, 24:25],
                        scalar1=1e-6, op0=ALU.add, scalar2=None),
                        reads=((tag, "bn"),), writes=((tag, "den"),))
                    tk.emit("vector", lambda e: e.reciprocal(
                        out=bnst[0:1, 27:28], in_=bnst[0:1, 27:28]),
                        reads=((tag, "den"),), writes=((tag, "den"),))
                    tk.emit("vector", lambda e: e.tensor_tensor(
                        out=bnst[0:1, 26:27], in0=bnst[0:1, 26:27],
                        in1=bnst[0:1, 27:28], op=ALU.mult),
                        reads=((tag, "rr"), (tag, "den")), writes=((tag, "rr"),))
                    tk.emit("vector", lambda e: e.tensor_scalar(
                        out=bnst[0:1, 26:27], in0=bnst[0:1, 26:27],
                        scalar1=10.0, op0=ALU.mult, scalar2=None),
                        reads=((tag, "rr"),), writes=((tag, "rr"),))
                    tk.emit("vector", lambda e: e.tensor_scalar(
                        out=bnst[0:1, 28:38], in0=self.jrow[0:1, 0:10],
                        scalar1=0.5, op0=ALU.add, scalar2=None),
                        reads=("jrow",), writes=((tag, "validrow"),))
                    tk.emit("vector", lambda e: e.tensor_scalar(
                        out=bnst[0:1, 28:38], in0=bnst[0:1, 28:38],
                        scalar1=bnst[0:1, 26:27], op0=ALU.is_le, scalar2=None),
                        reads=((tag, "validrow"), (tag, "rr")),
                        writes=((tag, "validrow"),))
                    tk.emit("vector", lambda e: e.tensor_scalar(
                        out=bnst[0:1, 38:39], in0=self.jrow[0:1, 0:10],
                        scalar1=2.5, op0=ALU.is_le, scalar2=None)
                        if False else e.tensor_scalar(
                        out=bnst[0:1, 40:50], in0=self.jrow[0:1, 0:10],
                        scalar1=2.5, op0=ALU.is_le, scalar2=None),
                        reads=("jrow",), writes=((tag, "j012"),))
                    tk.emit("vector", lambda e: e.tensor_tensor(
                        out=bnst[0:1, 28:38], in0=bnst[0:1, 28:38],
                        in1=bnst[0:1, 40:50], op=ALU.max),
                        reads=((tag, "validrow"), (tag, "j012")),
                        writes=((tag, "validrow"),))
                    tk.emit("tensor", lambda e: e.transpose(
                        self.ps[7][0:10, 0:1], in_=bnst[0:1, 28:38],
                        identity=self.ident[0:1, 0:1]),
                        reads=((tag, "validrow"), "ident"), writes=(("ps", 7),))
                    tk.emit("vector", lambda e: e.tensor_copy(
                        out=cols[0:10, 3:4], in_=self.ps[7][0:10, 0:1]),
                        reads=(("ps", 7),), writes=((tag, "validcol"),))
                tk.emit("vector", lambda e: e.max(out=v16[0:2, 0:8],
                                                  in_=rb_[0:2, :]),
                        reads=((tag, "r0blk"),), writes=((tag, "v16"),))
                tk.emit("vector", lambda e: e.match_replace(
                    out=rb2[0:2, :], in_to_replace=v16[0:2, 0:8],
                    in_values=rb_[0:2, :], imm_value=-3.0e38),
                    reads=((tag, "r0blk"), (tag, "v16")), writes=((tag, "gt1"),))
                tk.emit("vector", lambda e: e.max(out=v16[0:2, 8:16],
                                                  in_=rb2[0:2, :]),
                        reads=((tag, "gt1"),), writes=((tag, "v16"),))
                tk.emit("tensor", lambda e: e.transpose(
                    self.ps[7][0:16, 0:2], in_=v16[0:2, 0:16],
                    identity=self.ident[0:2, 0:2]),
                    reads=((tag, "v16"), "ident"), writes=(("ps", 7),))
                tk.emit("vector", lambda e: e.tensor_copy(
                    out=vcolT[0:16, :], in_=self.ps[7][0:16, 0:2]),
                    reads=(("ps", 7),), writes=((tag, "vcolT"),))
                for w in range(NW):
                    banks = []
                    for h2 in range(2):
                        bkk, br = self.bank()
                        self.mm([(lambda h2=h2: self.patg[0:2, h2 * 10:h2 * 10 + 10],
                                  lambda w=w: rb_[0:2, w * WIN:(w + 1) * WIN])],
                                bkk[0:10, 0:WIN], br,
                                reads=("patg", (tag, "r0blk")))
                        banks.append((bkk, br))
                    for h2, dstb, dreg in ((0, rb_, (tag, "r0blk")),
                                           (1, rb2, (tag, "gt1"))):
                        bkk, br = banks[h2]
                        tk.emit("vector", lambda e, bkk=bkk, w=w, h2=h2,
                                dstb=dstb, dreg=dreg: e.tensor_scalar(
                                    out=dstb[0:10, w * WIN:(w + 1) * WIN],
                                    in0=bkk[0:10, 0:WIN],
                                    scalar1=vcolT[0:10, h2:h2 + 1],
                                    op0=ALU.is_equal,
                                    scalar2=cols[0:10, 3:4], op1=ALU.mult),
                                reads=(br, (tag, "vcolT"), (tag, "validcol")),
                                writes=(dreg,))
                for h2, srcb, sreg in ((0, rb_, (tag, "r0blk")),
                                       (1, rb2, (tag, "gt1"))):
                    for tc in range(16):
                        bkk, br = self.bank()
                        tk.emit("tensor", lambda e, tc=tc, srcb=srcb, bkk=bkk:
                                e.transpose(
                                    bkk[:, 0:10],
                                    in_=srcb[0:10, tc * 128:(tc + 1) * 128],
                                    identity=self.ident[0:10, 0:10]),
                                reads=(sreg, "ident"), writes=(br,))
                        tk.emit("vector", lambda e, bkk=bkk, tc=tc, h2=h2:
                                e.tensor_copy(
                                    out=gtok[:, tc, h2 * 10:(h2 + 1) * 10],
                                    in_=bkk[:, 0:10]),
                                reads=(br,), writes=((tag, "gtok"),))
                bkk, br = self.bank()
                self.mm([(lambda tc=tc: qg[:, tc, :],
                          lambda tc=tc: gtok[:, tc, :]) for tc in range(16)],
                        bkk[:, 0:20], br, reads=((tag, "qg"), (tag, "gtok")))
                tk.emit("vector", lambda e, bkk=bkk: e.tensor_copy(
                    out=qsg[:], in_=bkk[:, 0:20]),
                    reads=(br,), writes=((tag, "qsg"),))
                for h2 in range(2):
                    hp = h2 * 64
                    for w in range(NW):
                        bkk, br = self.bank()
                        self.mm([(lambda h2=h2, hp=hp: qsg[hp:hp + 64,
                                                           h2 * 10:h2 * 10 + 10],
                                  lambda w=w, hp=hp: ktg[hp:hp + 64,
                                                         w * WIN:(w + 1) * WIN])],
                                bkk[32:42, 0:WIN], br,
                                reads=((tag, "qsg"), (tag, "ktg")),
                                tile_position=(hp, 32))
                        tk.emit("vector", lambda e, bkk=bkk, w=w: e.tensor_copy(
                            out=rb_[32:42, w * WIN:(w + 1) * WIN],
                            in_=bkk[32:42, 0:WIN]),
                            reads=(br,), writes=((tag, "sc32"),))
                    tk.emit("vector", lambda e: e.tensor_scalar(
                        out=cols[32:42, 5:6], in0=rb_[32:42, :],
                        scalar1=0.0, op0=ALU.bypass, scalar2=None)
                        if False else e.tensor_reduce(
                        out=cols[32:42, 5:6], in_=rb_[32:42, :], op=ALU.max,
                        axis=AXL.X, negate=True),
                        reads=((tag, "sc32"),), writes=((tag, "negmax"),))
                    tk.emit("vector", lambda e: e.tensor_scalar(
                        out=cols[32:42, 5:6], in0=cols[32:42, 5:6],
                        scalar1=0.125, op0=ALU.mult, scalar2=None),
                        reads=((tag, "negmax"),), writes=((tag, "negmax"),))
                    tk.emit("scalar", lambda e: e.activation(
                        out=rb_[32:42, :], in_=rb_[32:42, :], func=ACT.Exp,
                        bias=cols[32:42, 5:6], scale=0.125,
                        accum_out=cols[32:42, 4:5]),
                        reads=((tag, "sc32"), (tag, "negmax")),
                        writes=((tag, "sc32"), (tag, "sums")))
                    tk.emit("vector", lambda e: e.reciprocal(
                        out=cols[32:42, 4:5], in_=cols[32:42, 4:5]),
                        reads=((tag, "sums"),), writes=((tag, "sums"),))
                    tk.emit("vector", lambda e: e.tensor_scalar(
                        out=rb_[32:42, :], in0=rb_[32:42, :],
                        scalar1=cols[32:42, 4:5], op0=ALU.mult, scalar2=None),
                        reads=((tag, "sc32"), (tag, "sums")),
                        writes=((tag, "sc32"),))
                    for tc in range(16):
                        bkk, br = self.bank()
                        tk.emit("tensor", lambda e, tc=tc, bkk=bkk: e.transpose(
                            bkk[:, 0:10], in_=rb_[32:42, tc * 128:(tc + 1) * 128],
                            identity=self.ident[32:42, 32:42]),
                            reads=((tag, "sc32"), "ident"), writes=(br,))
                        tk.emit("vector", lambda e, bkk=bkk, tc=tc: e.tensor_copy(
                            out=atok[:, tc, :], in_=bkk[:, 0:10]),
                            reads=(br,), writes=((tag, "atok"),))
                    bkk, br = self.bank()
                    self.mm([(lambda tc=tc: atok[:, tc, :],
                              lambda tc=tc, hp=hp: vgg[:, tc, hp:hp + 64])
                             for tc in range(16)],
                            bkk[0:10, 0:64], br,
                            reads=((tag, "atok"), (tag, "vgg")))
                    tk.emit("vector", lambda e, bkk=bkk, hp=hp: e.tensor_copy(
                        out=outs2[0:10, hp:hp + 64], in_=bkk[0:10, 0:64]),
                        reads=(br,), writes=((tag, "outs"),))
                for w in range(NW):
                    bkk, br = self.bank()
                    self.mm([(lambda: outs2[0:10, 0:64],
                              lambda w=w: rb_[0:10, w * WIN:(w + 1) * WIN])],
                            bkk[0:64, 0:WIN], br,
                            reads=((tag, "outs"), (tag, "r0blk")))
                    self.mm([(lambda: outs2[0:10, 64:128],
                              lambda w=w: rb2[0:10, w * WIN:(w + 1) * WIN])],
                            bkk[64:128, 0:WIN], br,
                            reads=((tag, "outs"), (tag, "gt1")),
                            tile_position=(0, 64))
                    tk.emit("vector", lambda e, bkk=bkk: e.tensor_copy(
                        out=tmp[:, 0, :], in_=bkk[:, 0:WIN]),
                        reads=(br,), writes=((tag, "tmp0"),))
                    for dd in range(NCH):
                        bkk2, br2 = self.bank()
                        self.mm([(lambda dd=dd: wsl[:, dd * 128:(dd + 1) * 128],
                                  lambda: tmp[:, 0, :])],
                                bkk2[:, 0:WIN], br2,
                                reads=((tag, "tmp0"), woreg))
                        if g == 0:
                            tk.emit("vector", lambda e, bkk2=bkk2, dd=dd, w=w:
                                    e.tensor_scalar(
                                        out=yac[:, dd, w * WIN:(w + 1) * WIN],
                                        in0=bkk2[:, 0:WIN],
                                        scalar1=self.vecs[:, bo, dd:dd + 1],
                                        op0=ALU.add, scalar2=None),
                                    reads=(br2, ("vecs", "l0aav")),
                                    writes=((tag, "y", dd, w),))
                        else:
                            tk.emit("vector", lambda e, bkk2=bkk2, dd=dd, w=w:
                                    e.tensor_tensor(
                                        out=yac[:, dd, w * WIN:(w + 1) * WIN],
                                        in0=yac[:, dd, w * WIN:(w + 1) * WIN],
                                        in1=bkk2[:, 0:WIN], op=ALU.add),
                                    reads=(br2, (tag, "y", dd, w)),
                                    writes=((tag, "y", dd, w),))
            for dd in range(NCH):
                for w in range(NW):
                    tk.emit("vector", lambda e, dd=dd, w=w: e.tensor_tensor(
                        out=xT[:, dd, w * WIN:(w + 1) * WIN],
                        in0=xT[:, dd, w * WIN:(w + 1) * WIN],
                        in1=yac[:, dd, w * WIN:(w + 1) * WIN], op=ALU.add),
                        reads=((tag, "y", dd, w), ("x", dd, w)),
                        writes=(("x", dd, w),))





NCH_ = D // 128
def prep_weights(inputs):
    g = lambda k: np.asarray(inputs[k])
    W = {}
    W["ffn0_wrT"] = np.ascontiguousarray(g("L0_ffn_W_r").T.astype(np.float32))
    W["ffn0_wkT"] = np.ascontiguousarray(g("L0_ffn_W_k").T.astype(np.float32))
    W["ffn0_wvT"] = np.ascontiguousarray(g("L0_ffn_W_v").T.astype(np.float32))
    for k in ("ffn0_wrT", "ffn0_wkT", "ffn0_wvT"):
        W[k + "_hi"] = np.ascontiguousarray(
            W[k].view(ml_dtypes.bfloat16)[:, 1::2])
    def vecpack(*vs):
        a = np.stack([np.asarray(v, dtype=np.float32) for v in vs])
        return np.ascontiguousarray(a.reshape(len(vs), NCH_, 128).transpose(2, 0, 1))
    W["ffn0_mu"] = vecpack(g("L0_ffn_mu_r"), g("L0_ffn_mu_k"))
    W["l1ffn_wqT"] = np.ascontiguousarray(
        g("L1_ffn_weight_q").T.astype(ml_dtypes.bfloat16))
    W["l1ffn_vec"] = vecpack(g("L1_ffn_weight_scale"), g("L1_ffn_bias"))
    W["l1aa_wq"] = np.ascontiguousarray(g("L1_attn_a_W_q").astype(np.float32))
    W["l1aa_wk"] = np.ascontiguousarray(g("L1_attn_a_W_k").astype(np.float32))
    W["l1aa_wv"] = np.ascontiguousarray(g("L1_attn_a_W_v").astype(np.float32))
    W["l1aa_wo"] = np.ascontiguousarray(g("L1_attn_a_W_o").astype(np.float32))
    W["l1aa_bias"] = vecpack(g("L1_attn_a_b_q"), g("L1_attn_a_b_k"),
                             g("L1_attn_a_b_v"), g("L1_attn_a_b_o"))
    W["l0aa_wqT"] = np.ascontiguousarray(g("L0_attn_a_Wq").T.astype(np.float32))
    W["l0aa_wkT"] = np.ascontiguousarray(g("L0_attn_a_Wk").T.astype(np.float32))
    W["l0aa_wvT"] = np.ascontiguousarray(g("L0_attn_a_Wv").T.astype(np.float32))
    W["l0aa_woT"] = np.ascontiguousarray(g("L0_attn_a_Wo").T.astype(np.float32))
    W["l0aa_bias"] = vecpack(g("L0_attn_a_bq"), g("L0_attn_a_bk"),
                             g("L0_attn_a_bv"), g("L0_attn_a_bo"))
    W["l0ab_wq"] = np.ascontiguousarray(g("L0_attn_b_Wq").astype(np.float32))
    W["l0ab_wk"] = np.ascontiguousarray(g("L0_attn_b_Wk").astype(np.float32))
    W["l0ab_wv"] = np.ascontiguousarray(g("L0_attn_b_Wv").astype(np.float32))
    W["l0ab_wb"] = np.ascontiguousarray(
        g("L0_attn_b_Wb").astype(np.float32).reshape(NCH_, 128).T)
    W["l0ab_vec"] = vecpack(g("L0_attn_b_ln_g"), g("L0_attn_b_ln_b"))
    W["l1ab_wqT"] = np.ascontiguousarray(g("L1_attn_b_wq").T.astype(np.float32))
    W["l1ab_wkT"] = np.ascontiguousarray(g("L1_attn_b_wk").T.astype(np.float32))
    W["l1ab_wvT"] = np.ascontiguousarray(g("L1_attn_b_wv").T.astype(np.float32))
    W["l1ab_woT"] = np.ascontiguousarray(g("L1_attn_b_wo").T.astype(np.float32))
    W["l1ab_trig"] = _rope_trig_table()
    return W


def _rope_trig_table():
    hd, seq = 128, 2048
    w = np.arange(0, hd, 2).astype(np.float32) / np.float32(hd)
    inv = np.float32(1.0) / np.power(10000.0, w.astype(np.float64)).astype(np.float32)
    t = np.arange(seq - 1, dtype=np.float32)
    ang = (t[:, None] * inv[None, :]).astype(np.float32)
    flat = ang.ravel()
    try:
        import ctypes
        import ctypes.util
        name = ctypes.util.find_library("m") or "libm.so.6"
        libm = ctypes.CDLL(name)
        libm.sinf.restype = ctypes.c_float
        libm.sinf.argtypes = [ctypes.c_float]
        libm.cosf.restype = ctypes.c_float
        libm.cosf.argtypes = [ctypes.c_float]
        sint = np.fromiter((libm.sinf(float(v)) for v in flat), np.float32,
                           flat.size).reshape(ang.shape)
        cost = np.fromiter((libm.cosf(float(v)) for v in flat), np.float32,
                           flat.size).reshape(ang.shape)
    except Exception:
        sint = np.sin(flat.astype(np.float64)).astype(np.float32).reshape(ang.shape)
        cost = np.cos(flat.astype(np.float64)).astype(np.float32).reshape(ang.shape)
    sfull = np.zeros((seq, 64), np.float32)
    sfull[1:] = sint
    cfull = np.ones((seq, 64), np.float32)
    cfull[1:] = cost
    trig = np.empty((128, 2, 16, 64), np.float32)
    trig[:, 0] = sfull.reshape(16, 128, 64).transpose(1, 0, 2)
    trig[:, 1] = cfull.reshape(16, 128, 64).transpose(1, 0, 2)
    return np.ascontiguousarray(trig)


LAYERS = CHAIN


def build_nc(layers=LAYERS):
    return Builder(set(layers))


def kernel(**inputs):
    x = np.ascontiguousarray(np.asarray(inputs["L0_attn_a_x"], dtype=np.float32))
    b = Builder(set(LAYERS))
    nc = b.build()
    W = prep_weights(inputs)
    need = set(b.weights.keys())
    in_maps = [dict({k: W[k] for k in need}, x=np.ascontiguousarray(x[i]))
               for i in range(B)]
    res = run_bass_kernel_spmd(nc, in_maps, core_ids=list(range(B)))
    out = np.stack([np.asarray(r["y"]) for r in res.results], axis=0)
    return out.astype(np.float32)
